# version 01
# baseline (speedup 1.0000x reference)
import numpy as np
import ml_dtypes
import concourse.bass as bass
import concourse.mybir as mybir
from concourse.bass_utils import run_bass_kernel_spmd

F32 = mybir.dt.float32
BF16 = mybir.dt.bfloat16
AF = mybir.ActivationFunctionType
ALU = mybir.AluOpType
AX = mybir.AxisListType

P = 128
D = 2048
H = 16
HD = 128
TOWN = 1024
TSEQ = 2048
NE = 32
FI = 768
KD = D // P
KF = FI // P
NT = TOWN // P
NTS = TSEQ // P
LN_EPS = 1e-5
RMS_EPS = 1.1920929e-07

SB_BASE = 16512
SB_TOP = 229344


def build_nc(taps=()):
    nc = bass.Bass("TRN2", target_bir_lowering=False, debug=False)

    x_own_d = nc.dram_tensor("x_own", [NT, P, D], F32, kind="ExternalInput")
    x_seq_d = nc.dram_tensor("x_seq", [NTS, P, D], F32, kind="ExternalInput")
    wqkvT_d = nc.dram_tensor("wqkvT", [D, 3 * D], F32, kind="ExternalInput")
    woT_d = nc.dram_tensor("woT", [D, D], F32, kind="ExternalInput")
    gate_d = nc.dram_tensor("gate", [D, 32], F32, kind="ExternalInput")
    wgu_d = nc.dram_tensor("wgu", [NE, D, 2 * FI], F32, kind="ExternalInput")
    wdn_d = nc.dram_tensor("wdn", [NE, FI, D], F32, kind="ExternalInput")
    ropeq_d = nc.dram_tensor("ropeq", [2, P, TOWN], BF16, kind="ExternalInput")
    ropek_d = nc.dram_tensor("ropek", [2, P, TSEQ], BF16, kind="ExternalInput")
    mask_d = nc.dram_tensor("maskm", [16, P, 512], BF16, kind="ExternalInput")
    constb_d = nc.dram_tensor("constb", [3, P, P], BF16, kind="ExternalInput")
    onesrf_d = nc.dram_tensor("onesrf", [1, P], F32, kind="ExternalInput")
    fconst_d = nc.dram_tensor("fconst", [P, 4], F32, kind="ExternalInput")
    vscr_d = nc.dram_tensor("vscr", [NTS, P, D], BF16)
    out_d = nc.dram_tensor("out", [NT, P, D], F32, kind="ExternalOutput")
    tap_d = {}
    for t in taps:
        shp = {"xn_own": [P, KD * TOWN], "xn_seq": [P, KD * TSEQ],
               "attn_out": [P, H * TOWN], "resid_mid": [NT, P, D],
               "wrt": [P, NT * 32], "qfm0": [P, TOWN], "kfm0": [P, TSEQ]}[t]
        tap_d[t] = nc.dram_tensor("tap_" + t, shp, F32, kind="ExternalOutput")

    def at(name, shape, dtype, off):
        esz = {BF16: 2, F32: 4}[dtype]
        nbytes = int(np.prod(shape[1:])) * esz
        assert SB_BASE + off + nbytes <= SB_TOP, (name, off, nbytes)
        return nc.alloc_sbuf_tensor_at(name, shape, dtype, offset=SB_BASE + off)

    xn_own = at("xn_own", [P, KD * TOWN], BF16, 0)
    onescol = at("onescol", [P, 1], BF16, 32768)
    onesrf = at("onesrf", [1, P], F32, 32800)
    permb = at("permb", [P, P], BF16, 33312)
    identb = at("identb", [P, P], BF16, 33568)
    lnrow = at("lnrow", [P, 4], F32, 33824)
    fconst = at("fconst", [P, 4], F32, 33856)
    R2 = 33920
    xn_seq = at("xn_seq", [P, KD * TSEQ], BF16, R2)
    resid = at("resid", [P, NT * D], F32, R2)
    R3 = R2 + 65536
    ropeq = at("ropeq", [P, 2 * TOWN], BF16, R3)
    ropek = at("ropek", [P, 2 * TSEQ], BF16, R3 + 4096)
    maskm = at("maskm", [P, 16 * 512], BF16, R3 + 12288)
    woTb = [at("woTa", [P, 8 * 512], BF16, R3),
            at("woTbb", [P, 8 * 512], BF16, R3 + 8192)]
    silu_g = at("silu_g", [P, KF * TOWN], BF16, R3)
    gated = at("gated", [P, KF * TOWN], BF16, R3 + 12288)
    gateb = at("gateb", [P, KD * 32], BF16, R3 + 24576)
    wrt = at("wrt", [P, NT * 32], F32, R3 + 25600)
    tk_p = at("tk_p", [P, 32], F32, R3 + 26624)
    tk_w = at("tk_w", [P, 32], F32, R3 + 26752)
    tk_keep = at("tk_keep", [P, 32], F32, R3 + 26880)
    tk_mx = at("tk_mx", [P, 4], F32, R3 + 27008)
    tk_s = at("tk_s", [P, 1], F32, R3 + 27136)
    R4 = R3 + 28672
    vw_buf = at("vw_buf", [P, 2 * KD * 512], BF16, R4)
    vstage = at("vstage", [P, 2 * 512], BF16, R4 + 32768)
    wqh = at("wqh", [P, 2 * D], BF16, R4)
    wkh = at("wkh", [P, 2 * D], BF16, R4 + 8192)
    v_tm = at("v_tm", [P, 2 * D], BF16, R4 + 16384)
    q_fm = at("q_fm", [P, TOWN], BF16, R4 + 24576)
    k_fm = at("k_fm", [P, TSEQ], BF16, R4 + 26624)
    q_sb = at("q_sb", [P, 512], BF16, R4 + 30720)
    qa = at("qa", [P, 512], F32, R4 + 31744)
    qb = at("qb", [P, 512], F32, R4 + 33792)
    attnT = at("attnT", [P, 2 * 512], BF16, R4 + 35840)
    rb_sb = at("rb_sb", [P, 512], BF16, R4 + 37888)
    rsum_row = at("rsum_row", [1, 512], F32, R4 + 38912)
    rstd_row = at("rstd_row", [1, 512], F32, R4 + 40960)
    wg_bf = at("wg_bf", [P, KD * FI], BF16, R4)
    wu_bf = at("wu_bf", [P, KD * FI], BF16, R4 + 24576)
    R5 = R4 + 49152
    xstage = at("xstage", [P, 2 * D], F32, R5)
    lnsq = at("lnsq", [P, D], F32, R5 + 16384)
    xn_tmp = at("xn_tmp", [P, D], BF16, R5 + 24576)
    attn_out = at("attn_out", [P, H * TOWN], BF16, R5)
    wd_bf = at("wd_bf", [P, KF * D], BF16, R5)

    from contextlib import ExitStack
    ES = ExitStack()
    PB = [ES.enter_context(nc.psum_tensor(f"PB{i}", [P, 512], F32))
          for i in range(8)]

    sem = lambda n: ES.enter_context(nc.semaphore(n))
    SEM = {n: sem(n) for n in
           ("s_wq", "s_wrdy", "s_xq", "s_xrdy", "s_aq", "s_ardy",
            "s_pe", "s_act", "s_dve", "s_out", "s_vs0", "s_vs1", "s_tap")}
    DMA_SEMS = {"s_wq", "s_xq", "s_aq", "s_out", "s_vs0", "s_vs1", "s_tap"}
    cnt = {k: 0 for k in SEM}

    blocks, cur, shadow = [], None, None

    def new_block():
        nonlocal cur, shadow
        cur = {k: [] for k in ("sync", "tensor", "vector", "scalar", "gpsimd")}
        shadow = {k: (set(), set()) for k in cur}
        blocks.append(cur)

    def em(eng, fn, wait=None, inc=None, reads=(), writes=()):
        rs, ws = shadow[eng]
        if eng in ("vector", "scalar", "gpsimd"):
            if (ws & set(reads)) or (set(writes) & (ws | rs)):
                cur[eng].append((lambda e: e.drain(), [], None))
                rs.clear(); ws.clear()
        rs |= set(reads); ws |= set(writes)
        n = None
        if inc is not None:
            step = 16 if inc in DMA_SEMS else 1
            cnt[inc] += step
            n = (SEM[inc], step)
        wl = [(SEM[s], v) for s, v in (wait or [])
              if s is not None and v is not None]
        cur[eng].append((fn, wl, n))
        return cnt[inc] if inc else None

    def q_barrier(eng, q_sem, rdy_sem):
        em(eng, lambda e, v=cnt[q_sem], s=SEM[q_sem]: e.wait_ge(s, v))
        return em(eng, lambda e: e.nop(), inc=rdy_sem)

    def replay_all():
        def mk(eng_ops):
            def f(eng):
                for fn, waits, inc in eng_ops:
                    for s, v in waits:
                        eng.wait_ge(s, v)
                    r = fn(eng)
                    if inc is not None:
                        s, step = inc
                        if r is None:
                            r = eng.nop()
                        r.then_inc(s, step)
            return f
        for ops in blocks:
            with nc.Block() as block:
                block.sync(mk(ops["sync"]))
                block.tensor(mk(ops["tensor"]))
                block.vector(mk(ops["vector"]))
                block.scalar(mk(ops["scalar"]))
                block.gpsimd(mk(ops["gpsimd"]))

    def rearr_w(dram2d, c0, c1):
        return dram2d[:].rearrange("(k p) o -> p k o", p=P)[:, :, c0:c1]

    new_block()
    pT = [PB[0][:].bitcast(BF16), PB[1][:].bitcast(BF16)]

    em("scalar", lambda e: e.dma_start(out=identb[:], in_=constb_d[0]), inc="s_aq")
    em("scalar", lambda e: e.dma_start(out=permb[:], in_=constb_d[1]), inc="s_aq")
    em("vector", lambda e: e.memset(onescol[:], 1.0), inc="s_dve", writes=("onescol",))
    em("scalar", lambda e: e.dma_start(out=onesrf[:], in_=onesrf_d[:]), inc="s_aq")
    em("scalar", lambda e: e.dma_start(out=fconst[:], in_=fconst_d[:]), inc="s_aq")
    em("scalar", lambda e: e.dma_start(out=ropeq[:, 0:TOWN], in_=ropeq_d[0]), inc="s_aq")
    em("scalar", lambda e: e.dma_start(out=ropeq[:, TOWN:2 * TOWN], in_=ropeq_d[1]), inc="s_aq")
    em("scalar", lambda e: e.dma_start(out=ropek[:, 0:TSEQ], in_=ropek_d[0]), inc="s_aq")
    em("scalar", lambda e: e.dma_start(out=ropek[:, TSEQ:2 * TSEQ], in_=ropek_d[1]), inc="s_aq")
    em("scalar", lambda e: e.dma_start(
        out=maskm[:].rearrange("p (j t) -> p j t", t=512),
        in_=mask_d[:].rearrange("j p t -> p j t")), inc="s_aq")
    consts_rdy = q_barrier("scalar", "s_aq", "s_ardy")
    onescol_dve = cnt["s_dve"]

    vw_rdy, vw_pe = {}, {}

    def stream_vw(og):
        par = og & 1
        em("gpsimd", lambda e, og=og, par=par: e.dma_start(
            out=vw_buf[:, par * KD * 512:(par + 1) * KD * 512].rearrange(
                "p (k o) -> p k o", o=512),
            in_=rearr_w(wqkvT_d, 2 * D + og * 512, 2 * D + (og + 1) * 512)),
            wait=[("s_pe", vw_pe.get(og - 2))], inc="s_wq",
            writes=(f"vw{par}",))
        vw_rdy[og] = q_barrier("gpsimd", "s_wq", "s_wrdy")

    stream_vw(0)
    stream_vw(1)

    x_rdy, ln_done = [], []
    tp_guard = [None, None]
    tp_last = None
    a1 = None
    for i in range(NTS + NT):
        sl = i & 1
        w = [("s_dve", ln_done[i - 2])] if i >= 2 else []
        em("sync", lambda e, i=i, sl=sl: e.dma_start(
            out=xstage[:, sl * D:(sl + 1) * D],
            in_=(x_seq_d[i] if i < NTS else x_own_d[i - NTS])),
            wait=w, inc="s_xq")
        x_rdy.append(q_barrier("sync", "s_xq", "s_xrdy"))

        em("vector", lambda e, sl=sl: e.reduce_sum(
            out=lnrow[:, 0:1], in_=xstage[:, sl * D:(sl + 1) * D], axis=AX.X),
            wait=[("s_xrdy", x_rdy[i]), ("s_act", a1)],
            inc="s_dve", reads=(f"xst{sl}",), writes=("lnrow0",))
        v2 = em("vector", lambda e: e.tensor_scalar(
            out=lnrow[:, 1:2], in0=lnrow[:, 0:1], scalar1=-1.0 / D, scalar2=None,
            op0=ALU.mult), inc="s_dve", reads=("lnrow0",), writes=("lnrow1",))
        a1 = em("scalar", lambda e, sl=sl: e.activation(
            out=lnsq[:], in_=xstage[:, sl * D:(sl + 1) * D], func=AF.Square,
            bias=lnrow[:, 1:2], accum_out=lnrow[:, 2:3]),
            wait=[("s_dve", v2)], inc="s_act",
            reads=(f"xst{sl}", "lnrow1"), writes=("lnsq", "lnrow2"))
        em("scalar", lambda e: e.activation(
            out=lnrow[:, 3:4], in_=lnrow[:, 2:3], func=AF.Ln,
            bias=fconst[:, 0:1], scale=1.0 / D), inc="s_act",
            reads=("lnrow2",), writes=("lnrow3",))
        a3 = em("scalar", lambda e: e.activation(
            out=lnrow[:, 3:4], in_=lnrow[:, 3:4], func=AF.Exp,
            scale=-0.5), inc="s_act", reads=("lnrow3",), writes=("lnrow3",))
        v3 = em("vector", lambda e, sl=sl: e.tensor_scalar(
            out=xn_tmp[:], in0=xstage[:, sl * D:(sl + 1) * D],
            scalar1=lnrow[:, 1:2], scalar2=lnrow[:, 3:4],
            op0=ALU.add, op1=ALU.mult),
            wait=[("s_act", a3), ("s_pe", tp_last)], inc="s_dve",
            reads=(f"xst{sl}", "lnrow1", "lnrow3"), writes=("xn_tmp",))
        ln_done.append(v3)
        dst, tcols, tb = ((xn_seq, TSEQ, i) if i < NTS else (xn_own, TOWN, i - NTS))
        for k in range(KD):
            par = k & 1
            w = [("s_dve", v3), ("s_act", tp_guard[par])]
            if i == 0 and k == 0:
                w.append(("s_ardy", consts_rdy))
            pe = em("tensor", lambda e, k=k, par=par: e.transpose(
                out=pT[par][:, 0:P], in_=xn_tmp[:, k * P:(k + 1) * P],
                identity=identb[:]), wait=w, inc="s_pe")
            tp_guard[par] = em(
                "scalar",
                lambda e, k=k, tb=tb, par=par, dst=dst, tcols=tcols:
                e.copy(out=dst[:, k * tcols + tb * P: k * tcols + (tb + 1) * P],
                       in_=pT[par][:, 0:P]),
                wait=[("s_pe", pe)], inc="s_act", writes=("xn",))
        tp_last = cnt["s_pe"]
    xn_all_act = cnt["s_act"]

    vst_dma = {}
    vst_guard = [None, None]
    for og in range(4):
        par = og & 1
        if og >= 2:
            stream_vw(og)
        for tc in range(NTS):
            idx0 = og * NTS + tc
            ppar = idx0 & 1
            pp = PB[2 + ppar]

            def v_mms(eng, tc=tc, par=par, pp=pp):
                r = None
                for k in range(KD):
                    r = eng.matmul(
                        out=pp[:],
                        lhsT=xn_seq[:, k * TSEQ + tc * P: k * TSEQ + (tc + 1) * P],
                        rhs=vw_buf[:, par * KD * 512 + k * 512:
                                   par * KD * 512 + (k + 1) * 512],
                        start=(k == 0), stop=(k == KD - 1))
                return r
            w = [("s_wrdy", vw_rdy[og]), ("s_act", xn_all_act),
                 ("s_act", vst_guard[ppar])]
            pe = em("tensor", v_mms, wait=w, inc="s_pe")
            idx = og * NTS + tc
            slot = idx & 1
            wv = [("s_pe", pe)]
            if idx >= 2:
                wv.append((f"s_vs{slot}", 16 * (idx // 2)))
            vst_guard[ppar] = em(
                "scalar", lambda e, ppar=ppar, pp=pp:
                e.copy(out=vstage[:, ppar * 512:(ppar + 1) * 512], in_=pp[:]),
                wait=wv, inc="s_act", writes=(f"vstg{ppar}",))
            em("scalar", lambda e, tc=tc, og=og, ppar=ppar: e.dma_start(
                out=vscr_d[tc, :, og * 512:(og + 1) * 512],
                in_=vstage[:, ppar * 512:(ppar + 1) * 512]),
                inc=f"s_vs{slot}", reads=(f"vstg{ppar}",))
        vw_pe[og] = cnt["s_pe"]
    em("scalar", lambda e, v=cnt["s_vs0"], s=SEM["s_vs0"]: e.wait_ge(s, v))
    em("scalar", lambda e, v=cnt["s_vs1"], s=SEM["s_vs1"]: e.wait_ge(s, v))
    vscr_all = em("scalar", lambda e: e.nop(), inc="s_ardy")

    if "xn_own" in tap_d:
        td = tap_d["xn_own"]
        em("gpsimd", lambda e, td=td: e.dma_start(out=td[:], in_=xn_own[:]),
           wait=[("s_act", xn_all_act)], inc="s_tap")
    if "xn_seq" in tap_d:
        td = tap_d["xn_seq"]
        em("gpsimd", lambda e, td=td: e.dma_start(out=td[:], in_=xn_seq[:]),
           wait=[("s_act", xn_all_act)], inc="s_tap")

    new_block()
    pQK, pAV, pBCr, pBCs = PB[0], PB[3], PB[4], PB[5]
    pSC = [PB[1], PB[2]]
    pSS = PB[6][0:1, :]
    pSUM = PB[7][0:1, :]

    wq_rdy, wk_rdy, v_rdy = {}, {}, {}
    whead_pe, vhead_pe = {}, {}

    def stream_head(h):
        par = h & 1
        em("gpsimd", lambda e, h=h, par=par: e.dma_start(
            out=wqh[:, par * D:(par + 1) * D].rearrange("p (k o) -> p k o", o=HD),
            in_=rearr_w(wqkvT_d, h * HD, (h + 1) * HD)),
            wait=[("s_pe", whead_pe.get(h - 2))], inc="s_wq",
            writes=(f"wqh{par}",))
        wq_rdy[h] = q_barrier("gpsimd", "s_wq", "s_wrdy")
        em("gpsimd", lambda e, h=h, par=par: e.dma_start(
            out=wkh[:, par * D:(par + 1) * D].rearrange("p (k o) -> p k o", o=HD),
            in_=rearr_w(wqkvT_d, D + h * HD, D + (h + 1) * HD)),
            inc="s_wq", writes=(f"wkh{par}",))
        wk_rdy[h] = q_barrier("gpsimd", "s_wq", "s_wrdy")
        em("scalar", lambda e, h=h, par=par: e.dma_start(
            out=v_tm[:, par * D:(par + 1) * D].rearrange("p (tc o) -> p tc o", o=HD),
            in_=vscr_d[:].rearrange("tc p o -> p tc o")[:, :, h * HD:(h + 1) * HD]),
            wait=[("s_ardy", vscr_all), ("s_pe", vhead_pe.get(h - 2))],
            inc="s_aq", writes=(f"v_tm{par}",))
        v_rdy[h] = q_barrier("scalar", "s_aq", "s_ardy")

    stream_head(0)
    stream_head(1)

    rope_done = {}
    b_guard = [None]
    bc_guard = [None]
    ss_l1 = [None]
    ss_pe = [None]
    rsum_rd = [None]
    rstd_rd = [None]
    sc_exp = [None, None]
    av_done = [None]
    sum_cons = [None]
    bcs_cons = [None]

    for h in range(H):
        par = h & 1
        for which, ntg in (("q", 2), ("k", 4)):
            wbuf, rdy = (wqh, wq_rdy[h]) if which == "q" else (wkh, wk_rdy[h])
            dst = q_fm if which == "q" else k_fm
            dcols = TOWN if which == "q" else TSEQ
            rope = ropeq if which == "q" else ropek
            xsrc, xcols = (xn_own, TOWN) if which == "q" else (xn_seq, TSEQ)
            for tg in range(ntg):
                def qk_mms(eng, wbuf=wbuf, par=par, xsrc=xsrc, xcols=xcols, tg=tg):
                    r = None
                    for k in range(KD):
                        r = eng.matmul(
                            out=pQK[:],
                            lhsT=wbuf[:, par * D + k * HD: par * D + (k + 1) * HD],
                            rhs=xsrc[:, k * xcols + tg * 512:
                                     k * xcols + tg * 512 + 512],
                            start=(k == 0), stop=(k == KD - 1))
                    return r
                pe1 = em("tensor", qk_mms,
                         wait=[("s_wrdy", rdy), ("s_act", xn_all_act),
                               ("s_dve", b_guard[0])], inc="s_pe")
                ev = em("scalar", lambda e: e.copy(out=q_sb[:], in_=pQK[:]),
                        wait=[("s_pe", pe1), ("s_pe", ss_pe[0]),
                              ("s_dve", b_guard[0])],
                        inc="s_act", writes=("q_sb",))
                pe2 = em("tensor", lambda e: e.matmul(
                    out=pQK[:], lhsT=permb[:], rhs=q_sb[:], start=True, stop=True),
                    wait=[("s_act", ev)], inc="s_pe")
                em("vector", lambda e, rope=rope, tg=tg: e.tensor_tensor(
                    out=qa[:], in0=q_sb[:], in1=rope[:, tg * 512: tg * 512 + 512],
                    op=ALU.mult),
                    wait=[("s_act", ev)], inc="s_dve",
                    reads=("q_sb",), writes=("qa",))
                vb = em("vector", lambda e, rope=rope, tg=tg, dcols=dcols:
                        e.tensor_tensor(
                            out=qb[:], in0=pQK[:],
                            in1=rope[:, dcols + tg * 512: dcols + tg * 512 + 512],
                            op=ALU.mult),
                        wait=[("s_pe", pe2)], inc="s_dve", writes=("qb",))
                b_guard[0] = vb
                vq = em("vector", lambda e: e.tensor_add(
                    out=qa[:], in0=qa[:], in1=qb[:]),
                    inc="s_dve", reads=("qa", "qb"), writes=("qa",))
                sq = em("scalar", lambda e: e.activation(
                    out=q_sb[:], in_=qa[:], func=AF.Square),
                    wait=[("s_dve", vq)], inc="s_act",
                    reads=("qa",), writes=("q_sb",))
                pe3 = em("tensor", lambda e: e.matmul(
                    out=pSS, lhsT=onescol[:], rhs=q_sb[:], start=True, stop=True),
                    wait=[("s_act", sq), ("s_act", ss_l1[0])], inc="s_pe")
                ss_pe[0] = pe3
                l1 = em("scalar", lambda e: e.activation(
                    out=rsum_row[:], in_=pSS, func=AF.Ln,
                    bias=fconst[0:1, 1:2], scale=1.0 / HD),
                    wait=[("s_pe", pe3), ("s_pe", rsum_rd[0])],
                    inc="s_act", writes=("rsum_row",))
                ss_l1[0] = l1
                bias2 = fconst[0:1, 2:3] if which == "q" else fconst[0:1, 3:4]
                l2 = em("scalar", lambda e, bias2=bias2: e.activation(
                    out=rstd_row[:], in_=rsum_row[:], func=AF.Exp,
                    scale=-0.5, bias=bias2),
                    wait=[("s_pe", rstd_rd[0])], inc="s_act",
                    reads=("rsum_row",), writes=("rstd_row",))
                pe4 = em("tensor", lambda e: e.matmul(
                    out=pBCr[:], lhsT=onesrf[:], rhs=rstd_row[:],
                    start=True, stop=True),
                    wait=[("s_act", l2), ("s_dve", bc_guard[0])], inc="s_pe")
                rstd_rd[0] = pe4
                rsum_rd[0] = pe4
                vn = em("vector",
                        lambda e, dst=dst, tg=tg:
                        e.tensor_tensor(
                            out=dst[:, tg * 512: tg * 512 + 512],
                            in0=qa[:], in1=pBCr[:], op=ALU.mult),
                        wait=[("s_pe", pe4)], inc="s_dve",
                        reads=("qa",), writes=("qkfm",))
                bc_guard[0] = vn
        whead_pe[h] = cnt["s_pe"]
        rope_done[h] = cnt["s_dve"]
        if h == 0 and "qfm0" in tap_d:
            em("gpsimd", lambda e: e.dma_start(out=tap_d["qfm0"][:], in_=q_fm[:]),
               wait=[("s_dve", rope_done[0])], inc="s_tap")
            em("gpsimd", lambda e: e.dma_start(out=tap_d["kfm0"][:], in_=k_fm[:]),
               inc="s_tap")
            em("vector", lambda e, v=cnt["s_tap"], s=SEM["s_tap"]: e.wait_ge(s, v))

        for grp in range(2):
            js = list(range(8)) if grp == 0 else list(range(16))
            nj = len(js)
            pe_s = None
            for idx, j in enumerate(js):
                spar = j & 1
                first, last = idx == 0, idx == nj - 1
                w = [("s_dve", rope_done[h]), ("s_act", sc_exp[spar]),
                     ("s_ardy", v_rdy[h])]
                pe = em("tensor", lambda e, j=j, grp=grp, spar=spar:
                        e.matmul(out=pSC[spar][:],
                                 lhsT=k_fm[:, j * P:(j + 1) * P],
                                 rhs=q_fm[:, grp * 512: grp * 512 + 512],
                                 start=True, stop=True, skip_group_check=True),
                        wait=w, inc="s_pe")
                ex = em("scalar", lambda e, spar=spar: e.activation(
                    out=attnT[:, spar * 512:(spar + 1) * 512], in_=pSC[spar][:],
                    func=AF.Exp),
                    wait=[("s_pe", pe)], inc="s_act", writes=(f"attnT{spar}",))
                sc_exp[spar] = ex
                mi = j if (grp == 0 or j >= 8) else None
                dep = ("s_act", ex)
                if mi is not None:
                    vm = em("vector", lambda e, spar=spar, mi=mi: e.tensor_mul(
                        out=attnT[:, spar * 512:(spar + 1) * 512],
                        in0=attnT[:, spar * 512:(spar + 1) * 512],
                        in1=maskm[:, mi * 512:(mi + 1) * 512]),
                        wait=[("s_act", ex)], inc="s_dve",
                        reads=(f"attnT{spar}",), writes=(f"attnT{spar}",))
                    dep = ("s_dve", vm)
                w5 = [dep]
                if first:
                    w5 += [("s_dve", av_done[0]), ("s_dve", sum_cons[0])]
                em("tensor", lambda e, j=j, par=par, spar=spar, first=first, last=last:
                   e.matmul(out=pAV[:],
                            lhsT=v_tm[:, par * D + j * P: par * D + (j + 1) * P],
                            rhs=attnT[:, spar * 512:(spar + 1) * 512],
                            start=first, stop=last, skip_group_check=True),
                   wait=w5, inc="s_pe")
                pe_s = em("tensor", lambda e, spar=spar, first=first, last=last:
                          e.matmul(out=pSUM, lhsT=onescol[:],
                                   rhs=attnT[:, spar * 512:(spar + 1) * 512],
                                   start=first, stop=last, skip_group_check=True),
                          inc="s_pe")
            rc = em("vector", lambda e: e.reciprocal(out=rsum_row[:], in_=pSUM),
                    wait=[("s_pe", pe_s), ("s_act", ss_l1[0]),
                          ("s_pe", rsum_rd[0])],
                    inc="s_dve", writes=("rsum_row",))
            sum_cons[0] = rc
            pe6 = em("tensor", lambda e: e.matmul(
                out=pBCs[:], lhsT=onesrf[:], rhs=rsum_row[:], start=True, stop=True),
                wait=[("s_dve", rc), ("s_act", bcs_cons[0])], inc="s_pe")
            rsum_rd[0] = pe6
            rbc = em("scalar", lambda e: e.copy(out=rb_sb[:], in_=pBCs[:]),
                     wait=[("s_pe", pe6)], inc="s_act", writes=("rb_sb",))
            bcs_cons[0] = rbc
            ao = em("vector", lambda e, h=h, grp=grp: e.tensor_tensor(
                out=attn_out[:, h * TOWN + grp * 512: h * TOWN + grp * 512 + 512],
                in0=pAV[:], in1=rb_sb[:], op=ALU.mult),
                wait=[("s_act", rbc)], inc="s_dve",
                reads=("rb_sb",), writes=("attn_out",))
            av_done[0] = ao
        vhead_pe[h] = cnt["s_pe"]
        if h + 2 < H:
            stream_head(h + 2)
    attn_done_dve = cnt["s_dve"]

    if "attn_out" in tap_d:
        td = tap_d["attn_out"]
        em("gpsimd", lambda e, td=td: e.dma_start(out=td[:], in_=attn_out[:]),
           wait=[("s_dve", attn_done_dve)], inc="s_tap")

    new_block()

    em("sync", lambda e: e.dma_start(
        out=resid[:].rearrange("p (tb d) -> p tb d", d=D),
        in_=x_own_d[:].rearrange("tb p d -> p tb d")),
        wait=[("s_tap", cnt["s_tap"] or None)], inc="s_xq")
    xown_rdy = q_barrier("sync", "s_xq", "s_xrdy")

    wo_rdy = {}
    wo_buf_pe = {}
    y_evac = {}

    def stream_wo(og, half):
        par = (og * 2 + half) & 1
        em("gpsimd", lambda e, og=og, half=half, par=par: e.dma_start(
            out=woTb[par][:].rearrange("p (k o) -> p k o", o=512),
            in_=woT_d[:].rearrange("(k p) o -> p k o", p=P)[
                :, half * 8:(half + 1) * 8, og * 512:(og + 1) * 512]),
            wait=[("s_pe", wo_buf_pe.get((og * 2 + half) - 2))], inc="s_wq",
            writes=(f"woT{par}",))
        wo_rdy[(og, half)] = q_barrier("gpsimd", "s_wq", "s_wrdy")

    stream_wo(0, 0)
    stream_wo(0, 1)
    for og in range(4):
        for half in range(2):
            par = (og * 2 + half) & 1
            for tb in range(NT):
                def y_mms(eng, tb=tb, half=half, par=par):
                    r = None
                    for cc in range(8):
                        c = half * 8 + cc
                        r = eng.matmul(
                            out=PB[tb][:],
                            lhsT=attn_out[:, c * TOWN + tb * P:
                                          c * TOWN + (tb + 1) * P],
                            rhs=woTb[par][:, cc * 512:(cc + 1) * 512],
                            start=(half == 0 and cc == 0),
                            stop=(half == 1 and cc == 7),
                            skip_group_check=True)
                    return r
                w = [("s_wrdy", wo_rdy[(og, half)]), ("s_dve", y_evac.get(tb))]
                pe = em("tensor", y_mms, wait=w, inc="s_pe")
                if half == 1:
                    y_evac[tb] = em(
                        "vector", lambda e, tb=tb, og=og: e.tensor_tensor(
                            out=resid[:, tb * D + og * 512:
                                      tb * D + og * 512 + 512],
                            in0=resid[:, tb * D + og * 512:
                                      tb * D + og * 512 + 512],
                            in1=PB[tb][:], op=ALU.add),
                        wait=[("s_pe", pe), ("s_xrdy", xown_rdy)], inc="s_dve",
                        reads=("resid",), writes=("resid",))
            wo_buf_pe[og * 2 + half] = cnt["s_pe"]
            nxt = og * 2 + half + 2
            if nxt < 8:
                stream_wo(nxt // 2, nxt % 2)
    wo_done_dve = cnt["s_dve"]

    moe_rdy = {}
    em("gpsimd", lambda e: e.dma_start(
        out=wg_bf[:].rearrange("p (k f) -> p k f", f=FI),
        in_=wgu_d[0].rearrange("(k p) f -> p k f", p=P)[:, :, 0:FI]), inc="s_wq")
    moe_rdy[("g", 0)] = q_barrier("gpsimd", "s_wq", "s_wrdy")
    em("gpsimd", lambda e: e.dma_start(
        out=wu_bf[:].rearrange("p (k f) -> p k f", f=FI),
        in_=wgu_d[0].rearrange("(k p) f -> p k f", p=P)[:, :, FI:2 * FI]), inc="s_wq")
    moe_rdy[("u", 0)] = q_barrier("gpsimd", "s_wq", "s_wrdy")

    resid_tap_out = None
    if "resid_mid" in tap_d:
        td = tap_d["resid_mid"]
        em("gpsimd", lambda e, td=td: e.dma_start(
            out=td[:].rearrange("tb p d -> p tb d"),
            in_=resid[:].rearrange("p (tb d) -> p tb d", d=D)),
            wait=[("s_dve", wo_done_dve)], inc="s_tap")
        resid_tap_out = cnt["s_tap"]

    new_block()
    pH_ = [PB[0], PB[1]]
    pR = PB[2]
    pD_ = [PB[4], PB[5], PB[6], PB[7]]

    em("gpsimd", lambda e: e.dma_start(
        out=gateb[:].rearrange("p (k o) -> p k o", o=32),
        in_=gate_d[:].rearrange("(k p) o -> p k o", p=P)), inc="s_wq")
    gate_rdy = q_barrier("gpsimd", "s_wq", "s_wrdy")

    rt_done = []
    for tb in range(NT):
        def r_mms(eng, tb=tb):
            r = None
            for k in range(KD):
                r = eng.matmul(out=pR[:, 0:32],
                               lhsT=xn_own[:, k * TOWN + tb * P:
                                           k * TOWN + (tb + 1) * P],
                               rhs=gateb[:, k * 32:(k + 1) * 32],
                               start=(k == 0), stop=(k == KD - 1))
            return r
        w = [("s_wrdy", gate_rdy)]
        if tb > 0:
            w.append(("s_dve", rt_done[-1]))
        pe = em("tensor", r_mms, wait=w, inc="s_pe")
        vmx = em("vector", lambda e: e.reduce_max(
            out=tk_s[:], in_=pR[:, 0:32], axis=AX.X, negate=True),
            wait=[("s_pe", pe)], inc="s_dve", writes=("tk_s",))
        aex = em("scalar", lambda e: e.activation(
            out=tk_p[:], in_=pR[:, 0:32], func=AF.Exp, bias=tk_s[:],
            accum_out=tk_mx[:, 0:1]),
            wait=[("s_dve", vmx)], inc="s_act",
            reads=("tk_s",), writes=("tk_p", "tk_mx"))
        em("vector", lambda e: e.reciprocal(out=tk_s[:], in_=tk_mx[:, 0:1]),
           wait=[("s_act", aex)], inc="s_dve",
           reads=("tk_mx",), writes=("tk_s",))
        em("vector", lambda e: e.tensor_scalar(
            out=tk_p[:], in0=tk_p[:], scalar1=tk_s[:], scalar2=None, op0=ALU.mult),
            inc="s_dve", reads=("tk_p", "tk_s"), writes=("tk_p",))
        em("vector", lambda e: e.tensor_copy(out=tk_w[:], in_=tk_p[:]),
           inc="s_dve", reads=("tk_p",), writes=("tk_w",))
        for it in range(4):
            em("vector", lambda e, it=it: e.reduce_max(
                out=tk_mx[:, it:it + 1], in_=tk_w[:], axis=AX.X),
                inc="s_dve", reads=("tk_w",), writes=("tk_mx",))
            if it < 3:
                em("vector", lambda e, it=it: e.tensor_scalar(
                    out=tk_keep[:], in0=tk_w[:], scalar1=tk_mx[:, it:it + 1],
                    scalar2=None, op0=ALU.is_ge),
                    inc="s_dve", reads=("tk_w", "tk_mx"), writes=("tk_keep",))
                em("vector", lambda e: e.scalar_tensor_tensor(
                    out=tk_w[:], in0=tk_keep[:], scalar=-2.0, in1=tk_w[:],
                    op0=ALU.mult, op1=ALU.add),
                    inc="s_dve", reads=("tk_keep", "tk_w"), writes=("tk_w",))
        em("vector", lambda e: e.reduce_sum(out=tk_s[:], in_=tk_mx[:], axis=AX.X),
           inc="s_dve", reads=("tk_mx",), writes=("tk_s",))
        em("vector", lambda e: e.reciprocal(out=tk_s[:], in_=tk_s[:]),
           inc="s_dve", reads=("tk_s",), writes=("tk_s",))
        em("vector", lambda e: e.tensor_scalar(
            out=tk_keep[:], in0=tk_p[:], scalar1=tk_mx[:, 3:4], scalar2=None,
            op0=ALU.is_ge), inc="s_dve",
            reads=("tk_p", "tk_mx"), writes=("tk_keep",))
        v = em("vector", lambda e, tb=tb: e.scalar_tensor_tensor(
            out=wrt[:, tb * 32:(tb + 1) * 32], in0=tk_p[:], scalar=tk_s[:],
            in1=tk_keep[:], op0=ALU.mult, op1=ALU.mult),
            inc="s_dve", reads=("tk_p", "tk_s", "tk_keep"), writes=("wrt",))
        rt_done.append(v)
    router_done = cnt["s_dve"]
    if "wrt" in tap_d:
        td = tap_d["wrt"]
        em("gpsimd", lambda e, td=td: e.dma_start(out=td[:], in_=wrt[:]),
           wait=[("s_dve", router_done)], inc="s_tap")

    pe_after = {}

    def moe_piece(kind, e):
        if kind == "g":
            dst = wg_bf[:].rearrange("p (k f) -> p k f", f=FI)
            src = wgu_d[e].rearrange("(k p) f -> p k f", p=P)[:, :, 0:FI]
            gw = pe_after.get(("G", e - 1))
        elif kind == "u":
            dst = wu_bf[:].rearrange("p (k f) -> p k f", f=FI)
            src = wgu_d[e].rearrange("(k p) f -> p k f", p=P)[:, :, FI:2 * FI]
            gw = pe_after.get(("U", e - 1))
        else:
            dst = wd_bf[:].rearrange("p (k f) -> p k f", f=D)
            src = wdn_d[e].rearrange("(k p) f -> p k f", p=P)
            gw = pe_after.get(("D", e - 1))
        em("gpsimd", lambda e_, dst=dst, src=src: e_.dma_start(out=dst, in_=src),
           wait=[("s_pe", gw)], inc="s_wq")
        moe_rdy[(kind, e)] = q_barrier("gpsimd", "s_wq", "s_wrdy")

    hpar = [0]
    pH_guard = [None, None]
    silu_done = {}
    pass1_done = {}
    last_D_evac = {}

    if cnt["s_tap"]:
        em("gpsimd", lambda e, v=cnt["s_tap"], s=SEM["s_tap"]: e.wait_ge(s, v))
    moe_piece("d", 0)
    for e in range(NE):
        for j in range(KF):
            for tg in range(2):
                p_ = hpar[0]; hpar[0] ^= 1
                pp = pH_[p_]

                def g_mms(eng, j=j, tg=tg, pp=pp):
                    r = None
                    for k in range(KD):
                        r = eng.matmul(
                            out=pp[:],
                            lhsT=wg_bf[:, k * FI + j * P: k * FI + (j + 1) * P],
                            rhs=xn_own[:, k * TOWN + tg * 512:
                                       k * TOWN + tg * 512 + 512],
                            start=(k == 0), stop=(k == KD - 1))
                    return r
                gw = pH_guard[p_]
                pe = em("tensor", g_mms,
                        wait=[("s_wrdy", moe_rdy[("g", e)])] + ([gw] if gw else []),
                        inc="s_pe")
                p1 = pass1_done.get((j, tg))
                sv = em("scalar", lambda e_, j=j, tg=tg, pp=pp: e_.activation(
                    out=silu_g[:, j * TOWN + tg * 512: j * TOWN + tg * 512 + 512],
                    in_=pp[:], func=AF.Silu),
                    wait=[("s_pe", pe)] + ([p1] if p1 else []),
                    inc="s_act", writes=("silu_g",))
                silu_done[(j, tg)] = sv
                pH_guard[p_] = ("s_act", sv)
        pe_after[("G", e)] = cnt["s_pe"]
        if e + 1 < NE:
            moe_piece("g", e + 1)
        for j in range(KF):
            for tg in range(2):
                p_ = hpar[0]; hpar[0] ^= 1
                pp = pH_[p_]

                def u_mms(eng, j=j, tg=tg, pp=pp):
                    r = None
                    for k in range(KD):
                        r = eng.matmul(
                            out=pp[:],
                            lhsT=wu_bf[:, k * FI + j * P: k * FI + (j + 1) * P],
                            rhs=xn_own[:, k * TOWN + tg * 512:
                                       k * TOWN + tg * 512 + 512],
                            start=(k == 0), stop=(k == KD - 1))
                    return r
                gw = pH_guard[p_]
                pe = em("tensor", u_mms,
                        wait=[("s_wrdy", moe_rdy[("u", e)])] + ([gw] if gw else []),
                        inc="s_pe")
                w1 = [("s_pe", pe), ("s_act", silu_done[(j, tg)]),
                      ("s_pe", pe_after.get(("D", e - 1)))]
                v = em("vector", lambda e_, j=j, tg=tg, pp=pp: e_.tensor_tensor(
                    out=gated[:, j * TOWN + tg * 512: j * TOWN + tg * 512 + 512],
                    in0=pp[:],
                    in1=silu_g[:, j * TOWN + tg * 512: j * TOWN + tg * 512 + 512],
                    op=ALU.mult),
                    wait=w1, inc="s_dve", reads=("silu_g",), writes=("gated",))
                pass1_done[(j, tg)] = ("s_dve", v)
                pH_guard[p_] = ("s_dve", v)
        pe_after[("U", e)] = cnt["s_pe"]
        gated_done = cnt["s_dve"]
        if e + 1 < NE:
            moe_piece("u", e + 1)
        for tb in range(NT):
            for m in range(4):
                def d_mms(eng, tb=tb, m=m):
                    r = None
                    for k in range(KF):
                        r = eng.matmul(
                            out=pD_[m][:],
                            lhsT=gated[:, k * TOWN + tb * P:
                                       k * TOWN + (tb + 1) * P],
                            rhs=wd_bf[:, k * D + m * 512: k * D + (m + 1) * 512],
                            start=(k == 0), stop=(k == KF - 1))
                    return r
                w = [("s_wrdy", moe_rdy[("d", e)]), ("s_dve", gated_done),
                     ("s_dve", router_done), ("s_dve", last_D_evac.get(m))]
                pe = em("tensor", d_mms, wait=w, inc="s_pe")
                v = em("vector", lambda e_, tb=tb, m=m, e=e: e_.scalar_tensor_tensor(
                    out=resid[:, tb * D + m * 512: tb * D + (m + 1) * 512],
                    in0=pD_[m][:],
                    scalar=wrt[:, tb * 32 + e: tb * 32 + e + 1],
                    in1=resid[:, tb * D + m * 512: tb * D + (m + 1) * 512],
                    op0=ALU.mult, op1=ALU.add),
                    wait=[("s_pe", pe)] + (
                        [("s_tap", cnt["s_tap"] or None)] if e == 0 else []),
                    inc="s_dve",
                    reads=("resid", "wrt"), writes=("resid",))
                last_D_evac[m] = v
        pe_after[("D", e)] = cnt["s_pe"]
        if e + 1 < NE:
            moe_piece("d", e + 1)
    moe_done = cnt["s_dve"]

    new_block()
    em("sync", lambda e: e.dma_start(
        out=out_d[:].rearrange("tb p d -> p tb d"),
        in_=resid[:].rearrange("p (tb d) -> p tb d", d=D)),
        wait=[("s_dve", moe_done)], inc="s_out")
    em("sync", lambda e, v=cnt["s_out"], s=SEM["s_out"]: e.wait_ge(s, v))
    em("gpsimd", lambda e, v=cnt["s_tap"], s=SEM["s_tap"]: e.wait_ge(s, v))

    with ES:
        replay_all()
    return nc


def _host_tables():
    inv = 1.0 / 10000.0 ** (np.arange(0, HD, 2, dtype=np.float64) / HD)
    invr = np.repeat(inv, 2)
    posk = np.arange(TSEQ, dtype=np.float64)

    def mk(pos):
        fr = invr[:, None] * pos[None, :]
        return np.stack([np.cos(fr), np.sin(fr)]).astype(ml_dtypes.bfloat16)

    ropek = mk(posk)
    ropeq, masks = {}, {}
    tri = (np.arange(P)[:, None] <= np.arange(P)[None, :]).astype(np.float32)
    for par in (0, 1):
        pos = np.concatenate(
            [np.arange((2 * b + par) * P, (2 * b + par + 1) * P)
             for b in range(NT)]).astype(np.float64)
        ropeq[par] = mk(pos)
        blocks = [2 * b + par for b in range(NT)]
        m = np.zeros((16, P, 512), dtype=np.float32)
        for gi, grp in enumerate((blocks[:4], blocks[4:])):
            for jj in range(8):
                j = jj if gi == 0 else 8 + jj
                for bi, B in enumerate(grp):
                    if j < B:
                        m[j, :, bi * P:(bi + 1) * P] = 1.0
                    elif j == B:
                        m[j, :, bi * P:(bi + 1) * P] = tri
        masks[par] = m.astype(ml_dtypes.bfloat16)

    perm = np.zeros((P, P), dtype=np.float32)
    for i in range(HD // 2):
        perm[2 * i + 1, 2 * i] = -1.0
        perm[2 * i, 2 * i + 1] = 1.0
    constb = np.stack([np.eye(P, dtype=np.float32), perm,
                       np.ones((P, P), dtype=np.float32)]).astype(ml_dtypes.bfloat16)
    onesrf = np.ones((1, P), dtype=np.float32)
    fconst = np.tile(np.array([[LN_EPS, RMS_EPS, -0.5 * np.log(HD), 0.0]],
                              dtype=np.float32), (P, 1))
    return ropeq, ropek, constb, onesrf, fconst, masks


def make_in_maps(L0_attn_x, L0_attn_W_qkv, L0_attn_W_o, L0_ffn_gate,
                 L0_ffn_gate_up_proj, L0_ffn_down_proj):
    x = np.ascontiguousarray(np.asarray(L0_attn_x, dtype=np.float32))
    wqkvT = np.ascontiguousarray(np.asarray(L0_attn_W_qkv, dtype=np.float32).T)
    woT = np.ascontiguousarray(np.asarray(L0_attn_W_o, dtype=np.float32).T)
    gate = np.ascontiguousarray(np.asarray(L0_ffn_gate, dtype=np.float32))
    wgu = np.ascontiguousarray(np.asarray(L0_ffn_gate_up_proj, dtype=np.float32))
    wdn = np.ascontiguousarray(np.asarray(L0_ffn_down_proj, dtype=np.float32))
    ropeq, ropek, constb, onesrf, fconst, masks = _host_tables()
    in_maps = []
    for c in range(8):
        s, par = c // 2, c % 2
        xs = x[s]
        x_seq = np.ascontiguousarray(xs.reshape(NTS, P, D))
        x_own = np.ascontiguousarray(
            np.stack([xs[(2 * b + par) * P:(2 * b + par + 1) * P]
                      for b in range(NT)]))
        in_maps.append(dict(
            x_own=x_own, x_seq=x_seq, wqkvT=wqkvT, woT=woT, gate=gate,
            wgu=wgu, wdn=wdn, ropeq=ropeq[par], ropek=ropek,
            maskm=masks[par], constb=constb, onesrf=onesrf, fconst=fconst))
    return in_maps


def kernel(L0_attn_x, L0_attn_W_qkv, L0_attn_W_o, L0_ffn_gate,
           L0_ffn_gate_up_proj, L0_ffn_down_proj):
    in_maps = make_in_maps(L0_attn_x, L0_attn_W_qkv, L0_attn_W_o, L0_ffn_gate,
                           L0_ffn_gate_up_proj, L0_ffn_down_proj)
    nc = build_nc()
    res = run_bass_kernel_spmd(nc, in_maps, core_ids=list(range(8)))
    out = np.empty((4, TSEQ, D), dtype=np.float32)
    for c in range(8):
        s, par = c // 2, c % 2
        r = np.asarray(res.results[c]["out"])
        for b in range(NT):
            out[s, (2 * b + par) * P:(2 * b + par + 1) * P, :] = r[b]
    return out


# version 02
# speedup vs baseline: 1.0221x; 1.0221x over previous
import numpy as np
import ml_dtypes
import concourse.bass as bass
import concourse.mybir as mybir
from concourse.bass_utils import run_bass_kernel_spmd

F32 = mybir.dt.float32
BF16 = mybir.dt.bfloat16
AF = mybir.ActivationFunctionType
ALU = mybir.AluOpType
AX = mybir.AxisListType

P = 128
D = 2048
H = 16
HD = 128
TOWN = 1024
TSEQ = 2048
NE = 32
FI = 768
KD = D // P
KF = FI // P
NT = TOWN // P
NTS = TSEQ // P
LN_EPS = 1e-5
RMS_EPS = 1.1920929e-07

SB_BASE = 16512
SB_TOP = 229344


def build_nc(taps=()):
    nc = bass.Bass("TRN2", target_bir_lowering=False, debug=False)

    x_own_d = nc.dram_tensor("x_own", [NT, P, D], F32, kind="ExternalInput")
    x_seq_d = nc.dram_tensor("x_seq", [NTS, P, D], F32, kind="ExternalInput")
    wqkvT_d = nc.dram_tensor("wqkvT", [D, 3 * D], F32, kind="ExternalInput")
    woT_d = nc.dram_tensor("woT", [D, D], F32, kind="ExternalInput")
    gate_d = nc.dram_tensor("gate", [D, 32], F32, kind="ExternalInput")
    wgu_d = nc.dram_tensor("wgu", [NE, D, 2 * FI], F32, kind="ExternalInput")
    wdn_d = nc.dram_tensor("wdn", [NE, FI, D], F32, kind="ExternalInput")
    ropeq_d = nc.dram_tensor("ropeq", [2, P, TOWN], BF16, kind="ExternalInput")
    ropek_d = nc.dram_tensor("ropek", [2, P, TSEQ], BF16, kind="ExternalInput")
    mask_d = nc.dram_tensor("maskm", [16, P, 512], BF16, kind="ExternalInput")
    constb_d = nc.dram_tensor("constb", [3, P, P], BF16, kind="ExternalInput")
    onesrf_d = nc.dram_tensor("onesrf", [1, P], F32, kind="ExternalInput")
    fconst_d = nc.dram_tensor("fconst", [P, 4], F32, kind="ExternalInput")
    vscr_d = nc.dram_tensor("vscr", [NTS, P, D], BF16)
    qscr_d = nc.dram_tensor("qscr", [H, P, TOWN], BF16)
    kscr_d = nc.dram_tensor("kscr", [H, P, TSEQ], BF16)
    out_d = nc.dram_tensor("out", [NT, P, D], F32, kind="ExternalOutput")
    tap_d = {}
    for t in taps:
        shp = {"xn_own": [P, KD * TOWN], "xn_seq": [P, KD * TSEQ],
               "attn_out": [P, H * TOWN], "resid_mid": [NT, P, D],
               "wrt": [P, NT * 32], "qfm0": [P, TOWN], "kfm0": [P, TSEQ]}[t]
        tap_d[t] = nc.dram_tensor("tap_" + t, shp, F32, kind="ExternalOutput")

    def at(name, shape, dtype, off):
        esz = {BF16: 2, F32: 4}[dtype]
        nbytes = int(np.prod(shape[1:])) * esz
        assert SB_BASE + off + nbytes <= SB_TOP, (name, off, nbytes)
        return nc.alloc_sbuf_tensor_at(name, shape, dtype, offset=SB_BASE + off)

    xn_own = at("xn_own", [P, KD * TOWN], BF16, 0)
    onescol = at("onescol", [P, 1], BF16, 32768)
    onesrf = at("onesrf", [1, P], F32, 32800)
    onesb2d = at("onesb2d", [P, P], BF16, 210560)
    permb = at("permb", [P, P], BF16, 33312)
    identb = at("identb", [P, P], BF16, 33568)
    lnrow = at("lnrow", [P, 4], F32, 33824)
    fconst = at("fconst", [P, 4], F32, 33856)
    R2 = 33920
    xn_seq = at("xn_seq", [P, KD * TSEQ], BF16, R2)
    resid = at("resid", [P, NT * D], F32, R2)
    R3 = R2 + 65536
    ropeq = at("ropeq", [P, 2 * TOWN], BF16, R3)
    ropek = at("ropek", [P, 2 * TSEQ], BF16, R3 + 4096)
    maskm = at("maskm", [P, 16 * 512], BF16, R3 + 12288)
    woTb = [at("woTa", [P, 8 * 512], BF16, R3),
            at("woTbb", [P, 8 * 512], BF16, R3 + 8192)]
    silu_g = at("silu_g", [P, KF * TOWN], BF16, R3)
    gated = at("gated", [P, KF * TOWN], BF16, R3 + 12288)
    gateb = at("gateb", [P, KD * 32], BF16, R3 + 24576)
    wrt = at("wrt", [P, NT * 32], F32, R3 + 25600)
    tk_p = at("tk_p", [P, 32], F32, R3 + 26624)
    tk_w = at("tk_w", [P, 32], F32, R3 + 26752)
    tk_keep = at("tk_keep", [P, 32], F32, R3 + 26880)
    tk_mx = at("tk_mx", [P, 4], F32, R3 + 27008)
    tk_s = at("tk_s", [P, 1], F32, R3 + 27136)
    R4 = R3 + 28672
    vw_buf = at("vw_buf", [P, 2 * KD * 512], BF16, R4)
    vstage = at("vstage", [P, 2 * 512], BF16, R4 + 32768)
    wqh = at("wqh", [P, D], BF16, R4)
    wkh = at("wkh", [P, D], BF16, R4 + 4096)
    q_fm = at("q_fm", [P, 2 * TOWN], BF16, R4 + 8192)
    k_fm = at("k_fm", [P, 2 * TSEQ], BF16, R4 + 12288)
    q_sb = at("q_sb", [P, 2 * 512], BF16, R4 + 20480)
    qa = at("qa", [P, 2 * 512], F32, R4 + 22528)
    qb = at("qb", [P, 2 * 512], F32, R4 + 26624)
    rsum_row = at("rsum_row", [1, 512], F32, R4 + 30720)
    rstd_row = at("rstd_row", [1, 512], BF16, R4 + 32768)
    qr_fm = at("qr_fm", [P, 2 * TOWN], BF16, R4)
    kr_fm = at("kr_fm", [P, 2 * TSEQ], BF16, R4 + 4096)
    v_tm = at("v_tm", [P, 2 * D], BF16, R4 + 12288)
    attnT = at("attnT", [P, 2 * 512], BF16, R4 + 20480)
    rb_sb = at("rb_sb", [P, 512], BF16, R4 + 22528)
    wg_bf = at("wg_bf", [P, KD * FI], BF16, R4)
    wu_bf = at("wu_bf", [P, KD * FI], BF16, R4 + 24576)
    R5 = R4 + 49152
    xstage = at("xstage", [P, 2 * D], F32, R5)
    lnsq = at("lnsq", [P, D], F32, R5 + 16384)
    xn_tmp = at("xn_tmp", [P, D], BF16, R5 + 24576)
    attn_out = at("attn_out", [P, H * TOWN], BF16, R5)
    wd_bf = at("wd_bf", [P, KF * D], BF16, R5)

    from contextlib import ExitStack
    ES = ExitStack()
    PB = [ES.enter_context(nc.psum_tensor(f"PB{i}", [P, 512], F32))
          for i in range(8)]

    sem = lambda n: ES.enter_context(nc.semaphore(n))
    SEM = {n: sem(n) for n in
           ("s_wq", "s_wrdy", "s_xq", "s_xrdy", "s_aq", "s_ardy",
            "s_pe", "s_act", "s_dve", "s_out", "s_vs0", "s_vs1", "s_tap")}
    DMA_SEMS = {"s_wq", "s_xq", "s_aq", "s_out", "s_vs0", "s_vs1", "s_tap"}
    cnt = {k: 0 for k in SEM}

    blocks, cur, shadow = [], None, None

    def new_block():
        nonlocal cur, shadow
        cur = {k: [] for k in ("sync", "tensor", "vector", "scalar", "gpsimd")}
        shadow = {k: (set(), set()) for k in cur}
        blocks.append(cur)

    def em(eng, fn, wait=None, inc=None, reads=(), writes=()):
        rs, ws = shadow[eng]
        if eng in ("vector", "scalar", "gpsimd"):
            if (ws & set(reads)) or (set(writes) & (ws | rs)):
                cur[eng].append((lambda e: e.drain(), [], None))
                rs.clear(); ws.clear()
        rs |= set(reads); ws |= set(writes)
        n = None
        if inc is not None:
            step = 16 if inc in DMA_SEMS else 1
            cnt[inc] += step
            n = (SEM[inc], step)
        wl = [(SEM[s], v) for s, v in (wait or [])
              if s is not None and v is not None]
        cur[eng].append((fn, wl, n))
        return cnt[inc] if inc else None

    def q_barrier(eng, q_sem, rdy_sem):
        em(eng, lambda e, v=cnt[q_sem], s=SEM[q_sem]: e.wait_ge(s, v))
        return em(eng, lambda e: e.nop(), inc=rdy_sem)

    def replay_all():
        def mk(eng_ops):
            def f(eng):
                for fn, waits, inc in eng_ops:
                    for s, v in waits:
                        eng.wait_ge(s, v)
                    r = fn(eng)
                    if inc is not None:
                        s, step = inc
                        if r is None:
                            r = eng.nop()
                        r.then_inc(s, step)
            return f
        for ops in blocks:
            with nc.Block() as block:
                block.sync(mk(ops["sync"]))
                block.tensor(mk(ops["tensor"]))
                block.vector(mk(ops["vector"]))
                block.scalar(mk(ops["scalar"]))
                block.gpsimd(mk(ops["gpsimd"]))

    def rearr_w(dram2d, c0, c1):
        return dram2d[:].rearrange("(k p) o -> p k o", p=P)[:, :, c0:c1]

    new_block()
    pT = [PB[0][:].bitcast(BF16), PB[1][:].bitcast(BF16)]

    em("scalar", lambda e: e.dma_start(out=identb[:], in_=constb_d[0]), inc="s_aq")
    em("scalar", lambda e: e.dma_start(out=permb[:], in_=constb_d[1]), inc="s_aq")
    em("vector", lambda e: e.memset(onescol[:], 1.0), inc="s_dve", writes=("onescol",))
    em("scalar", lambda e: e.dma_start(out=onesrf[:], in_=onesrf_d[:]), inc="s_aq")
    em("scalar", lambda e: e.dma_start(out=onesb2d[:], in_=constb_d[2]), inc="s_aq")
    em("scalar", lambda e: e.dma_start(out=fconst[:], in_=fconst_d[:]), inc="s_aq")
    em("scalar", lambda e: e.dma_start(out=ropeq[:, 0:TOWN], in_=ropeq_d[0]), inc="s_aq")
    em("scalar", lambda e: e.dma_start(out=ropeq[:, TOWN:2 * TOWN], in_=ropeq_d[1]), inc="s_aq")
    em("scalar", lambda e: e.dma_start(out=ropek[:, 0:TSEQ], in_=ropek_d[0]), inc="s_aq")
    em("scalar", lambda e: e.dma_start(out=ropek[:, TSEQ:2 * TSEQ], in_=ropek_d[1]), inc="s_aq")
    em("scalar", lambda e: e.dma_start(
        out=maskm[:].rearrange("p (j t) -> p j t", t=512),
        in_=mask_d[:].rearrange("j p t -> p j t")), inc="s_aq")
    consts_rdy = q_barrier("scalar", "s_aq", "s_ardy")
    onescol_dve = cnt["s_dve"]

    vw_rdy, vw_pe = {}, {}

    def stream_vw(og):
        par = og & 1
        em("gpsimd", lambda e, og=og, par=par: e.dma_start(
            out=vw_buf[:, par * KD * 512:(par + 1) * KD * 512].rearrange(
                "p (k o) -> p k o", o=512),
            in_=rearr_w(wqkvT_d, 2 * D + og * 512, 2 * D + (og + 1) * 512)),
            wait=[("s_pe", vw_pe.get(og - 2))], inc="s_wq",
            writes=(f"vw{par}",))
        vw_rdy[og] = q_barrier("gpsimd", "s_wq", "s_wrdy")

    stream_vw(0)
    stream_vw(1)

    x_rdy, ln_done, chunk_act = [], [], []
    tp_guard = [None, None]
    tp_last = None
    a1 = None
    for i in range(NTS + NT):
        sl = i & 1
        w = [("s_dve", ln_done[i - 2])] if i >= 2 else []
        em("sync", lambda e, i=i, sl=sl: e.dma_start(
            out=xstage[:, sl * D:(sl + 1) * D],
            in_=(x_seq_d[i] if i < NTS else x_own_d[i - NTS])),
            wait=w, inc="s_xq")
        x_rdy.append(q_barrier("sync", "s_xq", "s_xrdy"))

        em("vector", lambda e, sl=sl: e.reduce_sum(
            out=lnrow[:, 0:1], in_=xstage[:, sl * D:(sl + 1) * D], axis=AX.X),
            wait=[("s_xrdy", x_rdy[i]), ("s_act", a1)],
            inc="s_dve", reads=(f"xst{sl}",), writes=("lnrow0",))
        v2 = em("vector", lambda e: e.tensor_scalar(
            out=lnrow[:, 1:2], in0=lnrow[:, 0:1], scalar1=-1.0 / D, scalar2=None,
            op0=ALU.mult), inc="s_dve", reads=("lnrow0",), writes=("lnrow1",))
        a1 = em("scalar", lambda e, sl=sl: e.activation(
            out=lnsq[:], in_=xstage[:, sl * D:(sl + 1) * D], func=AF.Square,
            bias=lnrow[:, 1:2], accum_out=lnrow[:, 2:3]),
            wait=[("s_dve", v2)], inc="s_act",
            reads=(f"xst{sl}", "lnrow1"), writes=("lnsq", "lnrow2"))
        em("scalar", lambda e: e.activation(
            out=lnrow[:, 3:4], in_=lnrow[:, 2:3], func=AF.Ln,
            bias=fconst[:, 0:1], scale=1.0 / D), inc="s_act",
            reads=("lnrow2",), writes=("lnrow3",))
        a3 = em("scalar", lambda e: e.activation(
            out=lnrow[:, 3:4], in_=lnrow[:, 3:4], func=AF.Exp,
            scale=-0.5), inc="s_act", reads=("lnrow3",), writes=("lnrow3",))
        v3 = em("vector", lambda e, sl=sl: e.tensor_scalar(
            out=xn_tmp[:], in0=xstage[:, sl * D:(sl + 1) * D],
            scalar1=lnrow[:, 1:2], scalar2=lnrow[:, 3:4],
            op0=ALU.add, op1=ALU.mult),
            wait=[("s_act", a3), ("s_pe", tp_last)], inc="s_dve",
            reads=(f"xst{sl}", "lnrow1", "lnrow3"), writes=("xn_tmp",))
        ln_done.append(v3)
        dst, tcols, tb = ((xn_seq, TSEQ, i) if i < NTS else (xn_own, TOWN, i - NTS))
        for k in range(KD):
            par = k & 1
            w = [("s_dve", v3), ("s_act", tp_guard[par])]
            if i == 0 and k == 0:
                w.append(("s_ardy", consts_rdy))
            pe = em("tensor", lambda e, k=k, par=par: e.transpose(
                out=pT[par][:, 0:P], in_=xn_tmp[:, k * P:(k + 1) * P],
                identity=identb[:]), wait=w, inc="s_pe")
            tp_guard[par] = em(
                "scalar",
                lambda e, k=k, tb=tb, par=par, dst=dst, tcols=tcols:
                e.copy(out=dst[:, k * tcols + tb * P: k * tcols + (tb + 1) * P],
                       in_=pT[par][:, 0:P]),
                wait=[("s_pe", pe)], inc="s_act", writes=("xn",))
        tp_last = cnt["s_pe"]
        chunk_act.append(cnt["s_act"])
    xn_all_act = cnt["s_act"]

    vst_dma = {}
    vst_guard = [None, None]
    for og in range(4):
        par = og & 1
        if og >= 2:
            stream_vw(og)
        for tc in range(NTS):
            idx0 = og * NTS + tc
            ppar = idx0 & 1
            pp = PB[2 + ppar]

            def v_mms(eng, tc=tc, par=par, pp=pp):
                r = None
                for k in range(KD):
                    r = eng.matmul(
                        out=pp[:],
                        lhsT=xn_seq[:, k * TSEQ + tc * P: k * TSEQ + (tc + 1) * P],
                        rhs=vw_buf[:, par * KD * 512 + k * 512:
                                   par * KD * 512 + (k + 1) * 512],
                        start=(k == 0), stop=(k == KD - 1))
                return r
            w = [("s_wrdy", vw_rdy[og]), ("s_act", chunk_act[tc]),
                 ("s_act", vst_guard[ppar])]
            pe = em("tensor", v_mms, wait=w, inc="s_pe")
            idx = og * NTS + tc
            slot = idx & 1
            wv = [("s_pe", pe)]
            if idx >= 2:
                wv.append((f"s_vs{slot}", 16 * (idx // 2)))
            vst_guard[ppar] = em(
                "scalar", lambda e, ppar=ppar, pp=pp:
                e.copy(out=vstage[:, ppar * 512:(ppar + 1) * 512], in_=pp[:]),
                wait=wv, inc="s_act", writes=(f"vstg{ppar}",))
            em("scalar", lambda e, tc=tc, og=og, ppar=ppar: e.dma_start(
                out=vscr_d[tc, :, og * 512:(og + 1) * 512],
                in_=vstage[:, ppar * 512:(ppar + 1) * 512]),
                inc=f"s_vs{slot}", reads=(f"vstg{ppar}",))
        vw_pe[og] = cnt["s_pe"]
    em("scalar", lambda e, v=cnt["s_vs0"], s=SEM["s_vs0"]: e.wait_ge(s, v))
    em("scalar", lambda e, v=cnt["s_vs1"], s=SEM["s_vs1"]: e.wait_ge(s, v))
    vscr_all = em("scalar", lambda e: e.nop(), inc="s_ardy")

    if "xn_own" in tap_d:
        td = tap_d["xn_own"]
        em("gpsimd", lambda e, td=td: e.dma_start(out=td[:], in_=xn_own[:]),
           wait=[("s_act", xn_all_act)], inc="s_tap")
    if "xn_seq" in tap_d:
        td = tap_d["xn_seq"]
        em("gpsimd", lambda e, td=td: e.dma_start(out=td[:], in_=xn_seq[:]),
           wait=[("s_act", xn_all_act)], inc="s_tap")

    new_block()
    pQK = [PB[0], PB[1]]
    pSS = [PB[2][0:1, :], PB[3][0:1, :]]
    pBCr = [PB[4], PB[5]]

    wq_rdy, wk_rdy = {}, {}
    q_pe, k_pe = {}, {}
    spill_rdy = {}

    def stream_qkw(h):
        em("gpsimd", lambda e, h=h: e.dma_start(
            out=wqh[:].rearrange("p (k o) -> p k o", o=HD),
            in_=rearr_w(wqkvT_d, h * HD, (h + 1) * HD)),
            wait=[("s_pe", q_pe.get(h - 1))], inc="s_wq", writes=("wqh",))
        wq_rdy[h] = q_barrier("gpsimd", "s_wq", "s_wrdy")
        em("gpsimd", lambda e, h=h: e.dma_start(
            out=wkh[:].rearrange("p (k o) -> p k o", o=HD),
            in_=rearr_w(wqkvT_d, D + h * HD, D + (h + 1) * HD)),
            wait=[("s_pe", k_pe.get(h - 1))], inc="s_wq", writes=("wkh",))
        wk_rdy[h] = q_barrier("gpsimd", "s_wq", "s_wrdy")

    stream_qkw(0)
    rope_done = {}
    b_guard = [None, None]
    bc_guard = [None, None]
    ss_l1 = [None, None]
    ss_pe = [None, None]
    rstd_rd = [None, None]
    tpar = [0]

    for h in range(H):
        hpar = h & 1
        for which, ntg in (("q", 2), ("k", 4)):
            wbuf, rdy = (wqh, wq_rdy[h]) if which == "q" else (wkh, wk_rdy[h])
            dst = q_fm if which == "q" else k_fm
            dcols = TOWN if which == "q" else TSEQ
            rope = ropeq if which == "q" else ropek
            xsrc, xcols = (xn_own, TOWN) if which == "q" else (xn_seq, TSEQ)
            for tg in range(ntg):
                cp = tpar[0]; tpar[0] ^= 1
                pq = pQK[cp]
                qsb = q_sb[:, cp * 512:(cp + 1) * 512]
                qav = qa[:, cp * 512:(cp + 1) * 512]
                qbv = qb[:, cp * 512:(cp + 1) * 512]

                def qk_mms(eng, wbuf=wbuf, xsrc=xsrc, xcols=xcols, tg=tg, pq=pq):
                    r = None
                    for k in range(KD):
                        r = eng.matmul(
                            out=pq[:],
                            lhsT=wbuf[:, k * HD:(k + 1) * HD],
                            rhs=xsrc[:, k * xcols + tg * 512:
                                     k * xcols + tg * 512 + 512],
                            start=(k == 0), stop=(k == KD - 1))
                    return r
                pe1 = em("tensor", qk_mms,
                         wait=[("s_wrdy", rdy), ("s_act", xn_all_act),
                               ("s_dve", b_guard[cp])], inc="s_pe")
                ev = em("scalar", lambda e, qsb=qsb, pq=pq: e.copy(out=qsb, in_=pq[:]),
                        wait=[("s_pe", pe1), ("s_pe", ss_pe[cp]),
                              ("s_dve", b_guard[cp])],
                        inc="s_act", writes=(f"q_sb{cp}",))
                pe2 = em("tensor", lambda e, qsb=qsb, pq=pq: e.matmul(
                    out=pq[:], lhsT=permb[:], rhs=qsb, start=True, stop=True),
                    wait=[("s_act", ev)], inc="s_pe")
                em("vector", lambda e, rope=rope, tg=tg, qsb=qsb, qav=qav:
                   e.tensor_tensor(out=qav, in0=qsb,
                                   in1=rope[:, tg * 512: tg * 512 + 512],
                                   op=ALU.mult),
                   wait=[("s_act", ev)], inc="s_dve",
                   reads=(f"q_sb{cp}",), writes=(f"qa{cp}",))
                vb = em("vector", lambda e, rope=rope, tg=tg, dcols=dcols, pq=pq, qbv=qbv:
                        e.tensor_tensor(
                            out=qbv, in0=pq[:],
                            in1=rope[:, dcols + tg * 512: dcols + tg * 512 + 512],
                            op=ALU.mult),
                        wait=[("s_pe", pe2)], inc="s_dve", writes=(f"qb{cp}",))
                b_guard[cp] = vb
                vq = em("vector", lambda e, qav=qav, qbv=qbv: e.tensor_add(
                    out=qav, in0=qav, in1=qbv),
                    inc="s_dve", reads=(f"qa{cp}", f"qb{cp}"), writes=(f"qa{cp}",))
                sq = em("scalar", lambda e, qav=qav, qsb=qsb: e.activation(
                    out=qsb, in_=qav, func=AF.Square),
                    wait=[("s_dve", vq)], inc="s_act",
                    reads=(f"qa{cp}",), writes=(f"q_sb{cp}",))
                pe3 = em("tensor", lambda e, qsb=qsb, cp=cp: e.matmul(
                    out=pSS[cp], lhsT=onescol[:], rhs=qsb, start=True, stop=True),
                    wait=[("s_act", sq), ("s_act", ss_l1[cp])], inc="s_pe")
                ss_pe[cp] = pe3
                l1 = em("scalar", lambda e, cp=cp: e.activation(
                    out=rsum_row[:], in_=pSS[cp], func=AF.Ln,
                    bias=fconst[0:1, 1:2], scale=1.0 / HD),
                    wait=[("s_pe", pe3)],
                    inc="s_act", writes=("rsum_row",))
                ss_l1[cp] = l1
                bias2 = fconst[0:1, 2:3] if which == "q" else fconst[0:1, 3:4]
                l2 = em("scalar", lambda e, bias2=bias2: e.activation(
                    out=rstd_row[:], in_=rsum_row[:], func=AF.Exp,
                    scale=-0.5, bias=bias2),
                    wait=[("s_pe", rstd_rd[0]), ("s_pe", rstd_rd[1])],
                    inc="s_act",
                    reads=("rsum_row",), writes=("rstd_row",))
                pe4 = em("tensor", lambda e, cp=cp: e.matmul(
                    out=pBCr[cp][:], lhsT=onesb2d[0:1, :], rhs=rstd_row[:],
                    start=True, stop=True),
                    wait=[("s_act", l2), ("s_dve", bc_guard[cp])], inc="s_pe")
                rstd_rd[cp] = pe4
                w6 = [("s_pe", pe4)]
                if h >= 2:
                    w6.append(("s_ardy", spill_rdy[h - 2]))
                vn = em("vector",
                        lambda e, dst=dst, tg=tg, hpar=hpar, dcols=dcols, cp=cp, qav=qav:
                        e.tensor_tensor(
                            out=dst[:, hpar * dcols + tg * 512:
                                    hpar * dcols + tg * 512 + 512],
                            in0=qav, in1=pBCr[cp][:], op=ALU.mult),
                        wait=w6, inc="s_dve",
                        reads=(f"qa{cp}",), writes=("qkfm",))
                bc_guard[cp] = vn
            if which == "q":
                q_pe[h] = cnt["s_pe"]
            else:
                k_pe[h] = cnt["s_pe"]
        rope_done[h] = cnt["s_dve"]
        if h + 1 < H:
            stream_qkw(h + 1)
        em("scalar", lambda e, h=h, hpar=hpar: e.dma_start(
            out=qscr_d[h], in_=q_fm[:, hpar * TOWN:(hpar + 1) * TOWN]),
            wait=[("s_dve", rope_done[h])], inc="s_aq")
        em("scalar", lambda e, h=h, hpar=hpar: e.dma_start(
            out=kscr_d[h], in_=k_fm[:, hpar * TSEQ:(hpar + 1) * TSEQ]),
            inc="s_aq")
        spill_rdy[h] = q_barrier("scalar", "s_aq", "s_ardy")
    qk_spill_all = cnt["s_ardy"]

    new_block()
    pAV, pBCs = PB[3], PB[4]
    pSC = [PB[1], PB[2]]
    pSUM = PB[7][0:1, :]

    r_rdy = {}
    vhead_pe = {}

    def reload_head(h):
        par = h & 1
        wv = [("s_ardy", vscr_all), ("s_ardy", qk_spill_all),
              ("s_pe", vhead_pe.get(h - 2))]
        em("scalar", lambda e, h=h, par=par: e.dma_start(
            out=qr_fm[:, par * TOWN:(par + 1) * TOWN], in_=qscr_d[h]),
            wait=wv, inc="s_aq", writes=(f"qr{par}",))
        em("scalar", lambda e, h=h, par=par: e.dma_start(
            out=kr_fm[:, par * TSEQ:(par + 1) * TSEQ], in_=kscr_d[h]),
            inc="s_aq", writes=(f"kr{par}",))
        em("scalar", lambda e, h=h, par=par: e.dma_start(
            out=v_tm[:, par * D:(par + 1) * D].rearrange("p (tc o) -> p tc o", o=HD),
            in_=vscr_d[:].rearrange("tc p o -> p tc o")[:, :, h * HD:(h + 1) * HD]),
            inc="s_aq", writes=(f"v_tm{par}",))
        r_rdy[h] = q_barrier("scalar", "s_aq", "s_ardy")

    reload_head(0)
    reload_head(1)

    sc_exp = [None, None]
    av_done = [None]
    sum_cons = [None]
    bcs_cons = [None]
    rsum_rd2 = [None]

    for h in range(H):
        par = h & 1
        for grp in range(2):
            js = list(range(8)) if grp == 0 else list(range(16))
            nj = len(js)
            pe_s = None
            for idx, j in enumerate(js):
                spar = j & 1
                first, last = idx == 0, idx == nj - 1
                w = [("s_ardy", r_rdy[h]), ("s_act", sc_exp[spar])]
                pe = em("tensor", lambda e, j=j, grp=grp, spar=spar, par=par:
                        e.matmul(out=pSC[spar][:],
                                 lhsT=kr_fm[:, par * TSEQ + j * P:
                                           par * TSEQ + (j + 1) * P],
                                 rhs=qr_fm[:, par * TOWN + grp * 512:
                                          par * TOWN + grp * 512 + 512],
                                 start=True, stop=True, skip_group_check=True),
                        wait=w, inc="s_pe")
                ex = em("scalar", lambda e, spar=spar: e.activation(
                    out=attnT[:, spar * 512:(spar + 1) * 512], in_=pSC[spar][:],
                    func=AF.Exp),
                    wait=[("s_pe", pe)], inc="s_act", writes=(f"attnT{spar}",))
                sc_exp[spar] = ex
                mi = j if (grp == 0 or j >= 8) else None
                dep = ("s_act", ex)
                if mi is not None:
                    vm = em("vector", lambda e, spar=spar, mi=mi: e.tensor_mul(
                        out=attnT[:, spar * 512:(spar + 1) * 512],
                        in0=attnT[:, spar * 512:(spar + 1) * 512],
                        in1=maskm[:, mi * 512:(mi + 1) * 512]),
                        wait=[("s_act", ex)], inc="s_dve",
                        reads=(f"attnT{spar}",), writes=(f"attnT{spar}",))
                    dep = ("s_dve", vm)
                w5 = [dep]
                if first:
                    w5 += [("s_dve", av_done[0]), ("s_dve", sum_cons[0])]
                em("tensor", lambda e, j=j, par=par, spar=spar, first=first, last=last:
                   e.matmul(out=pAV[:],
                            lhsT=v_tm[:, par * D + j * P: par * D + (j + 1) * P],
                            rhs=attnT[:, spar * 512:(spar + 1) * 512],
                            start=first, stop=last, skip_group_check=True),
                   wait=w5, inc="s_pe")
                pe_s = em("tensor", lambda e, spar=spar, first=first, last=last:
                          e.matmul(out=pSUM, lhsT=onescol[:],
                                   rhs=attnT[:, spar * 512:(spar + 1) * 512],
                                   start=first, stop=last, skip_group_check=True),
                          inc="s_pe")
            lr1 = em("scalar", lambda e: e.activation(
                out=rsum_row[:], in_=pSUM, func=AF.Ln),
                wait=[("s_pe", pe_s), ("s_pe", rsum_rd2[0])],
                inc="s_act", writes=("rsum_row",))
            lr2 = em("scalar", lambda e: e.activation(
                out=rstd_row[:], in_=rsum_row[:], func=AF.Exp, scale=-1.0),
                wait=[("s_pe", rsum_rd2[0])], inc="s_act",
                reads=("rsum_row",), writes=("rstd_row",))
            pe6 = em("tensor", lambda e: e.matmul(
                out=pBCs[:], lhsT=onesb2d[0:1, :], rhs=rstd_row[:], start=True, stop=True),
                wait=[("s_act", lr2), ("s_act", bcs_cons[0])], inc="s_pe")
            rsum_rd2[0] = pe6
            rbc = em("scalar", lambda e: e.copy(out=rb_sb[:], in_=pBCs[:]),
                     wait=[("s_pe", pe6)], inc="s_act", writes=("rb_sb",))
            bcs_cons[0] = rbc
            ao = em("vector", lambda e, h=h, grp=grp: e.tensor_tensor(
                out=attn_out[:, h * TOWN + grp * 512: h * TOWN + grp * 512 + 512],
                in0=pAV[:], in1=rb_sb[:], op=ALU.mult),
                wait=[("s_act", rbc)], inc="s_dve",
                reads=("rb_sb",), writes=("attn_out",))
            av_done[0] = ao
        vhead_pe[h] = cnt["s_pe"]
        if h + 2 < H:
            reload_head(h + 2)
    attn_done_dve = cnt["s_dve"]

    if "attn_out" in tap_d:
        td = tap_d["attn_out"]
        em("gpsimd", lambda e, td=td: e.dma_start(out=td[:], in_=attn_out[:]),
           wait=[("s_dve", attn_done_dve)], inc="s_tap")

    new_block()

    em("sync", lambda e: e.dma_start(
        out=resid[:].rearrange("p (tb d) -> p tb d", d=D),
        in_=x_own_d[:].rearrange("tb p d -> p tb d")),
        wait=[("s_tap", cnt["s_tap"] or None)], inc="s_xq")
    xown_rdy = q_barrier("sync", "s_xq", "s_xrdy")

    wo_rdy = {}
    wo_buf_pe = {}
    y_evac = {}

    def stream_wo(og, half):
        par = (og * 2 + half) & 1
        em("gpsimd", lambda e, og=og, half=half, par=par: e.dma_start(
            out=woTb[par][:].rearrange("p (k o) -> p k o", o=512),
            in_=woT_d[:].rearrange("(k p) o -> p k o", p=P)[
                :, half * 8:(half + 1) * 8, og * 512:(og + 1) * 512]),
            wait=[("s_pe", wo_buf_pe.get((og * 2 + half) - 2))], inc="s_wq",
            writes=(f"woT{par}",))
        wo_rdy[(og, half)] = q_barrier("gpsimd", "s_wq", "s_wrdy")

    stream_wo(0, 0)
    stream_wo(0, 1)
    for og in range(4):
        for half in range(2):
            par = (og * 2 + half) & 1
            for tb in range(NT):
                def y_mms(eng, tb=tb, half=half, par=par):
                    r = None
                    for cc in range(8):
                        c = half * 8 + cc
                        r = eng.matmul(
                            out=PB[tb][:],
                            lhsT=attn_out[:, c * TOWN + tb * P:
                                          c * TOWN + (tb + 1) * P],
                            rhs=woTb[par][:, cc * 512:(cc + 1) * 512],
                            start=(half == 0 and cc == 0),
                            stop=(half == 1 and cc == 7),
                            skip_group_check=True)
                    return r
                w = [("s_wrdy", wo_rdy[(og, half)]), ("s_dve", y_evac.get(tb))]
                pe = em("tensor", y_mms, wait=w, inc="s_pe")
                if half == 1:
                    y_evac[tb] = em(
                        "vector", lambda e, tb=tb, og=og: e.tensor_tensor(
                            out=resid[:, tb * D + og * 512:
                                      tb * D + og * 512 + 512],
                            in0=resid[:, tb * D + og * 512:
                                      tb * D + og * 512 + 512],
                            in1=PB[tb][:], op=ALU.add),
                        wait=[("s_pe", pe), ("s_xrdy", xown_rdy)], inc="s_dve",
                        reads=("resid",), writes=("resid",))
            wo_buf_pe[og * 2 + half] = cnt["s_pe"]
            nxt = og * 2 + half + 2
            if nxt < 8:
                stream_wo(nxt // 2, nxt % 2)
    wo_done_dve = cnt["s_dve"]

    moe_rdy = {}
    em("gpsimd", lambda e: e.dma_start(
        out=wg_bf[:].rearrange("p (k f) -> p k f", f=FI),
        in_=wgu_d[0].rearrange("(k p) f -> p k f", p=P)[:, :, 0:FI]), inc="s_wq")
    moe_rdy[("g", 0)] = q_barrier("gpsimd", "s_wq", "s_wrdy")
    em("gpsimd", lambda e: e.dma_start(
        out=wu_bf[:].rearrange("p (k f) -> p k f", f=FI),
        in_=wgu_d[0].rearrange("(k p) f -> p k f", p=P)[:, :, FI:2 * FI]), inc="s_wq")
    moe_rdy[("u", 0)] = q_barrier("gpsimd", "s_wq", "s_wrdy")

    resid_tap_out = None
    if "resid_mid" in tap_d:
        td = tap_d["resid_mid"]
        em("gpsimd", lambda e, td=td: e.dma_start(
            out=td[:].rearrange("tb p d -> p tb d"),
            in_=resid[:].rearrange("p (tb d) -> p tb d", d=D)),
            wait=[("s_dve", wo_done_dve)], inc="s_tap")
        resid_tap_out = cnt["s_tap"]

    new_block()
    pH_ = [PB[0], PB[1]]
    pR = PB[2]
    pD_ = [PB[4], PB[5], PB[6], PB[7]]

    em("gpsimd", lambda e: e.dma_start(
        out=gateb[:].rearrange("p (k o) -> p k o", o=32),
        in_=gate_d[:].rearrange("(k p) o -> p k o", p=P)), inc="s_wq")
    gate_rdy = q_barrier("gpsimd", "s_wq", "s_wrdy")

    rt_done = []
    for tb in range(NT):
        def r_mms(eng, tb=tb):
            r = None
            for k in range(KD):
                r = eng.matmul(out=pR[:, 0:32],
                               lhsT=xn_own[:, k * TOWN + tb * P:
                                           k * TOWN + (tb + 1) * P],
                               rhs=gateb[:, k * 32:(k + 1) * 32],
                               start=(k == 0), stop=(k == KD - 1))
            return r
        w = [("s_wrdy", gate_rdy)]
        if tb > 0:
            w.append(("s_dve", rt_done[-1]))
        pe = em("tensor", r_mms, wait=w, inc="s_pe")
        vmx = em("vector", lambda e: e.reduce_max(
            out=tk_s[:], in_=pR[:, 0:32], axis=AX.X, negate=True),
            wait=[("s_pe", pe)], inc="s_dve", writes=("tk_s",))
        aex = em("scalar", lambda e: e.activation(
            out=tk_p[:], in_=pR[:, 0:32], func=AF.Exp, bias=tk_s[:],
            accum_out=tk_mx[:, 0:1]),
            wait=[("s_dve", vmx)], inc="s_act",
            reads=("tk_s",), writes=("tk_p", "tk_mx"))
        em("vector", lambda e: e.reciprocal(out=tk_s[:], in_=tk_mx[:, 0:1]),
           wait=[("s_act", aex)], inc="s_dve",
           reads=("tk_mx",), writes=("tk_s",))
        em("vector", lambda e: e.tensor_scalar(
            out=tk_p[:], in0=tk_p[:], scalar1=tk_s[:], scalar2=None, op0=ALU.mult),
            inc="s_dve", reads=("tk_p", "tk_s"), writes=("tk_p",))
        em("vector", lambda e: e.tensor_copy(out=tk_w[:], in_=tk_p[:]),
           inc="s_dve", reads=("tk_p",), writes=("tk_w",))
        for it in range(4):
            em("vector", lambda e, it=it: e.reduce_max(
                out=tk_mx[:, it:it + 1], in_=tk_w[:], axis=AX.X),
                inc="s_dve", reads=("tk_w",), writes=("tk_mx",))
            if it < 3:
                em("vector", lambda e, it=it: e.tensor_scalar(
                    out=tk_keep[:], in0=tk_w[:], scalar1=tk_mx[:, it:it + 1],
                    scalar2=None, op0=ALU.is_ge),
                    inc="s_dve", reads=("tk_w", "tk_mx"), writes=("tk_keep",))
                em("vector", lambda e: e.scalar_tensor_tensor(
                    out=tk_w[:], in0=tk_keep[:], scalar=-2.0, in1=tk_w[:],
                    op0=ALU.mult, op1=ALU.add),
                    inc="s_dve", reads=("tk_keep", "tk_w"), writes=("tk_w",))
        em("vector", lambda e: e.reduce_sum(out=tk_s[:], in_=tk_mx[:], axis=AX.X),
           inc="s_dve", reads=("tk_mx",), writes=("tk_s",))
        em("vector", lambda e: e.reciprocal(out=tk_s[:], in_=tk_s[:]),
           inc="s_dve", reads=("tk_s",), writes=("tk_s",))
        em("vector", lambda e: e.tensor_scalar(
            out=tk_keep[:], in0=tk_p[:], scalar1=tk_mx[:, 3:4], scalar2=None,
            op0=ALU.is_ge), inc="s_dve",
            reads=("tk_p", "tk_mx"), writes=("tk_keep",))
        v = em("vector", lambda e, tb=tb: e.scalar_tensor_tensor(
            out=wrt[:, tb * 32:(tb + 1) * 32], in0=tk_p[:], scalar=tk_s[:],
            in1=tk_keep[:], op0=ALU.mult, op1=ALU.mult),
            inc="s_dve", reads=("tk_p", "tk_s", "tk_keep"), writes=("wrt",))
        rt_done.append(v)
    router_done = cnt["s_dve"]
    if "wrt" in tap_d:
        td = tap_d["wrt"]
        em("gpsimd", lambda e, td=td: e.dma_start(out=td[:], in_=wrt[:]),
           wait=[("s_dve", router_done)], inc="s_tap")

    pe_after = {}

    def moe_piece(kind, e):
        if kind == "g":
            dst = wg_bf[:].rearrange("p (k f) -> p k f", f=FI)
            src = wgu_d[e].rearrange("(k p) f -> p k f", p=P)[:, :, 0:FI]
            gw = pe_after.get(("G", e - 1))
        elif kind == "u":
            dst = wu_bf[:].rearrange("p (k f) -> p k f", f=FI)
            src = wgu_d[e].rearrange("(k p) f -> p k f", p=P)[:, :, FI:2 * FI]
            gw = pe_after.get(("U", e - 1))
        else:
            dst = wd_bf[:].rearrange("p (k f) -> p k f", f=D)
            src = wdn_d[e].rearrange("(k p) f -> p k f", p=P)
            gw = pe_after.get(("D", e - 1))
        em("gpsimd", lambda e_, dst=dst, src=src: e_.dma_start(out=dst, in_=src),
           wait=[("s_pe", gw)], inc="s_wq")
        moe_rdy[(kind, e)] = q_barrier("gpsimd", "s_wq", "s_wrdy")

    hpar = [0]
    pH_guard = [None, None]
    silu_done = {}
    pass1_done = {}
    last_D_evac = {}

    if cnt["s_tap"]:
        em("gpsimd", lambda e, v=cnt["s_tap"], s=SEM["s_tap"]: e.wait_ge(s, v))
    moe_piece("d", 0)
    for e in range(NE):
        for j in range(KF):
            for tg in range(2):
                p_ = hpar[0]; hpar[0] ^= 1
                pp = pH_[p_]

                def g_mms(eng, j=j, tg=tg, pp=pp):
                    r = None
                    for k in range(KD):
                        r = eng.matmul(
                            out=pp[:],
                            lhsT=wg_bf[:, k * FI + j * P: k * FI + (j + 1) * P],
                            rhs=xn_own[:, k * TOWN + tg * 512:
                                       k * TOWN + tg * 512 + 512],
                            start=(k == 0), stop=(k == KD - 1))
                    return r
                gw = pH_guard[p_]
                pe = em("tensor", g_mms,
                        wait=[("s_wrdy", moe_rdy[("g", e)])] + ([gw] if gw else []),
                        inc="s_pe")
                p1 = pass1_done.get((j, tg))
                sv = em("scalar", lambda e_, j=j, tg=tg, pp=pp: e_.activation(
                    out=silu_g[:, j * TOWN + tg * 512: j * TOWN + tg * 512 + 512],
                    in_=pp[:], func=AF.Silu),
                    wait=[("s_pe", pe)] + ([p1] if p1 else []),
                    inc="s_act", writes=("silu_g",))
                silu_done[(j, tg)] = sv
                pH_guard[p_] = ("s_act", sv)
        pe_after[("G", e)] = cnt["s_pe"]
        if e + 1 < NE:
            moe_piece("g", e + 1)
        for j in range(KF):
            for tg in range(2):
                p_ = hpar[0]; hpar[0] ^= 1
                pp = pH_[p_]

                def u_mms(eng, j=j, tg=tg, pp=pp):
                    r = None
                    for k in range(KD):
                        r = eng.matmul(
                            out=pp[:],
                            lhsT=wu_bf[:, k * FI + j * P: k * FI + (j + 1) * P],
                            rhs=xn_own[:, k * TOWN + tg * 512:
                                       k * TOWN + tg * 512 + 512],
                            start=(k == 0), stop=(k == KD - 1))
                    return r
                gw = pH_guard[p_]
                pe = em("tensor", u_mms,
                        wait=[("s_wrdy", moe_rdy[("u", e)])] + ([gw] if gw else []),
                        inc="s_pe")
                w1 = [("s_pe", pe), ("s_act", silu_done[(j, tg)]),
                      ("s_pe", pe_after.get(("D", e - 1)))]
                v = em("vector", lambda e_, j=j, tg=tg, pp=pp: e_.tensor_tensor(
                    out=gated[:, j * TOWN + tg * 512: j * TOWN + tg * 512 + 512],
                    in0=pp[:],
                    in1=silu_g[:, j * TOWN + tg * 512: j * TOWN + tg * 512 + 512],
                    op=ALU.mult),
                    wait=w1, inc="s_dve", reads=("silu_g",), writes=("gated",))
                pass1_done[(j, tg)] = ("s_dve", v)
                pH_guard[p_] = ("s_dve", v)
        pe_after[("U", e)] = cnt["s_pe"]
        gated_done = cnt["s_dve"]
        if e + 1 < NE:
            moe_piece("u", e + 1)
        for tb in range(NT):
            for m in range(4):
                def d_mms(eng, tb=tb, m=m):
                    r = None
                    for k in range(KF):
                        r = eng.matmul(
                            out=pD_[m][:],
                            lhsT=gated[:, k * TOWN + tb * P:
                                       k * TOWN + (tb + 1) * P],
                            rhs=wd_bf[:, k * D + m * 512: k * D + (m + 1) * 512],
                            start=(k == 0), stop=(k == KF - 1))
                    return r
                w = [("s_wrdy", moe_rdy[("d", e)]), ("s_dve", gated_done),
                     ("s_dve", router_done), ("s_dve", last_D_evac.get(m))]
                pe = em("tensor", d_mms, wait=w, inc="s_pe")
                v = em("vector", lambda e_, tb=tb, m=m, e=e: e_.scalar_tensor_tensor(
                    out=resid[:, tb * D + m * 512: tb * D + (m + 1) * 512],
                    in0=pD_[m][:],
                    scalar=wrt[:, tb * 32 + e: tb * 32 + e + 1],
                    in1=resid[:, tb * D + m * 512: tb * D + (m + 1) * 512],
                    op0=ALU.mult, op1=ALU.add),
                    wait=[("s_pe", pe)] + (
                        [("s_tap", cnt["s_tap"] or None)] if e == 0 else []),
                    inc="s_dve",
                    reads=("resid", "wrt"), writes=("resid",))
                last_D_evac[m] = v
        pe_after[("D", e)] = cnt["s_pe"]
        if e + 1 < NE:
            moe_piece("d", e + 1)
    moe_done = cnt["s_dve"]

    new_block()
    em("sync", lambda e: e.dma_start(
        out=out_d[:].rearrange("tb p d -> p tb d"),
        in_=resid[:].rearrange("p (tb d) -> p tb d", d=D)),
        wait=[("s_dve", moe_done)], inc="s_out")
    em("sync", lambda e, v=cnt["s_out"], s=SEM["s_out"]: e.wait_ge(s, v))
    em("gpsimd", lambda e, v=cnt["s_tap"], s=SEM["s_tap"]: e.wait_ge(s, v))

    with ES:
        replay_all()
    return nc


def _host_tables():
    inv = 1.0 / 10000.0 ** (np.arange(0, HD, 2, dtype=np.float64) / HD)
    invr = np.repeat(inv, 2)
    posk = np.arange(TSEQ, dtype=np.float64)

    def mk(pos):
        fr = invr[:, None] * pos[None, :]
        return np.stack([np.cos(fr), np.sin(fr)]).astype(ml_dtypes.bfloat16)

    ropek = mk(posk)
    ropeq, masks = {}, {}
    tri = (np.arange(P)[:, None] <= np.arange(P)[None, :]).astype(np.float32)
    for par in (0, 1):
        pos = np.concatenate(
            [np.arange((2 * b + par) * P, (2 * b + par + 1) * P)
             for b in range(NT)]).astype(np.float64)
        ropeq[par] = mk(pos)
        blocks = [2 * b + par for b in range(NT)]
        m = np.zeros((16, P, 512), dtype=np.float32)
        for gi, grp in enumerate((blocks[:4], blocks[4:])):
            for jj in range(8):
                j = jj if gi == 0 else 8 + jj
                for bi, B in enumerate(grp):
                    if j < B:
                        m[j, :, bi * P:(bi + 1) * P] = 1.0
                    elif j == B:
                        m[j, :, bi * P:(bi + 1) * P] = tri
        masks[par] = m.astype(ml_dtypes.bfloat16)

    perm = np.zeros((P, P), dtype=np.float32)
    for i in range(HD // 2):
        perm[2 * i + 1, 2 * i] = -1.0
        perm[2 * i, 2 * i + 1] = 1.0
    constb = np.stack([np.eye(P, dtype=np.float32), perm,
                       np.ones((P, P), dtype=np.float32)]).astype(ml_dtypes.bfloat16)
    onesrf = np.ones((1, P), dtype=np.float32)
    fconst = np.tile(np.array([[LN_EPS, RMS_EPS, -0.5 * np.log(HD), 0.0]],
                              dtype=np.float32), (P, 1))
    return ropeq, ropek, constb, onesrf, fconst, masks


def make_in_maps(L0_attn_x, L0_attn_W_qkv, L0_attn_W_o, L0_ffn_gate,
                 L0_ffn_gate_up_proj, L0_ffn_down_proj):
    x = np.ascontiguousarray(np.asarray(L0_attn_x, dtype=np.float32))
    wqkvT = np.ascontiguousarray(np.asarray(L0_attn_W_qkv, dtype=np.float32).T)
    woT = np.ascontiguousarray(np.asarray(L0_attn_W_o, dtype=np.float32).T)
    gate = np.ascontiguousarray(np.asarray(L0_ffn_gate, dtype=np.float32))
    wgu = np.ascontiguousarray(np.asarray(L0_ffn_gate_up_proj, dtype=np.float32))
    wdn = np.ascontiguousarray(np.asarray(L0_ffn_down_proj, dtype=np.float32))
    ropeq, ropek, constb, onesrf, fconst, masks = _host_tables()
    in_maps = []
    for c in range(8):
        s, par = c // 2, c % 2
        xs = x[s]
        x_seq = np.ascontiguousarray(xs.reshape(NTS, P, D))
        x_own = np.ascontiguousarray(
            np.stack([xs[(2 * b + par) * P:(2 * b + par + 1) * P]
                      for b in range(NT)]))
        in_maps.append(dict(
            x_own=x_own, x_seq=x_seq, wqkvT=wqkvT, woT=woT, gate=gate,
            wgu=wgu, wdn=wdn, ropeq=ropeq[par], ropek=ropek,
            maskm=masks[par], constb=constb, onesrf=onesrf, fconst=fconst))
    return in_maps


def kernel(L0_attn_x, L0_attn_W_qkv, L0_attn_W_o, L0_ffn_gate,
           L0_ffn_gate_up_proj, L0_ffn_down_proj):
    in_maps = make_in_maps(L0_attn_x, L0_attn_W_qkv, L0_attn_W_o, L0_ffn_gate,
                           L0_ffn_gate_up_proj, L0_ffn_down_proj)
    nc = build_nc()
    res = run_bass_kernel_spmd(nc, in_maps, core_ids=list(range(8)))
    out = np.empty((4, TSEQ, D), dtype=np.float32)
    for c in range(8):
        s, par = c // 2, c % 2
        r = np.asarray(res.results[c]["out"])
        for b in range(NT):
            out[s, (2 * b + par) * P:(2 * b + par + 1) * P, :] = r[b]
    return out


# version 03
# speedup vs baseline: 1.3039x; 1.2757x over previous
import numpy as np
import ml_dtypes
import concourse.bass as bass
import concourse.mybir as mybir
from concourse.bass_utils import run_bass_kernel_spmd

F32 = mybir.dt.float32
BF16 = mybir.dt.bfloat16
AF = mybir.ActivationFunctionType
ALU = mybir.AluOpType
AX = mybir.AxisListType

P = 128
D = 2048
H = 16
HD = 128
TOWN = 1024
TSEQ = 2048
NE = 32
FI = 768
KD = D // P
KF = FI // P
NT = TOWN // P
NTS = TSEQ // P
LN_EPS = 1e-5
RMS_EPS = 1.1920929e-07

SB_BASE = 16512
SB_TOP = 229344


def build_nc(taps=()):
    nc = bass.Bass("TRN2", target_bir_lowering=False, debug=False)

    x_own_d = nc.dram_tensor("x_own", [NT, P, D], F32, kind="ExternalInput")
    x_seq_d = nc.dram_tensor("x_seq", [NTS, P, D], F32, kind="ExternalInput")
    wqkvT_d = nc.dram_tensor("wqkvT", [D, 3 * D], F32, kind="ExternalInput")
    woT_d = nc.dram_tensor("woT", [D, D], F32, kind="ExternalInput")
    gate_d = nc.dram_tensor("gate", [D, 32], F32, kind="ExternalInput")
    wgu_d = nc.dram_tensor("wgu", [NE, D, 2 * FI], F32, kind="ExternalInput")
    wdn_d = nc.dram_tensor("wdn", [NE, FI, D], F32, kind="ExternalInput")
    ropeq_d = nc.dram_tensor("ropeq", [2, P, TOWN], BF16, kind="ExternalInput")
    ropek_d = nc.dram_tensor("ropek", [2, P, TSEQ], BF16, kind="ExternalInput")
    mask_d = nc.dram_tensor("maskm", [16, P, 512], BF16, kind="ExternalInput")
    constb_d = nc.dram_tensor("constb", [3, P, P], BF16, kind="ExternalInput")
    onesrf_d = nc.dram_tensor("onesrf", [1, P], F32, kind="ExternalInput")
    fconst_d = nc.dram_tensor("fconst", [P, 4], F32, kind="ExternalInput")
    vscr_d = nc.dram_tensor("vscr", [NTS, P, D], BF16)
    qscr_d = nc.dram_tensor("qscr", [H, P, TOWN], BF16)
    kscr_d = nc.dram_tensor("kscr", [H, P, TSEQ], BF16)
    out_d = nc.dram_tensor("out", [NT, P, D], F32, kind="ExternalOutput")
    tap_d = {}
    for t in taps:
        shp = {"xn_own": [P, KD * TOWN], "xn_seq": [P, KD * TSEQ],
               "attn_out": [P, H * TOWN], "resid_mid": [NT, P, D],
               "wrt": [P, NT * 32], "qfm0": [P, TOWN], "kfm0": [P, TSEQ]}[t]
        tap_d[t] = nc.dram_tensor("tap_" + t, shp, F32, kind="ExternalOutput")

    def at(name, shape, dtype, off):
        esz = {BF16: 2, F32: 4}[dtype]
        nbytes = int(np.prod(shape[1:])) * esz
        assert SB_BASE + off + nbytes <= SB_TOP, (name, off, nbytes)
        return nc.alloc_sbuf_tensor_at(name, shape, dtype, offset=SB_BASE + off)

    xn_own = at("xn_own", [P, KD * TOWN], BF16, 0)
    onescol = at("onescol", [P, 1], BF16, 32768)
    onesrf = at("onesrf", [1, P], F32, 32800)
    onesb2d = at("onesb2d", [P, P], BF16, 210560)
    permb = at("permb", [P, P], BF16, 33312)
    identb = at("identb", [P, P], BF16, 33568)
    lnrow = at("lnrow", [P, 4], F32, 33824)
    fconst = at("fconst", [P, 4], F32, 33856)
    R2 = 33920
    xn_seq = at("xn_seq", [P, KD * TSEQ], BF16, R2)
    resid = at("resid", [P, NT * D], F32, R2)
    R3 = R2 + 65536
    ropeq = at("ropeq", [P, 2 * TOWN], BF16, R3)
    ropek = at("ropek", [P, 2 * TSEQ], BF16, R3 + 4096)
    maskm = at("maskm", [P, 16 * 512], BF16, R3 + 12288)
    woTb = [at("woTa", [P, 8 * 512], BF16, R3),
            at("woTbb", [P, 8 * 512], BF16, R3 + 8192)]
    silu_g = at("silu_g", [P, KF * TOWN], BF16, R3)
    gated = at("gated", [P, KF * TOWN], BF16, R3 + 12288)
    gateb = at("gateb", [P, KD * 32], BF16, R3 + 24576)
    wrt = at("wrt", [P, NT * 32], F32, R3 + 25600)
    tk_p = at("tk_p", [P, 32], F32, R3 + 26624)
    tk_w = at("tk_w", [P, 32], F32, R3 + 26752)
    tk_keep = at("tk_keep", [P, 32], F32, R3 + 26880)
    tk_mx = at("tk_mx", [P, 4], F32, R3 + 27008)
    tk_s = at("tk_s", [P, 1], F32, R3 + 27136)
    R4 = R3 + 28672
    vw_buf = at("vw_buf", [P, 2 * KD * 512], BF16, R4)
    vstage = at("vstage", [P, 2 * 512], BF16, R4 + 32768)
    wqh = at("wqh", [P, D], BF16, R4)
    wkh = at("wkh", [P, D], BF16, R4 + 4096)
    q_fm = at("q_fm", [P, 2 * TOWN], BF16, R4 + 8192)
    k_fm = at("k_fm", [P, 2 * TSEQ], BF16, R4 + 12288)
    q_sb = at("q_sb", [P, 4 * 512], BF16, R4 + 20480)
    qa = at("qa", [P, 4 * 512], F32, R4 + 24576)
    qb = at("qb", [P, 4 * 512], F32, R4 + 32768)
    rsum_row = at("rsum_row", [1, 512], F32, R4 + 40960)
    rstd_row = at("rstd_row", [1, 512], BF16, R4 + 43008)
    qr_fm = at("qr_fm", [P, 2 * TOWN], BF16, R4)
    kr_fm = at("kr_fm", [P, 2 * TSEQ], BF16, R4 + 4096)
    v_tm = at("v_tm", [P, 2 * D], BF16, R4 + 12288)
    attnT = at("attnT", [P, 4 * 512], BF16, R4 + 20480)
    rb_sb = at("rb_sb", [P, 512], BF16, R4 + 24576 + 4096)
    wg_bf = at("wg_bf", [P, KD * FI], BF16, R4)
    wu_bf = at("wu_bf", [P, KD * FI], BF16, R4 + 24576)
    R5 = R4 + 49152
    xstage = at("xstage", [P, 2 * D], F32, R5)
    lnsq = at("lnsq", [P, D], F32, R5 + 16384)
    xn_tmp = at("xn_tmp", [P, D], BF16, R5 + 24576)
    attn_out = at("attn_out", [P, H * TOWN], BF16, R5)
    wd_bf = at("wd_bf", [P, KF * D], BF16, R5)

    from contextlib import ExitStack
    ES = ExitStack()
    PB = [ES.enter_context(nc.psum_tensor(f"PB{i}", [P, 512], F32))
          for i in range(8)]

    sem = lambda n: ES.enter_context(nc.semaphore(n))
    SEM = {n: sem(n) for n in
           ("s_wq", "s_wrdy", "s_xq", "s_xrdy", "s_aq", "s_ardy",
            "s_pe", "s_act", "s_dve", "s_out", "s_vs0", "s_vs1", "s_tap")}
    DMA_SEMS = {"s_wq", "s_xq", "s_aq", "s_out", "s_vs0", "s_vs1", "s_tap"}
    cnt = {k: 0 for k in SEM}

    blocks, cur, shadow = [], None, None

    def new_block():
        nonlocal cur, shadow
        cur = {k: [] for k in ("sync", "tensor", "vector", "scalar", "gpsimd")}
        shadow = {k: (set(), set()) for k in cur}
        blocks.append(cur)

    def em(eng, fn, wait=None, inc=None, reads=(), writes=()):
        rs, ws = shadow[eng]
        if eng in ("vector", "scalar", "gpsimd"):
            if (ws & set(reads)) or (set(writes) & (ws | rs)):
                cur[eng].append((lambda e: e.drain(), [], None))
                rs.clear(); ws.clear()
        rs |= set(reads); ws |= set(writes)
        n = None
        if inc is not None:
            step = 16 if inc in DMA_SEMS else 1
            cnt[inc] += step
            n = (SEM[inc], step)
        wl = [(SEM[s], v) for s, v in (wait or [])
              if s is not None and v is not None]
        cur[eng].append((fn, wl, n))
        return cnt[inc] if inc else None

    def q_barrier(eng, q_sem, rdy_sem):
        em(eng, lambda e, v=cnt[q_sem], s=SEM[q_sem]: e.wait_ge(s, v))
        return em(eng, lambda e: e.nop(), inc=rdy_sem)

    def replay_all():
        def mk(eng_ops):
            def f(eng):
                for fn, waits, inc in eng_ops:
                    for s, v in waits:
                        eng.wait_ge(s, v)
                    r = fn(eng)
                    if inc is not None:
                        s, step = inc
                        if r is None:
                            r = eng.nop()
                        r.then_inc(s, step)
            return f
        for ops in blocks:
            with nc.Block() as block:
                block.sync(mk(ops["sync"]))
                block.tensor(mk(ops["tensor"]))
                block.vector(mk(ops["vector"]))
                block.scalar(mk(ops["scalar"]))
                block.gpsimd(mk(ops["gpsimd"]))

    def rearr_w(dram2d, c0, c1):
        return dram2d[:].rearrange("(k p) o -> p k o", p=P)[:, :, c0:c1]

    new_block()
    pT = [PB[0][:].bitcast(BF16), PB[1][:].bitcast(BF16)]

    em("scalar", lambda e: e.dma_start(out=identb[:], in_=constb_d[0]), inc="s_aq")
    em("scalar", lambda e: e.dma_start(out=permb[:], in_=constb_d[1]), inc="s_aq")
    em("vector", lambda e: e.memset(onescol[:], 1.0), inc="s_dve", writes=("onescol",))
    em("scalar", lambda e: e.dma_start(out=onesrf[:], in_=onesrf_d[:]), inc="s_aq")
    em("scalar", lambda e: e.dma_start(out=onesb2d[:], in_=constb_d[2]), inc="s_aq")
    em("scalar", lambda e: e.dma_start(out=fconst[:], in_=fconst_d[:]), inc="s_aq")
    em("scalar", lambda e: e.dma_start(out=ropeq[:, 0:TOWN], in_=ropeq_d[0]), inc="s_aq")
    em("scalar", lambda e: e.dma_start(out=ropeq[:, TOWN:2 * TOWN], in_=ropeq_d[1]), inc="s_aq")
    em("scalar", lambda e: e.dma_start(out=ropek[:, 0:TSEQ], in_=ropek_d[0]), inc="s_aq")
    em("scalar", lambda e: e.dma_start(out=ropek[:, TSEQ:2 * TSEQ], in_=ropek_d[1]), inc="s_aq")
    em("scalar", lambda e: e.dma_start(
        out=maskm[:].rearrange("p (j t) -> p j t", t=512),
        in_=mask_d[:].rearrange("j p t -> p j t")), inc="s_aq")
    consts_rdy = q_barrier("scalar", "s_aq", "s_ardy")
    onescol_dve = cnt["s_dve"]

    vw_rdy, vw_pe = {}, {}

    def stream_vw(og):
        par = og & 1
        em("gpsimd", lambda e, og=og, par=par: e.dma_start(
            out=vw_buf[:, par * KD * 512:(par + 1) * KD * 512].rearrange(
                "p (k o) -> p k o", o=512),
            in_=rearr_w(wqkvT_d, 2 * D + og * 512, 2 * D + (og + 1) * 512)),
            wait=[("s_pe", vw_pe.get(og - 2))], inc="s_wq",
            writes=(f"vw{par}",))
        vw_rdy[og] = q_barrier("gpsimd", "s_wq", "s_wrdy")

    stream_vw(0)
    stream_vw(1)

    x_rdy, ln_done, chunk_act = [], [], []
    tp_guard = [None, None]
    tp_last = None
    a1 = None
    for i in range(NTS + NT):
        sl = i & 1
        w = [("s_dve", ln_done[i - 2])] if i >= 2 else []
        em("sync", lambda e, i=i, sl=sl: e.dma_start(
            out=xstage[:, sl * D:(sl + 1) * D],
            in_=(x_seq_d[i] if i < NTS else x_own_d[i - NTS])),
            wait=w, inc="s_xq")
        x_rdy.append(q_barrier("sync", "s_xq", "s_xrdy"))

        em("vector", lambda e, sl=sl: e.reduce_sum(
            out=lnrow[:, 0:1], in_=xstage[:, sl * D:(sl + 1) * D], axis=AX.X),
            wait=[("s_xrdy", x_rdy[i]), ("s_act", a1)],
            inc="s_dve", reads=(f"xst{sl}",), writes=("lnrow0",))
        v2 = em("vector", lambda e: e.tensor_scalar(
            out=lnrow[:, 1:2], in0=lnrow[:, 0:1], scalar1=-1.0 / D, scalar2=None,
            op0=ALU.mult), inc="s_dve", reads=("lnrow0",), writes=("lnrow1",))
        a1 = em("scalar", lambda e, sl=sl: e.activation(
            out=lnsq[:], in_=xstage[:, sl * D:(sl + 1) * D], func=AF.Square,
            bias=lnrow[:, 1:2], accum_out=lnrow[:, 2:3]),
            wait=[("s_dve", v2)], inc="s_act",
            reads=(f"xst{sl}", "lnrow1"), writes=("lnsq", "lnrow2"))
        em("scalar", lambda e: e.activation(
            out=lnrow[:, 3:4], in_=lnrow[:, 2:3], func=AF.Ln,
            bias=fconst[:, 0:1], scale=1.0 / D), inc="s_act",
            reads=("lnrow2",), writes=("lnrow3",))
        a3 = em("scalar", lambda e: e.activation(
            out=lnrow[:, 3:4], in_=lnrow[:, 3:4], func=AF.Exp,
            scale=-0.5), inc="s_act", reads=("lnrow3",), writes=("lnrow3",))
        v3 = em("vector", lambda e, sl=sl: e.tensor_scalar(
            out=xn_tmp[:], in0=xstage[:, sl * D:(sl + 1) * D],
            scalar1=lnrow[:, 1:2], scalar2=lnrow[:, 3:4],
            op0=ALU.add, op1=ALU.mult),
            wait=[("s_act", a3), ("s_pe", tp_last)], inc="s_dve",
            reads=(f"xst{sl}", "lnrow1", "lnrow3"), writes=("xn_tmp",))
        ln_done.append(v3)
        dst, tcols, tb = ((xn_seq, TSEQ, i) if i < NTS else (xn_own, TOWN, i - NTS))
        for k in range(KD):
            par = k & 1
            w = [("s_dve", v3), ("s_act", tp_guard[par])]
            if i == 0 and k == 0:
                w.append(("s_ardy", consts_rdy))
            pe = em("tensor", lambda e, k=k, par=par: e.transpose(
                out=pT[par][:, 0:P], in_=xn_tmp[:, k * P:(k + 1) * P],
                identity=identb[:]), wait=w, inc="s_pe")
            tp_guard[par] = em(
                "scalar",
                lambda e, k=k, tb=tb, par=par, dst=dst, tcols=tcols:
                e.copy(out=dst[:, k * tcols + tb * P: k * tcols + (tb + 1) * P],
                       in_=pT[par][:, 0:P]),
                wait=[("s_pe", pe)], inc="s_act", writes=("xn",))
        tp_last = cnt["s_pe"]
        chunk_act.append(cnt["s_act"])
    xn_all_act = cnt["s_act"]

    vst_dma = {}
    vst_guard = [None, None]
    for og in range(4):
        par = og & 1
        if og >= 2:
            stream_vw(og)
        for tc in range(NTS):
            idx0 = og * NTS + tc
            ppar = idx0 & 1
            pp = PB[2 + ppar]

            def v_mms(eng, tc=tc, par=par, pp=pp):
                r = None
                for k in range(KD):
                    r = eng.matmul(
                        out=pp[:],
                        lhsT=xn_seq[:, k * TSEQ + tc * P: k * TSEQ + (tc + 1) * P],
                        rhs=vw_buf[:, par * KD * 512 + k * 512:
                                   par * KD * 512 + (k + 1) * 512],
                        start=(k == 0), stop=(k == KD - 1))
                return r
            w = [("s_wrdy", vw_rdy[og]), ("s_act", chunk_act[tc]),
                 ("s_act", vst_guard[ppar])]
            pe = em("tensor", v_mms, wait=w, inc="s_pe")
            idx = og * NTS + tc
            slot = idx & 1
            wv = [("s_pe", pe)]
            if idx >= 2:
                wv.append((f"s_vs{slot}", 16 * (idx // 2)))
            vst_guard[ppar] = em(
                "scalar", lambda e, ppar=ppar, pp=pp:
                e.copy(out=vstage[:, ppar * 512:(ppar + 1) * 512], in_=pp[:]),
                wait=wv, inc="s_act", writes=(f"vstg{ppar}",))
            em("scalar", lambda e, tc=tc, og=og, ppar=ppar: e.dma_start(
                out=vscr_d[tc, :, og * 512:(og + 1) * 512],
                in_=vstage[:, ppar * 512:(ppar + 1) * 512]),
                inc=f"s_vs{slot}", reads=(f"vstg{ppar}",))
        vw_pe[og] = cnt["s_pe"]
    em("scalar", lambda e, v=cnt["s_vs0"], s=SEM["s_vs0"]: e.wait_ge(s, v))
    em("scalar", lambda e, v=cnt["s_vs1"], s=SEM["s_vs1"]: e.wait_ge(s, v))
    vscr_all = em("scalar", lambda e: e.nop(), inc="s_ardy")

    if "xn_own" in tap_d:
        td = tap_d["xn_own"]
        em("gpsimd", lambda e, td=td: e.dma_start(out=td[:], in_=xn_own[:]),
           wait=[("s_act", xn_all_act)], inc="s_tap")
    if "xn_seq" in tap_d:
        td = tap_d["xn_seq"]
        em("gpsimd", lambda e, td=td: e.dma_start(out=td[:], in_=xn_seq[:]),
           wait=[("s_act", xn_all_act)], inc="s_tap")

    new_block()
    pQK = [PB[0], PB[1], PB[2], PB[3]]
    pSS = [PB[4][0:1, :], PB[5][0:1, :]]
    pBCr = [PB[6], PB[7]]

    wq_rdy, wk_rdy = {}, {}
    spill_rdy = {}

    tiles = []
    for h in range(H):
        for which, ntg in (("q", 2), ("k", 4)):
            for tg in range(ntg):
                tiles.append((h, which, tg))
    NTI = len(tiles)

    st = [dict() for _ in range(NTI)]

    def tinfo(i):
        h, which, tg = tiles[i]
        wbuf = wqh if which == "q" else wkh
        rdy = (wq_rdy if which == "q" else wk_rdy)[h]
        dst = q_fm if which == "q" else k_fm
        dcols = TOWN if which == "q" else TSEQ
        rope = ropeq if which == "q" else ropek
        xsrc, xcols = (xn_own, TOWN) if which == "q" else (xn_seq, TSEQ)
        bias2 = fconst[0:1, 2:3] if which == "q" else fconst[0:1, 3:4]
        return h, which, tg, wbuf, rdy, dst, dcols, rope, xsrc, xcols, bias2

    def stream_qkw(h):
        wq = [("s_pe", st[6 * (h - 1) + 1].get("pe1"))] if h > 0 else None
        em("gpsimd", lambda e, h=h: e.dma_start(
            out=wqh[:].rearrange("p (k o) -> p k o", o=HD),
            in_=rearr_w(wqkvT_d, h * HD, (h + 1) * HD)),
            wait=wq, inc="s_wq", writes=("wqh",))
        wq_rdy[h] = q_barrier("gpsimd", "s_wq", "s_wrdy")
        wk = [("s_pe", st[6 * (h - 1) + 5].get("pe1"))] if h > 0 else None
        em("gpsimd", lambda e, h=h: e.dma_start(
            out=wkh[:].rearrange("p (k o) -> p k o", o=HD),
            in_=rearr_w(wqkvT_d, D + h * HD, D + (h + 1) * HD)),
            wait=wk, inc="s_wq", writes=("wkh",))
        wk_rdy[h] = q_barrier("gpsimd", "s_wq", "s_wrdy")

    stream_qkw(0)

    def emit_mmg(i):
        h, which, tg, wbuf, rdy, dst, dcols, rope, xsrc, xcols, bias2 = tinfo(i)
        cp = i % 4
        pq = pQK[cp]

        def qk_mms(eng, wbuf=wbuf, xsrc=xsrc, xcols=xcols, tg=tg, pq=pq):
            r = None
            for k in range(KD):
                r = eng.matmul(
                    out=pq[:], lhsT=wbuf[:, k * HD:(k + 1) * HD],
                    rhs=xsrc[:, k * xcols + tg * 512: k * xcols + tg * 512 + 512],
                    start=(k == 0), stop=(k == KD - 1))
            return r
        w = [("s_wrdy", rdy), ("s_act", xn_all_act)]
        if i >= 4:
            w.append(("s_dve", st[i - 4].get("vb")))
        st[i]["pe1"] = em("tensor", qk_mms, wait=w, inc="s_pe")
        qsb = q_sb[:, cp * 512:(cp + 1) * 512]
        wv = [("s_pe", st[i]["pe1"])]
        if i >= 4:
            wv += [("s_pe", st[i - 4].get("pe3")), ("s_dve", st[i - 4].get("vn"))]
        st[i]["ev"] = em("scalar", lambda e, qsb=qsb, pq=pq: e.copy(out=qsb, in_=pq[:]),
                         wait=wv, inc="s_act", writes=(f"q_sb{cp}",))
        qav = qa[:, cp * 512:(cp + 1) * 512]
        wva = [("s_act", st[i]["ev"])]
        if i >= 4:
            wva.append(("s_act", st[i - 4].get("sq")))
        em("vector", lambda e, rope=rope, tg=tg, qsb=qsb, qav=qav:
           e.tensor_tensor(out=qav, in0=qsb,
                           in1=rope[:, tg * 512: tg * 512 + 512], op=ALU.mult),
           wait=wva, inc="s_dve", reads=(f"q_sb{cp}",), writes=(f"qa{cp}",))

    def emit_perm(i):
        h, which, tg, wbuf, rdy, dst, dcols, rope, xsrc, xcols, bias2 = tinfo(i)
        cp = i % 4
        pq = pQK[cp]
        qsb = q_sb[:, cp * 512:(cp + 1) * 512]
        qav = qa[:, cp * 512:(cp + 1) * 512]
        qbv = qb[:, cp * 512:(cp + 1) * 512]
        pe2 = em("tensor", lambda e, qsb=qsb, pq=pq: e.matmul(
            out=pq[:], lhsT=permb[:], rhs=qsb, start=True, stop=True),
            wait=[("s_act", st[i]["ev"])], inc="s_pe")
        st[i]["pe2"] = pe2
        vb = em("vector", lambda e, rope=rope, tg=tg, dcols=dcols, pq=pq, qbv=qbv:
                e.tensor_tensor(out=qbv, in0=pq[:],
                                in1=rope[:, dcols + tg * 512: dcols + tg * 512 + 512],
                                op=ALU.mult),
                wait=[("s_pe", pe2)], inc="s_dve", writes=(f"qb{cp}",))
        st[i]["vb"] = vb
        em("vector", lambda e, qav=qav, qbv=qbv: e.tensor_add(
            out=qav, in0=qav, in1=qbv),
            inc="s_dve", reads=(f"qa{cp}", f"qb{cp}"), writes=(f"qa{cp}",))
        st[i]["vq"] = cnt["s_dve"]
        st[i]["sq"] = em("scalar", lambda e, qav=qav, qsb=qsb: e.activation(
            out=qsb, in_=qav, func=AF.Square),
            wait=[("s_dve", st[i]["vq"]), ("s_pe", pe2)], inc="s_act",
            reads=(f"qa{cp}",), writes=(f"q_sb{cp}",))

    def emit_ss(i):
        cp = i % 4
        cp2 = i % 2
        qsb = q_sb[:, cp * 512:(cp + 1) * 512]
        w = [("s_act", st[i]["sq"])]
        if i >= 2:
            w.append(("s_act", st[i - 2].get("l1")))
        st[i]["pe3"] = em("tensor", lambda e, qsb=qsb, cp2=cp2: e.matmul(
            out=pSS[cp2], lhsT=onescol[:], rhs=qsb, start=True, stop=True),
            wait=w, inc="s_pe")
        st[i]["l1"] = em("scalar", lambda e, cp2=cp2: e.activation(
            out=rsum_row[:], in_=pSS[cp2], func=AF.Ln,
            bias=fconst[0:1, 1:2], scale=1.0 / HD),
            wait=[("s_pe", st[i]["pe3"])], inc="s_act", writes=("rsum_row",))
        h, which, tg, wbuf, rdy, dst, dcols, rope, xsrc, xcols, bias2 = tinfo(i)
        wl2 = []
        if i >= 1:
            wl2.append(("s_pe", st[i - 1].get("pe4")))
        st[i]["l2"] = em("scalar", lambda e, bias2=bias2: e.activation(
            out=rstd_row[:], in_=rsum_row[:], func=AF.Exp, scale=-0.5, bias=bias2),
            wait=wl2, inc="s_act", reads=("rsum_row",), writes=("rstd_row",))

    def emit_bc(i):
        h, which, tg, wbuf, rdy, dst, dcols, rope, xsrc, xcols, bias2 = tinfo(i)
        cp = i % 4
        cp2 = i % 2
        qav = qa[:, cp * 512:(cp + 1) * 512]
        w = [("s_act", st[i]["l2"])]
        if i >= 2:
            w.append(("s_dve", st[i - 2].get("vn")))
        st[i]["pe4"] = em("tensor", lambda e, cp2=cp2: e.matmul(
            out=pBCr[cp2][:], lhsT=onesb2d[0:1, :], rhs=rstd_row[:],
            start=True, stop=True),
            wait=w, inc="s_pe")
        hpar = h & 1
        w6 = [("s_pe", st[i]["pe4"])]
        if h >= 2:
            w6.append(("s_ardy", spill_rdy[h - 2]))
        st[i]["vn"] = em("vector",
                         lambda e, dst=dst, tg=tg, hpar=hpar, dcols=dcols, cp2=cp2, qav=qav:
                         e.tensor_tensor(
                             out=dst[:, hpar * dcols + tg * 512:
                                     hpar * dcols + tg * 512 + 512],
                             in0=qav, in1=pBCr[cp2][:], op=ALU.mult),
                         wait=w6, inc="s_dve",
                         reads=(f"qa{cp}",), writes=("qkfm",))
        if tiles[i][1] == "k" and tiles[i][2] == 3:
            em("scalar", lambda e, h=h, hpar=hpar: e.dma_start(
                out=qscr_d[h], in_=q_fm[:, hpar * TOWN:(hpar + 1) * TOWN]),
                wait=[("s_dve", st[i]["vn"])], inc="s_aq")
            em("scalar", lambda e, h=h, hpar=hpar: e.dma_start(
                out=kscr_d[h], in_=k_fm[:, hpar * TSEQ:(hpar + 1) * TSEQ]),
                inc="s_aq")
            spill_rdy[h] = q_barrier("scalar", "s_aq", "s_ardy")

    for i in range(NTI + 3):
        if i < NTI:
            emit_mmg(i)
            if tiles[i][1] == "k" and tiles[i][2] == 3 and tiles[i][0] + 1 < H:
                stream_qkw(tiles[i][0] + 1)
        if i - 1 >= 0 and i - 1 < NTI:
            emit_perm(i - 1)
        if i - 3 >= 0:
            emit_bc(i - 3)
        if i - 2 >= 0 and i - 2 < NTI:
            emit_ss(i - 2)
    qk_spill_all = cnt["s_ardy"]

    new_block()
    pSC = [PB[0], PB[1], PB[2], PB[3]]
    pAV, pBCs = PB[4], PB[5]
    pSUM = PB[7][0:1, :]

    r_rdy = {}
    vhead_pe = {}

    def reload_head(h):
        par = h & 1
        wv = [("s_ardy", vscr_all), ("s_ardy", qk_spill_all),
              ("s_pe", vhead_pe.get(h - 2))]
        em("scalar", lambda e, h=h, par=par: e.dma_start(
            out=qr_fm[:, par * TOWN:(par + 1) * TOWN], in_=qscr_d[h]),
            wait=wv, inc="s_aq", writes=(f"qr{par}",))
        em("scalar", lambda e, h=h, par=par: e.dma_start(
            out=kr_fm[:, par * TSEQ:(par + 1) * TSEQ], in_=kscr_d[h]),
            inc="s_aq", writes=(f"kr{par}",))
        em("scalar", lambda e, h=h, par=par: e.dma_start(
            out=v_tm[:, par * D:(par + 1) * D].rearrange("p (tc o) -> p tc o", o=HD),
            in_=vscr_d[:].rearrange("tc p o -> p tc o")[:, :, h * HD:(h + 1) * HD]),
            inc="s_aq", writes=(f"v_tm{par}",))
        r_rdy[h] = q_barrier("scalar", "s_aq", "s_ardy")

    reload_head(0)
    reload_head(1)

    ents = []
    for h in range(H):
        for grp in range(2):
            js = list(range(8)) if grp == 0 else list(range(16))
            for idx, j in enumerate(js):
                ents.append((h, grp, j, idx == 0, idx == len(js) - 1))
    NEN = len(ents)
    est = [dict() for _ in range(NEN)]
    av_done = [None]
    sum_lr1 = [None]
    bcs_cons = [None]
    rsum_rd2 = [None]

    def emit_scores(t):
        h, grp, j, first, last = ents[t]
        par = h & 1
        cp = t % 4
        w = [("s_ardy", r_rdy[h])]
        if t >= 4:
            w += [("s_act", est[t - 4].get("ex")), ("s_pe", est[t - 4].get("av_pe"))]
        pe = em("tensor", lambda e, j=j, grp=grp, par=par, cp=cp:
                e.matmul(out=pSC[cp][:],
                         lhsT=kr_fm[:, par * TSEQ + j * P: par * TSEQ + (j + 1) * P],
                         rhs=qr_fm[:, par * TOWN + grp * 512:
                                  par * TOWN + grp * 512 + 512],
                         start=True, stop=True, skip_group_check=True),
                wait=w, inc="s_pe")
        est[t]["pe"] = pe
        ex = em("scalar", lambda e, cp=cp: e.activation(
            out=attnT[:, cp * 512:(cp + 1) * 512], in_=pSC[cp][:], func=AF.Exp),
            wait=[("s_pe", pe)] + ([("s_pe", est[t - 4].get("av_pe"))] if t >= 4 else []),
            inc="s_act", writes=(f"attnT{cp}",))
        est[t]["ex"] = ex
        mi = j if (grp == 0 or j >= 8) else None
        dep = ("s_act", ex)
        if mi is not None:
            vm = em("vector", lambda e, cp=cp, mi=mi: e.tensor_mul(
                out=attnT[:, cp * 512:(cp + 1) * 512],
                in0=attnT[:, cp * 512:(cp + 1) * 512],
                in1=maskm[:, mi * 512:(mi + 1) * 512]),
                wait=[("s_act", ex)], inc="s_dve",
                reads=(f"attnT{cp}",), writes=(f"attnT{cp}",))
            dep = ("s_dve", vm)
        est[t]["dep"] = dep

    def emit_av(t):
        h, grp, j, first, last = ents[t]
        par = h & 1
        cp = t % 4
        w5 = [est[t]["dep"]]
        if first:
            w5 += [("s_dve", av_done[0]), ("s_act", sum_lr1[0])]
        em("tensor", lambda e, j=j, par=par, cp=cp, first=first, last=last:
           e.matmul(out=pAV[:],
                    lhsT=v_tm[:, par * D + j * P: par * D + (j + 1) * P],
                    rhs=attnT[:, cp * 512:(cp + 1) * 512],
                    start=first, stop=last, skip_group_check=True),
           wait=w5, inc="s_pe")
        pe_s = em("tensor", lambda e, cp=cp, first=first, last=last:
                  e.matmul(out=pSUM, lhsT=onescol[:],
                           rhs=attnT[:, cp * 512:(cp + 1) * 512],
                           start=first, stop=last, skip_group_check=True),
                  inc="s_pe")
        est[t]["av_pe"] = pe_s
        if last:
            lr1 = em("scalar", lambda e: e.activation(
                out=rsum_row[:], in_=pSUM, func=AF.Ln),
                wait=[("s_pe", pe_s)], inc="s_act", writes=("rsum_row",))
            sum_lr1[0] = lr1
            lr2 = em("scalar", lambda e: e.activation(
                out=rstd_row[:], in_=rsum_row[:], func=AF.Exp, scale=-1.0),
                wait=[("s_pe", rsum_rd2[0])], inc="s_act",
                reads=("rsum_row",), writes=("rstd_row",))
            pe6 = em("tensor", lambda e: e.matmul(
                out=pBCs[:], lhsT=onesb2d[0:1, :], rhs=rstd_row[:],
                start=True, stop=True),
                wait=[("s_act", lr2), ("s_act", bcs_cons[0])], inc="s_pe")
            rsum_rd2[0] = pe6
            rbc = em("scalar", lambda e: e.copy(out=rb_sb[:], in_=pBCs[:]),
                     wait=[("s_pe", pe6)], inc="s_act", writes=("rb_sb",))
            bcs_cons[0] = rbc
            ao = em("vector", lambda e, h=h, grp=grp: e.tensor_tensor(
                out=attn_out[:, h * TOWN + grp * 512: h * TOWN + grp * 512 + 512],
                in0=pAV[:], in1=rb_sb[:], op=ALU.mult),
                wait=[("s_act", rbc)], inc="s_dve",
                reads=("rb_sb",), writes=("attn_out",))
            av_done[0] = ao
            if grp == 1:
                vhead_pe[h] = cnt["s_pe"]
                if h + 2 < H:
                    reload_head(h + 2)

    for t in range(NEN + 2):
        if t < NEN:
            emit_scores(t)
        if t - 2 >= 0:
            emit_av(t - 2)
    attn_done_dve = cnt["s_dve"]

    if "attn_out" in tap_d:
        td = tap_d["attn_out"]
        em("gpsimd", lambda e, td=td: e.dma_start(out=td[:], in_=attn_out[:]),
           wait=[("s_dve", attn_done_dve)], inc="s_tap")

    new_block()

    em("sync", lambda e: e.dma_start(
        out=resid[:].rearrange("p (tb d) -> p tb d", d=D),
        in_=x_own_d[:].rearrange("tb p d -> p tb d")),
        wait=[("s_tap", cnt["s_tap"] or None)], inc="s_xq")
    xown_rdy = q_barrier("sync", "s_xq", "s_xrdy")

    wo_rdy = {}
    wo_buf_pe = {}
    y_evac = {}

    def stream_wo(og, half):
        par = (og * 2 + half) & 1
        em("gpsimd", lambda e, og=og, half=half, par=par: e.dma_start(
            out=woTb[par][:].rearrange("p (k o) -> p k o", o=512),
            in_=woT_d[:].rearrange("(k p) o -> p k o", p=P)[
                :, half * 8:(half + 1) * 8, og * 512:(og + 1) * 512]),
            wait=[("s_pe", wo_buf_pe.get((og * 2 + half) - 2))], inc="s_wq",
            writes=(f"woT{par}",))
        wo_rdy[(og, half)] = q_barrier("gpsimd", "s_wq", "s_wrdy")

    stream_wo(0, 0)
    stream_wo(0, 1)
    for og in range(4):
        for half in range(2):
            par = (og * 2 + half) & 1
            for tb in range(NT):
                def y_mms(eng, tb=tb, half=half, par=par):
                    r = None
                    for cc in range(8):
                        c = half * 8 + cc
                        r = eng.matmul(
                            out=PB[tb][:],
                            lhsT=attn_out[:, c * TOWN + tb * P:
                                          c * TOWN + (tb + 1) * P],
                            rhs=woTb[par][:, cc * 512:(cc + 1) * 512],
                            start=(half == 0 and cc == 0),
                            stop=(half == 1 and cc == 7),
                            skip_group_check=True)
                    return r
                w = [("s_wrdy", wo_rdy[(og, half)]), ("s_dve", y_evac.get(tb))]
                pe = em("tensor", y_mms, wait=w, inc="s_pe")
                if half == 1:
                    y_evac[tb] = em(
                        "vector", lambda e, tb=tb, og=og: e.tensor_tensor(
                            out=resid[:, tb * D + og * 512:
                                      tb * D + og * 512 + 512],
                            in0=resid[:, tb * D + og * 512:
                                      tb * D + og * 512 + 512],
                            in1=PB[tb][:], op=ALU.add),
                        wait=[("s_pe", pe), ("s_xrdy", xown_rdy)], inc="s_dve",
                        reads=("resid",), writes=("resid",))
            wo_buf_pe[og * 2 + half] = cnt["s_pe"]
            nxt = og * 2 + half + 2
            if nxt < 8:
                stream_wo(nxt // 2, nxt % 2)
    wo_done_dve = cnt["s_dve"]

    moe_rdy = {}
    em("gpsimd", lambda e: e.dma_start(
        out=wg_bf[:].rearrange("p (k f) -> p k f", f=FI),
        in_=wgu_d[0].rearrange("(k p) f -> p k f", p=P)[:, :, 0:FI]), inc="s_wq")
    moe_rdy[("g", 0)] = q_barrier("gpsimd", "s_wq", "s_wrdy")
    em("gpsimd", lambda e: e.dma_start(
        out=wu_bf[:].rearrange("p (k f) -> p k f", f=FI),
        in_=wgu_d[0].rearrange("(k p) f -> p k f", p=P)[:, :, FI:2 * FI]), inc="s_wq")
    moe_rdy[("u", 0)] = q_barrier("gpsimd", "s_wq", "s_wrdy")

    resid_tap_out = None
    if "resid_mid" in tap_d:
        td = tap_d["resid_mid"]
        em("gpsimd", lambda e, td=td: e.dma_start(
            out=td[:].rearrange("tb p d -> p tb d"),
            in_=resid[:].rearrange("p (tb d) -> p tb d", d=D)),
            wait=[("s_dve", wo_done_dve)], inc="s_tap")
        resid_tap_out = cnt["s_tap"]

    new_block()
    pH_ = [PB[0], PB[1]]
    pR = PB[2]
    pD_ = [PB[4], PB[5], PB[6], PB[7]]

    em("gpsimd", lambda e: e.dma_start(
        out=gateb[:].rearrange("p (k o) -> p k o", o=32),
        in_=gate_d[:].rearrange("(k p) o -> p k o", p=P)), inc="s_wq")
    gate_rdy = q_barrier("gpsimd", "s_wq", "s_wrdy")

    rt_done = []
    for tb in range(NT):
        def r_mms(eng, tb=tb):
            r = None
            for k in range(KD):
                r = eng.matmul(out=pR[:, 0:32],
                               lhsT=xn_own[:, k * TOWN + tb * P:
                                           k * TOWN + (tb + 1) * P],
                               rhs=gateb[:, k * 32:(k + 1) * 32],
                               start=(k == 0), stop=(k == KD - 1))
            return r
        w = [("s_wrdy", gate_rdy)]
        if tb > 0:
            w.append(("s_dve", rt_done[-1]))
        pe = em("tensor", r_mms, wait=w, inc="s_pe")
        vmx = em("vector", lambda e: e.reduce_max(
            out=tk_s[:], in_=pR[:, 0:32], axis=AX.X, negate=True),
            wait=[("s_pe", pe)], inc="s_dve", writes=("tk_s",))
        aex = em("scalar", lambda e: e.activation(
            out=tk_p[:], in_=pR[:, 0:32], func=AF.Exp, bias=tk_s[:],
            accum_out=tk_mx[:, 0:1]),
            wait=[("s_dve", vmx)], inc="s_act",
            reads=("tk_s",), writes=("tk_p", "tk_mx"))
        em("vector", lambda e: e.reciprocal(out=tk_s[:], in_=tk_mx[:, 0:1]),
           wait=[("s_act", aex)], inc="s_dve",
           reads=("tk_mx",), writes=("tk_s",))
        em("vector", lambda e: e.tensor_scalar(
            out=tk_p[:], in0=tk_p[:], scalar1=tk_s[:], scalar2=None, op0=ALU.mult),
            inc="s_dve", reads=("tk_p", "tk_s"), writes=("tk_p",))
        em("vector", lambda e: e.tensor_copy(out=tk_w[:], in_=tk_p[:]),
           inc="s_dve", reads=("tk_p",), writes=("tk_w",))
        for it in range(4):
            em("vector", lambda e, it=it: e.reduce_max(
                out=tk_mx[:, it:it + 1], in_=tk_w[:], axis=AX.X),
                inc="s_dve", reads=("tk_w",), writes=("tk_mx",))
            if it < 3:
                em("vector", lambda e, it=it: e.tensor_scalar(
                    out=tk_keep[:], in0=tk_w[:], scalar1=tk_mx[:, it:it + 1],
                    scalar2=None, op0=ALU.is_ge),
                    inc="s_dve", reads=("tk_w", "tk_mx"), writes=("tk_keep",))
                em("vector", lambda e: e.scalar_tensor_tensor(
                    out=tk_w[:], in0=tk_keep[:], scalar=-2.0, in1=tk_w[:],
                    op0=ALU.mult, op1=ALU.add),
                    inc="s_dve", reads=("tk_keep", "tk_w"), writes=("tk_w",))
        em("vector", lambda e: e.reduce_sum(out=tk_s[:], in_=tk_mx[:], axis=AX.X),
           inc="s_dve", reads=("tk_mx",), writes=("tk_s",))
        em("vector", lambda e: e.reciprocal(out=tk_s[:], in_=tk_s[:]),
           inc="s_dve", reads=("tk_s",), writes=("tk_s",))
        em("vector", lambda e: e.tensor_scalar(
            out=tk_keep[:], in0=tk_p[:], scalar1=tk_mx[:, 3:4], scalar2=None,
            op0=ALU.is_ge), inc="s_dve",
            reads=("tk_p", "tk_mx"), writes=("tk_keep",))
        v = em("vector", lambda e, tb=tb: e.scalar_tensor_tensor(
            out=wrt[:, tb * 32:(tb + 1) * 32], in0=tk_p[:], scalar=tk_s[:],
            in1=tk_keep[:], op0=ALU.mult, op1=ALU.mult),
            inc="s_dve", reads=("tk_p", "tk_s", "tk_keep"), writes=("wrt",))
        rt_done.append(v)
    router_done = cnt["s_dve"]
    if "wrt" in tap_d:
        td = tap_d["wrt"]
        em("gpsimd", lambda e, td=td: e.dma_start(out=td[:], in_=wrt[:]),
           wait=[("s_dve", router_done)], inc="s_tap")

    pe_after = {}

    def moe_piece(kind, e):
        if kind == "g":
            dst = wg_bf[:].rearrange("p (k f) -> p k f", f=FI)
            src = wgu_d[e].rearrange("(k p) f -> p k f", p=P)[:, :, 0:FI]
            gw = pe_after.get(("G", e - 1))
        elif kind == "u":
            dst = wu_bf[:].rearrange("p (k f) -> p k f", f=FI)
            src = wgu_d[e].rearrange("(k p) f -> p k f", p=P)[:, :, FI:2 * FI]
            gw = pe_after.get(("U", e - 1))
        else:
            dst = wd_bf[:].rearrange("p (k f) -> p k f", f=D)
            src = wdn_d[e].rearrange("(k p) f -> p k f", p=P)
            gw = pe_after.get(("D", e - 1))
        em("gpsimd", lambda e_, dst=dst, src=src: e_.dma_start(out=dst, in_=src),
           wait=[("s_pe", gw)], inc="s_wq")
        moe_rdy[(kind, e)] = q_barrier("gpsimd", "s_wq", "s_wrdy")

    hpar = [0]
    pH_guard = [None, None]
    silu_done = {}
    pass1_done = {}
    last_D_evac = {}

    if cnt["s_tap"]:
        em("gpsimd", lambda e, v=cnt["s_tap"], s=SEM["s_tap"]: e.wait_ge(s, v))
    moe_piece("d", 0)
    for e in range(NE):
        for j in range(KF):
            for tg in range(2):
                p_ = hpar[0]; hpar[0] ^= 1
                pp = pH_[p_]

                def g_mms(eng, j=j, tg=tg, pp=pp):
                    r = None
                    for k in range(KD):
                        r = eng.matmul(
                            out=pp[:],
                            lhsT=wg_bf[:, k * FI + j * P: k * FI + (j + 1) * P],
                            rhs=xn_own[:, k * TOWN + tg * 512:
                                       k * TOWN + tg * 512 + 512],
                            start=(k == 0), stop=(k == KD - 1))
                    return r
                gw = pH_guard[p_]
                pe = em("tensor", g_mms,
                        wait=[("s_wrdy", moe_rdy[("g", e)])] + ([gw] if gw else []),
                        inc="s_pe")
                p1 = pass1_done.get((j, tg))
                sv = em("scalar", lambda e_, j=j, tg=tg, pp=pp: e_.activation(
                    out=silu_g[:, j * TOWN + tg * 512: j * TOWN + tg * 512 + 512],
                    in_=pp[:], func=AF.Silu),
                    wait=[("s_pe", pe)] + ([p1] if p1 else []),
                    inc="s_act", writes=("silu_g",))
                silu_done[(j, tg)] = sv
                pH_guard[p_] = ("s_act", sv)
        pe_after[("G", e)] = cnt["s_pe"]
        if e + 1 < NE:
            moe_piece("g", e + 1)
        for j in range(KF):
            for tg in range(2):
                p_ = hpar[0]; hpar[0] ^= 1
                pp = pH_[p_]

                def u_mms(eng, j=j, tg=tg, pp=pp):
                    r = None
                    for k in range(KD):
                        r = eng.matmul(
                            out=pp[:],
                            lhsT=wu_bf[:, k * FI + j * P: k * FI + (j + 1) * P],
                            rhs=xn_own[:, k * TOWN + tg * 512:
                                       k * TOWN + tg * 512 + 512],
                            start=(k == 0), stop=(k == KD - 1))
                    return r
                gw = pH_guard[p_]
                pe = em("tensor", u_mms,
                        wait=[("s_wrdy", moe_rdy[("u", e)])] + ([gw] if gw else []),
                        inc="s_pe")
                w1 = [("s_pe", pe), ("s_act", silu_done[(j, tg)]),
                      ("s_pe", pe_after.get(("D", e - 1)))]
                v = em("vector", lambda e_, j=j, tg=tg, pp=pp: e_.tensor_tensor(
                    out=gated[:, j * TOWN + tg * 512: j * TOWN + tg * 512 + 512],
                    in0=pp[:],
                    in1=silu_g[:, j * TOWN + tg * 512: j * TOWN + tg * 512 + 512],
                    op=ALU.mult),
                    wait=w1, inc="s_dve", reads=("silu_g",), writes=("gated",))
                pass1_done[(j, tg)] = ("s_dve", v)
                pH_guard[p_] = ("s_dve", v)
        pe_after[("U", e)] = cnt["s_pe"]
        gated_done = cnt["s_dve"]
        if e + 1 < NE:
            moe_piece("u", e + 1)
        for tb in range(NT):
            for m in range(4):
                def d_mms(eng, tb=tb, m=m):
                    r = None
                    for k in range(KF):
                        r = eng.matmul(
                            out=pD_[m][:],
                            lhsT=gated[:, k * TOWN + tb * P:
                                       k * TOWN + (tb + 1) * P],
                            rhs=wd_bf[:, k * D + m * 512: k * D + (m + 1) * 512],
                            start=(k == 0), stop=(k == KF - 1))
                    return r
                w = [("s_wrdy", moe_rdy[("d", e)]), ("s_dve", gated_done),
                     ("s_dve", router_done), ("s_dve", last_D_evac.get(m))]
                pe = em("tensor", d_mms, wait=w, inc="s_pe")
                v = em("vector", lambda e_, tb=tb, m=m, e=e: e_.scalar_tensor_tensor(
                    out=resid[:, tb * D + m * 512: tb * D + (m + 1) * 512],
                    in0=pD_[m][:],
                    scalar=wrt[:, tb * 32 + e: tb * 32 + e + 1],
                    in1=resid[:, tb * D + m * 512: tb * D + (m + 1) * 512],
                    op0=ALU.mult, op1=ALU.add),
                    wait=[("s_pe", pe)] + (
                        [("s_tap", cnt["s_tap"] or None)] if e == 0 else []),
                    inc="s_dve",
                    reads=("resid", "wrt"), writes=("resid",))
                last_D_evac[m] = v
        pe_after[("D", e)] = cnt["s_pe"]
        if e + 1 < NE:
            moe_piece("d", e + 1)
    moe_done = cnt["s_dve"]

    new_block()
    em("sync", lambda e: e.dma_start(
        out=out_d[:].rearrange("tb p d -> p tb d"),
        in_=resid[:].rearrange("p (tb d) -> p tb d", d=D)),
        wait=[("s_dve", moe_done)], inc="s_out")
    em("sync", lambda e, v=cnt["s_out"], s=SEM["s_out"]: e.wait_ge(s, v))
    em("gpsimd", lambda e, v=cnt["s_tap"], s=SEM["s_tap"]: e.wait_ge(s, v))

    with ES:
        replay_all()
    return nc


def _host_tables():
    inv = 1.0 / 10000.0 ** (np.arange(0, HD, 2, dtype=np.float64) / HD)
    invr = np.repeat(inv, 2)
    posk = np.arange(TSEQ, dtype=np.float64)

    def mk(pos):
        fr = invr[:, None] * pos[None, :]
        return np.stack([np.cos(fr), np.sin(fr)]).astype(ml_dtypes.bfloat16)

    ropek = mk(posk)
    ropeq, masks = {}, {}
    tri = (np.arange(P)[:, None] <= np.arange(P)[None, :]).astype(np.float32)
    for par in (0, 1):
        pos = np.concatenate(
            [np.arange((2 * b + par) * P, (2 * b + par + 1) * P)
             for b in range(NT)]).astype(np.float64)
        ropeq[par] = mk(pos)
        blocks = [2 * b + par for b in range(NT)]
        m = np.zeros((16, P, 512), dtype=np.float32)
        for gi, grp in enumerate((blocks[:4], blocks[4:])):
            for jj in range(8):
                j = jj if gi == 0 else 8 + jj
                for bi, B in enumerate(grp):
                    if j < B:
                        m[j, :, bi * P:(bi + 1) * P] = 1.0
                    elif j == B:
                        m[j, :, bi * P:(bi + 1) * P] = tri
        masks[par] = m.astype(ml_dtypes.bfloat16)

    perm = np.zeros((P, P), dtype=np.float32)
    for i in range(HD // 2):
        perm[2 * i + 1, 2 * i] = -1.0
        perm[2 * i, 2 * i + 1] = 1.0
    constb = np.stack([np.eye(P, dtype=np.float32), perm,
                       np.ones((P, P), dtype=np.float32)]).astype(ml_dtypes.bfloat16)
    onesrf = np.ones((1, P), dtype=np.float32)
    fconst = np.tile(np.array([[LN_EPS, RMS_EPS, -0.5 * np.log(HD), 0.0]],
                              dtype=np.float32), (P, 1))
    return ropeq, ropek, constb, onesrf, fconst, masks


def make_in_maps(L0_attn_x, L0_attn_W_qkv, L0_attn_W_o, L0_ffn_gate,
                 L0_ffn_gate_up_proj, L0_ffn_down_proj):
    x = np.ascontiguousarray(np.asarray(L0_attn_x, dtype=np.float32))
    wqkvT = np.ascontiguousarray(np.asarray(L0_attn_W_qkv, dtype=np.float32).T)
    woT = np.ascontiguousarray(np.asarray(L0_attn_W_o, dtype=np.float32).T)
    gate = np.ascontiguousarray(np.asarray(L0_ffn_gate, dtype=np.float32))
    wgu = np.ascontiguousarray(np.asarray(L0_ffn_gate_up_proj, dtype=np.float32))
    wdn = np.ascontiguousarray(np.asarray(L0_ffn_down_proj, dtype=np.float32))
    ropeq, ropek, constb, onesrf, fconst, masks = _host_tables()
    in_maps = []
    for c in range(8):
        s, par = c // 2, c % 2
        xs = x[s]
        x_seq = np.ascontiguousarray(xs.reshape(NTS, P, D))
        x_own = np.ascontiguousarray(
            np.stack([xs[(2 * b + par) * P:(2 * b + par + 1) * P]
                      for b in range(NT)]))
        in_maps.append(dict(
            x_own=x_own, x_seq=x_seq, wqkvT=wqkvT, woT=woT, gate=gate,
            wgu=wgu, wdn=wdn, ropeq=ropeq[par], ropek=ropek,
            maskm=masks[par], constb=constb, onesrf=onesrf, fconst=fconst))
    return in_maps


def kernel(L0_attn_x, L0_attn_W_qkv, L0_attn_W_o, L0_ffn_gate,
           L0_ffn_gate_up_proj, L0_ffn_down_proj):
    in_maps = make_in_maps(L0_attn_x, L0_attn_W_qkv, L0_attn_W_o, L0_ffn_gate,
                           L0_ffn_gate_up_proj, L0_ffn_down_proj)
    nc = build_nc()
    res = run_bass_kernel_spmd(nc, in_maps, core_ids=list(range(8)))
    out = np.empty((4, TSEQ, D), dtype=np.float32)
    for c in range(8):
        s, par = c // 2, c % 2
        r = np.asarray(res.results[c]["out"])
        for b in range(NT):
            out[s, (2 * b + par) * P:(2 * b + par + 1) * P, :] = r[b]
    return out


# version 04
# speedup vs baseline: 1.3244x; 1.0157x over previous
import numpy as np
import ml_dtypes
import concourse.bass as bass
import concourse.mybir as mybir
from concourse.bass_utils import run_bass_kernel_spmd

F32 = mybir.dt.float32
BF16 = mybir.dt.bfloat16
AF = mybir.ActivationFunctionType
ALU = mybir.AluOpType
AX = mybir.AxisListType

P = 128
D = 2048
H = 16
HD = 128
TOWN = 1024
TSEQ = 2048
NE = 32
FI = 768
KD = D // P
KF = FI // P
NT = TOWN // P
NTS = TSEQ // P
LN_EPS = 1e-5
RMS_EPS = 1.1920929e-07

SB_BASE = 16512
SB_TOP = 229344


def build_nc(taps=()):
    nc = bass.Bass("TRN2", target_bir_lowering=False, debug=False)

    x_own_d = nc.dram_tensor("x_own", [NT, P, D], F32, kind="ExternalInput")
    x_seq_d = nc.dram_tensor("x_seq", [NTS, P, D], F32, kind="ExternalInput")
    wqkvT_d = nc.dram_tensor("wqkvT", [D, 3 * D], F32, kind="ExternalInput")
    woT_d = nc.dram_tensor("woT", [D, D], F32, kind="ExternalInput")
    gate_d = nc.dram_tensor("gate", [D, 32], F32, kind="ExternalInput")
    wgu_d = nc.dram_tensor("wgu", [NE, D, 2 * FI], F32, kind="ExternalInput")
    wdn_d = nc.dram_tensor("wdn", [NE, FI, D], F32, kind="ExternalInput")
    ropeq_d = nc.dram_tensor("ropeq", [2, P, TOWN], BF16, kind="ExternalInput")
    ropek_d = nc.dram_tensor("ropek", [2, P, TSEQ], BF16, kind="ExternalInput")
    mask_d = nc.dram_tensor("maskm", [16, P, 512], BF16, kind="ExternalInput")
    constb_d = nc.dram_tensor("constb", [3, P, P], BF16, kind="ExternalInput")
    onesrf_d = nc.dram_tensor("onesrf", [1, P], F32, kind="ExternalInput")
    fconst_d = nc.dram_tensor("fconst", [P, 4], F32, kind="ExternalInput")
    vscr_d = nc.dram_tensor("vscr", [NTS, P, D], BF16)
    qscr_d = nc.dram_tensor("qscr", [H, P, TOWN], BF16)
    kscr_d = nc.dram_tensor("kscr", [H, P, TSEQ], BF16)
    out_d = nc.dram_tensor("out", [NT, P, D], F32, kind="ExternalOutput")
    tap_d = {}
    for t in taps:
        shp = {"xn_own": [P, KD * TOWN], "xn_seq": [P, KD * TSEQ],
               "attn_out": [P, H * TOWN], "resid_mid": [NT, P, D],
               "wrt": [P, NT * 32], "qfm0": [P, TOWN], "kfm0": [P, TSEQ]}[t]
        tap_d[t] = nc.dram_tensor("tap_" + t, shp, F32, kind="ExternalOutput")

    def at(name, shape, dtype, off):
        esz = {BF16: 2, F32: 4}[dtype]
        nbytes = int(np.prod(shape[1:])) * esz
        assert SB_BASE + off + nbytes <= SB_TOP, (name, off, nbytes)
        return nc.alloc_sbuf_tensor_at(name, shape, dtype, offset=SB_BASE + off)

    xn_own = at("xn_own", [P, KD * TOWN], BF16, 0)
    onescol = at("onescol", [P, 1], BF16, 32768)
    onesrf = at("onesrf", [1, P], F32, 32800)
    onesb2d = at("onesb2d", [P, P], BF16, 210560)
    permb = at("permb", [P, P], BF16, 33312)
    identb = at("identb", [P, P], BF16, 33568)
    lnrow = at("lnrow", [P, 4], F32, 33824)
    fconst = at("fconst", [P, 4], F32, 33856)
    R2 = 33920
    xn_seq = at("xn_seq", [P, KD * TSEQ], BF16, R2)
    resid = at("resid", [P, NT * D], F32, R2)
    R3 = R2 + 65536
    ropeq = at("ropeq", [P, 2 * TOWN], BF16, R3)
    ropek = at("ropek", [P, 2 * TSEQ], BF16, R3 + 4096)
    maskm = at("maskm", [P, 16 * 512], BF16, R3 + 12288)
    woTb = [at("woTa", [P, 8 * 512], BF16, R3),
            at("woTbb", [P, 8 * 512], BF16, R3 + 8192)]
    silu_g = at("silu_g", [P, KF * TOWN], BF16, R3)
    gated = at("gated", [P, KF * TOWN], BF16, R3 + 12288)
    gateb = at("gateb", [P, KD * 32], BF16, R3 + 24576)
    wrt = at("wrt", [P, NT * 32], F32, R3 + 25600)
    tk_p = at("tk_p", [P, 32], F32, R3 + 26624)
    tk_w = at("tk_w", [P, 32], F32, R3 + 26752)
    tk_keep = at("tk_keep", [P, 32], F32, R3 + 26880)
    tk_mx = at("tk_mx", [P, 4], F32, R3 + 27008)
    tk_s = at("tk_s", [P, 1], F32, R3 + 27136)
    R4 = R3 + 28672
    vw_buf = at("vw_buf", [P, 2 * KD * 512], BF16, R4)
    vstage = at("vstage", [P, 2 * 512], BF16, R4 + 32768)
    wqh = at("wqh", [P, D], BF16, R4)
    wkh = at("wkh", [P, D], BF16, R4 + 4096)
    q_fm = at("q_fm", [P, 2 * TOWN], BF16, R4 + 8192)
    k_fm = at("k_fm", [P, 2 * TSEQ], BF16, R4 + 12288)
    q_sb = at("q_sb", [P, 4 * 512], BF16, R4 + 20480)
    qa = at("qa", [P, 4 * 512], F32, R4 + 24576)
    qb = at("qb", [P, 4 * 512], F32, R4 + 32768)
    rbln = at("rbln", [P, 2 * 512], F32, R4 + 40960)
    rstd_sb = at("rstd_sb", [P, 2 * 512], BF16, R4 + 45056)
    qr_fm = at("qr_fm", [P, 2 * TOWN], BF16, R4)
    kr_fm = at("kr_fm", [P, 2 * TSEQ], BF16, R4 + 4096)
    v_tm = at("v_tm", [P, 2 * D], BF16, R4 + 12288)
    attnT = at("attnT", [P, 4 * 512], BF16, R4 + 20480)
    rb_sb = at("rb_sb", [P, 512], BF16, R4 + 24576 + 4096)
    wg_bf = at("wg_bf", [P, KD * FI], BF16, R4)
    wu_bf = at("wu_bf", [P, KD * FI], BF16, R4 + 24576)
    R5 = R4 + 49152
    xstage = at("xstage", [P, 2 * D], F32, R5)
    lnsq = at("lnsq", [P, D], F32, R5 + 16384)
    xn_tmp = at("xn_tmp", [P, D], BF16, R5 + 24576)
    attn_out = at("attn_out", [P, H * TOWN], BF16, R5)
    wd_bf = at("wd_bf", [P, KF * D], BF16, R5)

    from contextlib import ExitStack
    ES = ExitStack()
    PB = [ES.enter_context(nc.psum_tensor(f"PB{i}", [P, 512], F32))
          for i in range(8)]

    sem = lambda n: ES.enter_context(nc.semaphore(n))
    SEM = {n: sem(n) for n in
           ("s_wq", "s_wrdy", "s_xq", "s_xrdy", "s_aq", "s_ardy",
            "s_pe", "s_act", "s_dve", "s_out", "s_vs0", "s_vs1", "s_tap")}
    DMA_SEMS = {"s_wq", "s_xq", "s_aq", "s_out", "s_vs0", "s_vs1", "s_tap"}
    cnt = {k: 0 for k in SEM}

    blocks, cur, shadow = [], None, None

    def new_block():
        nonlocal cur, shadow
        cur = {k: [] for k in ("sync", "tensor", "vector", "scalar", "gpsimd")}
        shadow = {k: (set(), set()) for k in cur}
        blocks.append(cur)

    def em(eng, fn, wait=None, inc=None, reads=(), writes=()):
        rs, ws = shadow[eng]
        if eng in ("vector", "scalar", "gpsimd"):
            if (ws & set(reads)) or (set(writes) & (ws | rs)):
                cur[eng].append((lambda e: e.drain(), [], None))
                rs.clear(); ws.clear()
        rs |= set(reads); ws |= set(writes)
        n = None
        if inc is not None:
            step = 16 if inc in DMA_SEMS else 1
            cnt[inc] += step
            n = (SEM[inc], step)
        wl = [(SEM[s], v) for s, v in (wait or [])
              if s is not None and v is not None]
        cur[eng].append((fn, wl, n))
        return cnt[inc] if inc else None

    def q_barrier(eng, q_sem, rdy_sem):
        em(eng, lambda e, v=cnt[q_sem], s=SEM[q_sem]: e.wait_ge(s, v))
        return em(eng, lambda e: e.nop(), inc=rdy_sem)

    def replay_all():
        def mk(eng_ops):
            def f(eng):
                for fn, waits, inc in eng_ops:
                    for s, v in waits:
                        eng.wait_ge(s, v)
                    r = fn(eng)
                    if inc is not None:
                        s, step = inc
                        if r is None:
                            r = eng.nop()
                        r.then_inc(s, step)
            return f
        for ops in blocks:
            with nc.Block() as block:
                block.sync(mk(ops["sync"]))
                block.tensor(mk(ops["tensor"]))
                block.vector(mk(ops["vector"]))
                block.scalar(mk(ops["scalar"]))
                block.gpsimd(mk(ops["gpsimd"]))

    def rearr_w(dram2d, c0, c1):
        return dram2d[:].rearrange("(k p) o -> p k o", p=P)[:, :, c0:c1]

    new_block()
    pT = [PB[0][:].bitcast(BF16), PB[1][:].bitcast(BF16)]

    em("scalar", lambda e: e.dma_start(out=identb[:], in_=constb_d[0]), inc="s_aq")
    em("scalar", lambda e: e.dma_start(out=permb[:], in_=constb_d[1]), inc="s_aq")
    em("vector", lambda e: e.memset(onescol[:], 1.0), inc="s_dve", writes=("onescol",))
    em("scalar", lambda e: e.dma_start(out=onesrf[:], in_=onesrf_d[:]), inc="s_aq")
    em("scalar", lambda e: e.dma_start(out=onesb2d[:], in_=constb_d[2]), inc="s_aq")
    em("scalar", lambda e: e.dma_start(out=fconst[:], in_=fconst_d[:]), inc="s_aq")
    em("scalar", lambda e: e.dma_start(out=ropeq[:, 0:TOWN], in_=ropeq_d[0]), inc="s_aq")
    em("scalar", lambda e: e.dma_start(out=ropeq[:, TOWN:2 * TOWN], in_=ropeq_d[1]), inc="s_aq")
    em("scalar", lambda e: e.dma_start(out=ropek[:, 0:TSEQ], in_=ropek_d[0]), inc="s_aq")
    em("scalar", lambda e: e.dma_start(out=ropek[:, TSEQ:2 * TSEQ], in_=ropek_d[1]), inc="s_aq")
    em("scalar", lambda e: e.dma_start(
        out=maskm[:].rearrange("p (j t) -> p j t", t=512),
        in_=mask_d[:].rearrange("j p t -> p j t")), inc="s_aq")
    consts_rdy = q_barrier("scalar", "s_aq", "s_ardy")
    onescol_dve = cnt["s_dve"]

    vw_rdy, vw_pe = {}, {}

    def stream_vw(og):
        par = og & 1
        em("gpsimd", lambda e, og=og, par=par: e.dma_start(
            out=vw_buf[:, par * KD * 512:(par + 1) * KD * 512].rearrange(
                "p (k o) -> p k o", o=512),
            in_=rearr_w(wqkvT_d, 2 * D + og * 512, 2 * D + (og + 1) * 512)),
            wait=[("s_pe", vw_pe.get(og - 2))], inc="s_wq",
            writes=(f"vw{par}",))
        vw_rdy[og] = q_barrier("gpsimd", "s_wq", "s_wrdy")

    stream_vw(0)
    stream_vw(1)

    x_rdy, ln_done, chunk_act = [], [], []
    tp_guard = [None, None]
    tp_last = None
    a1 = None
    for i in range(NTS + NT):
        sl = i & 1
        w = [("s_dve", ln_done[i - 2])] if i >= 2 else []
        em("sync", lambda e, i=i, sl=sl: e.dma_start(
            out=xstage[:, sl * D:(sl + 1) * D],
            in_=(x_seq_d[i] if i < NTS else x_own_d[i - NTS])),
            wait=w, inc="s_xq")
        x_rdy.append(q_barrier("sync", "s_xq", "s_xrdy"))

        em("vector", lambda e, sl=sl: e.reduce_sum(
            out=lnrow[:, 0:1], in_=xstage[:, sl * D:(sl + 1) * D], axis=AX.X),
            wait=[("s_xrdy", x_rdy[i]), ("s_act", a1)],
            inc="s_dve", reads=(f"xst{sl}",), writes=("lnrow0",))
        v2 = em("vector", lambda e: e.tensor_scalar(
            out=lnrow[:, 1:2], in0=lnrow[:, 0:1], scalar1=-1.0 / D, scalar2=None,
            op0=ALU.mult), inc="s_dve", reads=("lnrow0",), writes=("lnrow1",))
        a1 = em("scalar", lambda e, sl=sl: e.activation(
            out=lnsq[:], in_=xstage[:, sl * D:(sl + 1) * D], func=AF.Square,
            bias=lnrow[:, 1:2], accum_out=lnrow[:, 2:3]),
            wait=[("s_dve", v2)], inc="s_act",
            reads=(f"xst{sl}", "lnrow1"), writes=("lnsq", "lnrow2"))
        em("scalar", lambda e: e.activation(
            out=lnrow[:, 3:4], in_=lnrow[:, 2:3], func=AF.Ln,
            bias=fconst[:, 0:1], scale=1.0 / D), inc="s_act",
            reads=("lnrow2",), writes=("lnrow3",))
        a3 = em("scalar", lambda e: e.activation(
            out=lnrow[:, 3:4], in_=lnrow[:, 3:4], func=AF.Exp,
            scale=-0.5), inc="s_act", reads=("lnrow3",), writes=("lnrow3",))
        v3 = em("vector", lambda e, sl=sl: e.tensor_scalar(
            out=xn_tmp[:], in0=xstage[:, sl * D:(sl + 1) * D],
            scalar1=lnrow[:, 1:2], scalar2=lnrow[:, 3:4],
            op0=ALU.add, op1=ALU.mult),
            wait=[("s_act", a3), ("s_pe", tp_last)], inc="s_dve",
            reads=(f"xst{sl}", "lnrow1", "lnrow3"), writes=("xn_tmp",))
        ln_done.append(v3)
        dst, tcols, tb = ((xn_seq, TSEQ, i) if i < NTS else (xn_own, TOWN, i - NTS))
        for k in range(KD):
            par = k & 1
            w = [("s_dve", v3), ("s_act", tp_guard[par])]
            if i == 0 and k == 0:
                w.append(("s_ardy", consts_rdy))
            pe = em("tensor", lambda e, k=k, par=par: e.transpose(
                out=pT[par][:, 0:P], in_=xn_tmp[:, k * P:(k + 1) * P],
                identity=identb[:]), wait=w, inc="s_pe")
            tp_guard[par] = em(
                "scalar",
                lambda e, k=k, tb=tb, par=par, dst=dst, tcols=tcols:
                e.copy(out=dst[:, k * tcols + tb * P: k * tcols + (tb + 1) * P],
                       in_=pT[par][:, 0:P]),
                wait=[("s_pe", pe)], inc="s_act", writes=("xn",))
        tp_last = cnt["s_pe"]
        chunk_act.append(cnt["s_act"])
    xn_all_act = cnt["s_act"]

    vst_dma = {}
    vst_guard = [None, None]
    for og in range(4):
        par = og & 1
        if og >= 2:
            stream_vw(og)
        for tc in range(NTS):
            idx0 = og * NTS + tc
            ppar = idx0 & 1
            pp = PB[2 + ppar]

            def v_mms(eng, tc=tc, par=par, pp=pp):
                r = None
                for k in range(KD):
                    r = eng.matmul(
                        out=pp[:],
                        lhsT=xn_seq[:, k * TSEQ + tc * P: k * TSEQ + (tc + 1) * P],
                        rhs=vw_buf[:, par * KD * 512 + k * 512:
                                   par * KD * 512 + (k + 1) * 512],
                        start=(k == 0), stop=(k == KD - 1))
                return r
            w = [("s_wrdy", vw_rdy[og]), ("s_act", chunk_act[tc]),
                 ("s_act", vst_guard[ppar])]
            pe = em("tensor", v_mms, wait=w, inc="s_pe")
            idx = og * NTS + tc
            slot = idx & 1
            wv = [("s_pe", pe)]
            if idx >= 2:
                wv.append((f"s_vs{slot}", 16 * (idx // 2)))
            vst_guard[ppar] = em(
                "scalar", lambda e, ppar=ppar, pp=pp:
                e.copy(out=vstage[:, ppar * 512:(ppar + 1) * 512], in_=pp[:]),
                wait=wv, inc="s_act", writes=(f"vstg{ppar}",))
            em("scalar", lambda e, tc=tc, og=og, ppar=ppar: e.dma_start(
                out=vscr_d[tc, :, og * 512:(og + 1) * 512],
                in_=vstage[:, ppar * 512:(ppar + 1) * 512]),
                inc=f"s_vs{slot}", reads=(f"vstg{ppar}",))
        vw_pe[og] = cnt["s_pe"]
    em("scalar", lambda e, v=cnt["s_vs0"], s=SEM["s_vs0"]: e.wait_ge(s, v))
    em("scalar", lambda e, v=cnt["s_vs1"], s=SEM["s_vs1"]: e.wait_ge(s, v))
    vscr_all = em("scalar", lambda e: e.nop(), inc="s_ardy")

    if "xn_own" in tap_d:
        td = tap_d["xn_own"]
        em("gpsimd", lambda e, td=td: e.dma_start(out=td[:], in_=xn_own[:]),
           wait=[("s_act", xn_all_act)], inc="s_tap")
    if "xn_seq" in tap_d:
        td = tap_d["xn_seq"]
        em("gpsimd", lambda e, td=td: e.dma_start(out=td[:], in_=xn_seq[:]),
           wait=[("s_act", xn_all_act)], inc="s_tap")

    new_block()
    pQK = [PB[0], PB[1], PB[2], PB[3]]
    pSSB = [PB[4], PB[5]]

    wq_rdy, wk_rdy = {}, {}
    spill_rdy = {}

    tiles = []
    for h in range(H):
        for which, ntg in (("q", 2), ("k", 4)):
            for tg in range(ntg):
                tiles.append((h, which, tg))
    NTI = len(tiles)

    st = [dict() for _ in range(NTI)]

    def tinfo(i):
        h, which, tg = tiles[i]
        wbuf = wqh if which == "q" else wkh
        rdy = (wq_rdy if which == "q" else wk_rdy)[h]
        dst = q_fm if which == "q" else k_fm
        dcols = TOWN if which == "q" else TSEQ
        rope = ropeq if which == "q" else ropek
        xsrc, xcols = (xn_own, TOWN) if which == "q" else (xn_seq, TSEQ)
        bias2 = fconst[:, 2:3] if which == "q" else fconst[:, 3:4]
        return h, which, tg, wbuf, rdy, dst, dcols, rope, xsrc, xcols, bias2

    def stream_qkw(h):
        wq = [("s_pe", st[6 * (h - 1) + 1].get("pe1"))] if h > 0 else None
        em("gpsimd", lambda e, h=h: e.dma_start(
            out=wqh[:].rearrange("p (k o) -> p k o", o=HD),
            in_=rearr_w(wqkvT_d, h * HD, (h + 1) * HD)),
            wait=wq, inc="s_wq", writes=("wqh",))
        wq_rdy[h] = q_barrier("gpsimd", "s_wq", "s_wrdy")
        wk = [("s_pe", st[6 * (h - 1) + 5].get("pe1"))] if h > 0 else None
        em("gpsimd", lambda e, h=h: e.dma_start(
            out=wkh[:].rearrange("p (k o) -> p k o", o=HD),
            in_=rearr_w(wqkvT_d, D + h * HD, D + (h + 1) * HD)),
            wait=wk, inc="s_wq", writes=("wkh",))
        wk_rdy[h] = q_barrier("gpsimd", "s_wq", "s_wrdy")

    stream_qkw(0)

    def emit_mmg(i):
        h, which, tg, wbuf, rdy, dst, dcols, rope, xsrc, xcols, bias2 = tinfo(i)
        cp = i % 4
        pq = pQK[cp]

        def qk_mms(eng, wbuf=wbuf, xsrc=xsrc, xcols=xcols, tg=tg, pq=pq):
            r = None
            for k in range(KD):
                r = eng.matmul(
                    out=pq[:], lhsT=wbuf[:, k * HD:(k + 1) * HD],
                    rhs=xsrc[:, k * xcols + tg * 512: k * xcols + tg * 512 + 512],
                    start=(k == 0), stop=(k == KD - 1))
            return r
        w = [("s_wrdy", rdy), ("s_act", xn_all_act)]
        if i >= 4:
            w.append(("s_dve", st[i - 4].get("vb")))
        st[i]["pe1"] = em("tensor", qk_mms, wait=w, inc="s_pe")
        qsb = q_sb[:, cp * 512:(cp + 1) * 512]
        wv = [("s_pe", st[i]["pe1"])]
        if i >= 4:
            wv += [("s_pe", st[i - 4].get("pe3")), ("s_dve", st[i - 4].get("vn"))]
        st[i]["ev"] = em("scalar", lambda e, qsb=qsb, pq=pq: e.copy(out=qsb, in_=pq[:]),
                         wait=wv, inc="s_act", writes=(f"q_sb{cp}",))
        qav = qa[:, cp * 512:(cp + 1) * 512]
        wva = [("s_act", st[i]["ev"])]
        if i >= 4:
            wva.append(("s_act", st[i - 4].get("sq")))
        em("vector", lambda e, rope=rope, tg=tg, qsb=qsb, qav=qav:
           e.tensor_tensor(out=qav, in0=qsb,
                           in1=rope[:, tg * 512: tg * 512 + 512], op=ALU.mult),
           wait=wva, inc="s_dve", reads=(f"q_sb{cp}",), writes=(f"qa{cp}",))

    def emit_perm(i):
        h, which, tg, wbuf, rdy, dst, dcols, rope, xsrc, xcols, bias2 = tinfo(i)
        cp = i % 4
        pq = pQK[cp]
        qsb = q_sb[:, cp * 512:(cp + 1) * 512]
        qav = qa[:, cp * 512:(cp + 1) * 512]
        qbv = qb[:, cp * 512:(cp + 1) * 512]
        pe2 = em("tensor", lambda e, qsb=qsb, pq=pq: e.matmul(
            out=pq[:], lhsT=permb[:], rhs=qsb, start=True, stop=True),
            wait=[("s_act", st[i]["ev"])], inc="s_pe")
        st[i]["pe2"] = pe2
        vb = em("vector", lambda e, rope=rope, tg=tg, dcols=dcols, pq=pq, qbv=qbv:
                e.tensor_tensor(out=qbv, in0=pq[:],
                                in1=rope[:, dcols + tg * 512: dcols + tg * 512 + 512],
                                op=ALU.mult),
                wait=[("s_pe", pe2)], inc="s_dve", writes=(f"qb{cp}",))
        st[i]["vb"] = vb
        em("vector", lambda e, qav=qav, qbv=qbv: e.tensor_add(
            out=qav, in0=qav, in1=qbv),
            inc="s_dve", reads=(f"qa{cp}", f"qb{cp}"), writes=(f"qa{cp}",))
        st[i]["vq"] = cnt["s_dve"]
        st[i]["sq"] = em("scalar", lambda e, qav=qav, qsb=qsb: e.activation(
            out=qsb, in_=qav, func=AF.Square),
            wait=[("s_dve", st[i]["vq"]), ("s_pe", pe2)], inc="s_act",
            reads=(f"qa{cp}",), writes=(f"q_sb{cp}",))

    def emit_ss(i):
        cp = i % 4
        cp2 = i % 2
        qsb = q_sb[:, cp * 512:(cp + 1) * 512]
        w = [("s_act", st[i]["sq"])]
        if i >= 2:
            w.append(("s_act", st[i - 2].get("l1")))
        st[i]["pe3"] = em("tensor", lambda e, qsb=qsb, cp2=cp2: e.matmul(
            out=pSSB[cp2][:], lhsT=onesb2d[:], rhs=qsb, start=True, stop=True),
            wait=w, inc="s_pe")
        st[i]["l1"] = em("scalar", lambda e, cp2=cp2: e.activation(
            out=rbln[:, cp2 * 512:(cp2 + 1) * 512], in_=pSSB[cp2][:], func=AF.Ln,
            bias=fconst[:, 1:2], scale=1.0 / HD),
            wait=[("s_pe", st[i]["pe3"])], inc="s_act", writes=(f"rbln{cp2}",))
        h, which, tg, wbuf, rdy, dst, dcols, rope, xsrc, xcols, bias2 = tinfo(i)
        st[i]["l2"] = em("scalar", lambda e, bias2=bias2, cp2=cp2: e.activation(
            out=rstd_sb[:, cp2 * 512:(cp2 + 1) * 512],
            in_=rbln[:, cp2 * 512:(cp2 + 1) * 512], func=AF.Exp,
            scale=-0.5, bias=bias2),
            wait=[("s_dve", st[i - 2].get("vn")) if i >= 2 else (None, None)],
            inc="s_act", reads=(f"rbln{cp2}",), writes=(f"rstd_sb{cp2}",))

    def emit_bc(i):
        h, which, tg, wbuf, rdy, dst, dcols, rope, xsrc, xcols, bias2 = tinfo(i)
        cp = i % 4
        cp2 = i % 2
        qav = qa[:, cp * 512:(cp + 1) * 512]
        hpar = h & 1
        w6 = [("s_act", st[i]["l2"])]
        if h >= 2:
            w6.append(("s_ardy", spill_rdy[h - 2]))
        st[i]["vn"] = em("vector",
                         lambda e, dst=dst, tg=tg, hpar=hpar, dcols=dcols, cp2=cp2, qav=qav:
                         e.tensor_tensor(
                             out=dst[:, hpar * dcols + tg * 512:
                                     hpar * dcols + tg * 512 + 512],
                             in0=qav, in1=rstd_sb[:, cp2 * 512:(cp2 + 1) * 512],
                             op=ALU.mult),
                         wait=w6, inc="s_dve",
                         reads=(f"qa{cp}", f"rstd_sb{cp2}"), writes=("qkfm",))
        if tiles[i][1] == "k" and tiles[i][2] == 3:
            em("scalar", lambda e, h=h, hpar=hpar: e.dma_start(
                out=qscr_d[h], in_=q_fm[:, hpar * TOWN:(hpar + 1) * TOWN]),
                wait=[("s_dve", st[i]["vn"])], inc="s_aq")
            em("scalar", lambda e, h=h, hpar=hpar: e.dma_start(
                out=kscr_d[h], in_=k_fm[:, hpar * TSEQ:(hpar + 1) * TSEQ]),
                inc="s_aq")
            spill_rdy[h] = q_barrier("scalar", "s_aq", "s_ardy")

    for i in range(NTI + 3):
        if i < NTI:
            emit_mmg(i)
            if tiles[i][1] == "k" and tiles[i][2] == 3 and tiles[i][0] + 1 < H:
                stream_qkw(tiles[i][0] + 1)
        if i - 1 >= 0 and i - 1 < NTI:
            emit_perm(i - 1)
        if i - 3 >= 0:
            emit_bc(i - 3)
        if i - 2 >= 0 and i - 2 < NTI:
            emit_ss(i - 2)
    qk_spill_all = cnt["s_ardy"]

    new_block()
    pSC = [PB[0], PB[1], PB[2], PB[3]]
    pAV = [PB[4], PB[5]]
    pSUMB = PB[7]

    r_rdy = {}
    vhead_pe = {}

    def reload_head(h):
        par = h & 1
        wv = [("s_ardy", vscr_all), ("s_ardy", qk_spill_all),
              ("s_pe", vhead_pe.get(h - 2))]
        em("scalar", lambda e, h=h, par=par: e.dma_start(
            out=qr_fm[:, par * TOWN:(par + 1) * TOWN], in_=qscr_d[h]),
            wait=wv, inc="s_aq", writes=(f"qr{par}",))
        em("scalar", lambda e, h=h, par=par: e.dma_start(
            out=kr_fm[:, par * TSEQ:(par + 1) * TSEQ], in_=kscr_d[h]),
            inc="s_aq", writes=(f"kr{par}",))
        em("scalar", lambda e, h=h, par=par: e.dma_start(
            out=v_tm[:, par * D:(par + 1) * D].rearrange("p (tc o) -> p tc o", o=HD),
            in_=vscr_d[:].rearrange("tc p o -> p tc o")[:, :, h * HD:(h + 1) * HD]),
            inc="s_aq", writes=(f"v_tm{par}",))
        r_rdy[h] = q_barrier("scalar", "s_aq", "s_ardy")

    reload_head(0)
    reload_head(1)

    ents = []
    for h in range(H):
        for grp in range(2):
            js = list(range(8)) if grp == 0 else list(range(16))
            for idx, j in enumerate(js):
                ents.append((h, grp, j, idx == 0, idx == len(js) - 1))
    NEN = len(ents)
    est = [dict() for _ in range(NEN)]
    av_done = [None, None]
    sum_lr1 = [None]
    gidx = [0]

    def emit_scores(t):
        h, grp, j, first, last = ents[t]
        par = h & 1
        cp = t % 4
        w = [("s_ardy", r_rdy[h])]
        if t >= 4:
            w += [("s_act", est[t - 4].get("ex")), ("s_pe", est[t - 4].get("av_pe"))]
        pe = em("tensor", lambda e, j=j, grp=grp, par=par, cp=cp:
                e.matmul(out=pSC[cp][:],
                         lhsT=kr_fm[:, par * TSEQ + j * P: par * TSEQ + (j + 1) * P],
                         rhs=qr_fm[:, par * TOWN + grp * 512:
                                  par * TOWN + grp * 512 + 512],
                         start=True, stop=True, skip_group_check=True),
                wait=w, inc="s_pe")
        est[t]["pe"] = pe
        ex = em("scalar", lambda e, cp=cp: e.activation(
            out=attnT[:, cp * 512:(cp + 1) * 512], in_=pSC[cp][:], func=AF.Exp),
            wait=[("s_pe", pe)] + ([("s_pe", est[t - 4].get("av_pe"))] if t >= 4 else []),
            inc="s_act", writes=(f"attnT{cp}",))
        est[t]["ex"] = ex
        mi = j if (grp == 0 or j >= 8) else None
        dep = ("s_act", ex)
        if mi is not None:
            vm = em("vector", lambda e, cp=cp, mi=mi: e.tensor_mul(
                out=attnT[:, cp * 512:(cp + 1) * 512],
                in0=attnT[:, cp * 512:(cp + 1) * 512],
                in1=maskm[:, mi * 512:(mi + 1) * 512]),
                wait=[("s_act", ex)], inc="s_dve",
                reads=(f"attnT{cp}",), writes=(f"attnT{cp}",))
            dep = ("s_dve", vm)
        est[t]["dep"] = dep

    def emit_av(t):
        h, grp, j, first, last = ents[t]
        par = h & 1
        cp = t % 4
        gp = gidx[0] & 1
        w5 = [est[t]["dep"]]
        if first:
            w5 += [("s_dve", av_done[gp]), ("s_act", sum_lr1[0])]
        em("tensor", lambda e, j=j, par=par, cp=cp, first=first, last=last, gp=gp:
           e.matmul(out=pAV[gp][:],
                    lhsT=v_tm[:, par * D + j * P: par * D + (j + 1) * P],
                    rhs=attnT[:, cp * 512:(cp + 1) * 512],
                    start=first, stop=last, skip_group_check=True),
           wait=w5, inc="s_pe")
        pe_s = em("tensor", lambda e, cp=cp, first=first, last=last:
                  e.matmul(out=pSUMB[:], lhsT=onesb2d[:],
                           rhs=attnT[:, cp * 512:(cp + 1) * 512],
                           start=first, stop=last, skip_group_check=True),
                  inc="s_pe")
        est[t]["av_pe"] = pe_s
        if last:
            lr1 = em("scalar", lambda e: e.activation(
                out=rbln[:, 0:512], in_=pSUMB[:], func=AF.Ln),
                wait=[("s_pe", pe_s)], inc="s_act", writes=("rbln0",))
            sum_lr1[0] = lr1
            lr2 = em("scalar", lambda e: e.activation(
                out=rb_sb[:], in_=rbln[:, 0:512], func=AF.Exp, scale=-1.0),
                wait=[("s_dve", av_done[1 - gp])], inc="s_act",
                reads=("rbln0",), writes=("rb_sb",))
            ao = em("vector", lambda e, h=h, grp=grp, gp=gp: e.tensor_tensor(
                out=attn_out[:, h * TOWN + grp * 512: h * TOWN + grp * 512 + 512],
                in0=pAV[gp][:], in1=rb_sb[:], op=ALU.mult),
                wait=[("s_act", lr2)], inc="s_dve",
                reads=("rb_sb",), writes=("attn_out",))
            av_done[gp] = ao
            gidx[0] += 1
            if grp == 1:
                vhead_pe[h] = cnt["s_pe"]
                if h + 2 < H:
                    reload_head(h + 2)

    for t in range(NEN + 2):
        if t < NEN:
            emit_scores(t)
        if t - 2 >= 0:
            emit_av(t - 2)
    attn_done_dve = cnt["s_dve"]

    if "attn_out" in tap_d:
        td = tap_d["attn_out"]
        em("gpsimd", lambda e, td=td: e.dma_start(out=td[:], in_=attn_out[:]),
           wait=[("s_dve", attn_done_dve)], inc="s_tap")

    new_block()

    em("sync", lambda e: e.dma_start(
        out=resid[:].rearrange("p (tb d) -> p tb d", d=D),
        in_=x_own_d[:].rearrange("tb p d -> p tb d")),
        wait=[("s_tap", cnt["s_tap"] or None)], inc="s_xq")
    xown_rdy = q_barrier("sync", "s_xq", "s_xrdy")

    wo_rdy = {}
    wo_buf_pe = {}
    y_evac = {}

    def stream_wo(og, half):
        par = (og * 2 + half) & 1
        em("gpsimd", lambda e, og=og, half=half, par=par: e.dma_start(
            out=woTb[par][:].rearrange("p (k o) -> p k o", o=512),
            in_=woT_d[:].rearrange("(k p) o -> p k o", p=P)[
                :, half * 8:(half + 1) * 8, og * 512:(og + 1) * 512]),
            wait=[("s_pe", wo_buf_pe.get((og * 2 + half) - 2))], inc="s_wq",
            writes=(f"woT{par}",))
        wo_rdy[(og, half)] = q_barrier("gpsimd", "s_wq", "s_wrdy")

    stream_wo(0, 0)
    stream_wo(0, 1)
    for og in range(4):
        for half in range(2):
            par = (og * 2 + half) & 1
            for tb in range(NT):
                def y_mms(eng, tb=tb, half=half, par=par):
                    r = None
                    for cc in range(8):
                        c = half * 8 + cc
                        r = eng.matmul(
                            out=PB[tb][:],
                            lhsT=attn_out[:, c * TOWN + tb * P:
                                          c * TOWN + (tb + 1) * P],
                            rhs=woTb[par][:, cc * 512:(cc + 1) * 512],
                            start=(half == 0 and cc == 0),
                            stop=(half == 1 and cc == 7),
                            skip_group_check=True)
                    return r
                w = [("s_wrdy", wo_rdy[(og, half)]), ("s_dve", y_evac.get(tb))]
                pe = em("tensor", y_mms, wait=w, inc="s_pe")
                if half == 1:
                    y_evac[tb] = em(
                        "vector", lambda e, tb=tb, og=og: e.tensor_tensor(
                            out=resid[:, tb * D + og * 512:
                                      tb * D + og * 512 + 512],
                            in0=resid[:, tb * D + og * 512:
                                      tb * D + og * 512 + 512],
                            in1=PB[tb][:], op=ALU.add),
                        wait=[("s_pe", pe), ("s_xrdy", xown_rdy)], inc="s_dve",
                        reads=("resid",), writes=("resid",))
            wo_buf_pe[og * 2 + half] = cnt["s_pe"]
            nxt = og * 2 + half + 2
            if nxt < 8:
                stream_wo(nxt // 2, nxt % 2)
    wo_done_dve = cnt["s_dve"]

    moe_rdy = {}
    em("gpsimd", lambda e: e.dma_start(
        out=wg_bf[:].rearrange("p (k f) -> p k f", f=FI),
        in_=wgu_d[0].rearrange("(k p) f -> p k f", p=P)[:, :, 0:FI]), inc="s_wq")
    moe_rdy[("g", 0)] = q_barrier("gpsimd", "s_wq", "s_wrdy")
    em("gpsimd", lambda e: e.dma_start(
        out=wu_bf[:].rearrange("p (k f) -> p k f", f=FI),
        in_=wgu_d[0].rearrange("(k p) f -> p k f", p=P)[:, :, FI:2 * FI]), inc="s_wq")
    moe_rdy[("u", 0)] = q_barrier("gpsimd", "s_wq", "s_wrdy")

    resid_tap_out = None
    if "resid_mid" in tap_d:
        td = tap_d["resid_mid"]
        em("gpsimd", lambda e, td=td: e.dma_start(
            out=td[:].rearrange("tb p d -> p tb d"),
            in_=resid[:].rearrange("p (tb d) -> p tb d", d=D)),
            wait=[("s_dve", wo_done_dve)], inc="s_tap")
        resid_tap_out = cnt["s_tap"]

    new_block()
    pH_ = [PB[0], PB[1]]
    pR = PB[2]
    pD_ = [PB[4], PB[5], PB[6], PB[7]]

    em("gpsimd", lambda e: e.dma_start(
        out=gateb[:].rearrange("p (k o) -> p k o", o=32),
        in_=gate_d[:].rearrange("(k p) o -> p k o", p=P)), inc="s_wq")
    gate_rdy = q_barrier("gpsimd", "s_wq", "s_wrdy")

    rt_done = []
    for tb in range(NT):
        def r_mms(eng, tb=tb):
            r = None
            for k in range(KD):
                r = eng.matmul(out=pR[:, 0:32],
                               lhsT=xn_own[:, k * TOWN + tb * P:
                                           k * TOWN + (tb + 1) * P],
                               rhs=gateb[:, k * 32:(k + 1) * 32],
                               start=(k == 0), stop=(k == KD - 1))
            return r
        w = [("s_wrdy", gate_rdy)]
        if tb > 0:
            w.append(("s_dve", rt_done[-1]))
        pe = em("tensor", r_mms, wait=w, inc="s_pe")
        vmx = em("vector", lambda e: e.reduce_max(
            out=tk_s[:], in_=pR[:, 0:32], axis=AX.X, negate=True),
            wait=[("s_pe", pe)], inc="s_dve", writes=("tk_s",))
        aex = em("scalar", lambda e: e.activation(
            out=tk_p[:], in_=pR[:, 0:32], func=AF.Exp, bias=tk_s[:],
            accum_out=tk_mx[:, 0:1]),
            wait=[("s_dve", vmx)], inc="s_act",
            reads=("tk_s",), writes=("tk_p", "tk_mx"))
        em("vector", lambda e: e.reciprocal(out=tk_s[:], in_=tk_mx[:, 0:1]),
           wait=[("s_act", aex)], inc="s_dve",
           reads=("tk_mx",), writes=("tk_s",))
        em("vector", lambda e: e.tensor_scalar(
            out=tk_p[:], in0=tk_p[:], scalar1=tk_s[:], scalar2=None, op0=ALU.mult),
            inc="s_dve", reads=("tk_p", "tk_s"), writes=("tk_p",))
        em("vector", lambda e: e.tensor_copy(out=tk_w[:], in_=tk_p[:]),
           inc="s_dve", reads=("tk_p",), writes=("tk_w",))
        for it in range(4):
            em("vector", lambda e, it=it: e.reduce_max(
                out=tk_mx[:, it:it + 1], in_=tk_w[:], axis=AX.X),
                inc="s_dve", reads=("tk_w",), writes=("tk_mx",))
            if it < 3:
                em("vector", lambda e, it=it: e.tensor_scalar(
                    out=tk_keep[:], in0=tk_w[:], scalar1=tk_mx[:, it:it + 1],
                    scalar2=None, op0=ALU.is_ge),
                    inc="s_dve", reads=("tk_w", "tk_mx"), writes=("tk_keep",))
                em("vector", lambda e: e.scalar_tensor_tensor(
                    out=tk_w[:], in0=tk_keep[:], scalar=-2.0, in1=tk_w[:],
                    op0=ALU.mult, op1=ALU.add),
                    inc="s_dve", reads=("tk_keep", "tk_w"), writes=("tk_w",))
        em("vector", lambda e: e.reduce_sum(out=tk_s[:], in_=tk_mx[:], axis=AX.X),
           inc="s_dve", reads=("tk_mx",), writes=("tk_s",))
        em("vector", lambda e: e.reciprocal(out=tk_s[:], in_=tk_s[:]),
           inc="s_dve", reads=("tk_s",), writes=("tk_s",))
        em("vector", lambda e: e.tensor_scalar(
            out=tk_keep[:], in0=tk_p[:], scalar1=tk_mx[:, 3:4], scalar2=None,
            op0=ALU.is_ge), inc="s_dve",
            reads=("tk_p", "tk_mx"), writes=("tk_keep",))
        v = em("vector", lambda e, tb=tb: e.scalar_tensor_tensor(
            out=wrt[:, tb * 32:(tb + 1) * 32], in0=tk_p[:], scalar=tk_s[:],
            in1=tk_keep[:], op0=ALU.mult, op1=ALU.mult),
            inc="s_dve", reads=("tk_p", "tk_s", "tk_keep"), writes=("wrt",))
        rt_done.append(v)
    router_done = cnt["s_dve"]
    if "wrt" in tap_d:
        td = tap_d["wrt"]
        em("gpsimd", lambda e, td=td: e.dma_start(out=td[:], in_=wrt[:]),
           wait=[("s_dve", router_done)], inc="s_tap")

    pe_after = {}

    def moe_piece(kind, e):
        if kind == "g":
            dst = wg_bf[:].rearrange("p (k f) -> p k f", f=FI)
            src = wgu_d[e].rearrange("(k p) f -> p k f", p=P)[:, :, 0:FI]
            gw = pe_after.get(("G", e - 1))
        elif kind == "u":
            dst = wu_bf[:].rearrange("p (k f) -> p k f", f=FI)
            src = wgu_d[e].rearrange("(k p) f -> p k f", p=P)[:, :, FI:2 * FI]
            gw = pe_after.get(("U", e - 1))
        else:
            dst = wd_bf[:].rearrange("p (k f) -> p k f", f=D)
            src = wdn_d[e].rearrange("(k p) f -> p k f", p=P)
            gw = pe_after.get(("D", e - 1))
        em("gpsimd", lambda e_, dst=dst, src=src: e_.dma_start(out=dst, in_=src),
           wait=[("s_pe", gw)], inc="s_wq")
        moe_rdy[(kind, e)] = q_barrier("gpsimd", "s_wq", "s_wrdy")

    hpar = [0]
    pH_guard = [None, None]
    silu_done = {}
    pass1_done = {}
    last_D_evac = {}

    if cnt["s_tap"]:
        em("gpsimd", lambda e, v=cnt["s_tap"], s=SEM["s_tap"]: e.wait_ge(s, v))
    moe_piece("d", 0)
    for e in range(NE):
        for j in range(KF):
            for tg in range(2):
                p_ = hpar[0]; hpar[0] ^= 1
                pp = pH_[p_]

                def g_mms(eng, j=j, tg=tg, pp=pp):
                    r = None
                    for k in range(KD):
                        r = eng.matmul(
                            out=pp[:],
                            lhsT=wg_bf[:, k * FI + j * P: k * FI + (j + 1) * P],
                            rhs=xn_own[:, k * TOWN + tg * 512:
                                       k * TOWN + tg * 512 + 512],
                            start=(k == 0), stop=(k == KD - 1))
                    return r
                gw = pH_guard[p_]
                pe = em("tensor", g_mms,
                        wait=[("s_wrdy", moe_rdy[("g", e)])] + ([gw] if gw else []),
                        inc="s_pe")
                p1 = pass1_done.get((j, tg))
                sv = em("scalar", lambda e_, j=j, tg=tg, pp=pp: e_.activation(
                    out=silu_g[:, j * TOWN + tg * 512: j * TOWN + tg * 512 + 512],
                    in_=pp[:], func=AF.Silu),
                    wait=[("s_pe", pe)] + ([p1] if p1 else []),
                    inc="s_act", writes=("silu_g",))
                silu_done[(j, tg)] = sv
                pH_guard[p_] = ("s_act", sv)
        pe_after[("G", e)] = cnt["s_pe"]
        if e + 1 < NE:
            moe_piece("g", e + 1)
        for j in range(KF):
            for tg in range(2):
                p_ = hpar[0]; hpar[0] ^= 1
                pp = pH_[p_]

                def u_mms(eng, j=j, tg=tg, pp=pp):
                    r = None
                    for k in range(KD):
                        r = eng.matmul(
                            out=pp[:],
                            lhsT=wu_bf[:, k * FI + j * P: k * FI + (j + 1) * P],
                            rhs=xn_own[:, k * TOWN + tg * 512:
                                       k * TOWN + tg * 512 + 512],
                            start=(k == 0), stop=(k == KD - 1))
                    return r
                gw = pH_guard[p_]
                pe = em("tensor", u_mms,
                        wait=[("s_wrdy", moe_rdy[("u", e)])] + ([gw] if gw else []),
                        inc="s_pe")
                w1 = [("s_pe", pe), ("s_act", silu_done[(j, tg)]),
                      ("s_pe", pe_after.get(("D", e - 1)))]
                v = em("vector", lambda e_, j=j, tg=tg, pp=pp: e_.tensor_tensor(
                    out=gated[:, j * TOWN + tg * 512: j * TOWN + tg * 512 + 512],
                    in0=pp[:],
                    in1=silu_g[:, j * TOWN + tg * 512: j * TOWN + tg * 512 + 512],
                    op=ALU.mult),
                    wait=w1, inc="s_dve", reads=("silu_g",), writes=("gated",))
                pass1_done[(j, tg)] = ("s_dve", v)
                pH_guard[p_] = ("s_dve", v)
        pe_after[("U", e)] = cnt["s_pe"]
        gated_done = cnt["s_dve"]
        if e + 1 < NE:
            moe_piece("u", e + 1)
        for tb in range(NT):
            for m in range(4):
                def d_mms(eng, tb=tb, m=m):
                    r = None
                    for k in range(KF):
                        r = eng.matmul(
                            out=pD_[m][:],
                            lhsT=gated[:, k * TOWN + tb * P:
                                       k * TOWN + (tb + 1) * P],
                            rhs=wd_bf[:, k * D + m * 512: k * D + (m + 1) * 512],
                            start=(k == 0), stop=(k == KF - 1))
                    return r
                w = [("s_wrdy", moe_rdy[("d", e)]), ("s_dve", gated_done),
                     ("s_dve", router_done), ("s_dve", last_D_evac.get(m))]
                pe = em("tensor", d_mms, wait=w, inc="s_pe")
                v = em("vector", lambda e_, tb=tb, m=m, e=e: e_.scalar_tensor_tensor(
                    out=resid[:, tb * D + m * 512: tb * D + (m + 1) * 512],
                    in0=pD_[m][:],
                    scalar=wrt[:, tb * 32 + e: tb * 32 + e + 1],
                    in1=resid[:, tb * D + m * 512: tb * D + (m + 1) * 512],
                    op0=ALU.mult, op1=ALU.add),
                    wait=[("s_pe", pe)] + (
                        [("s_tap", cnt["s_tap"] or None)] if e == 0 else []),
                    inc="s_dve",
                    reads=("resid", "wrt"), writes=("resid",))
                last_D_evac[m] = v
        pe_after[("D", e)] = cnt["s_pe"]
        if e + 1 < NE:
            moe_piece("d", e + 1)
    moe_done = cnt["s_dve"]

    new_block()
    em("sync", lambda e: e.dma_start(
        out=out_d[:].rearrange("tb p d -> p tb d"),
        in_=resid[:].rearrange("p (tb d) -> p tb d", d=D)),
        wait=[("s_dve", moe_done)], inc="s_out")
    em("sync", lambda e, v=cnt["s_out"], s=SEM["s_out"]: e.wait_ge(s, v))
    em("gpsimd", lambda e, v=cnt["s_tap"], s=SEM["s_tap"]: e.wait_ge(s, v))

    with ES:
        replay_all()
    return nc


def _host_tables():
    inv = 1.0 / 10000.0 ** (np.arange(0, HD, 2, dtype=np.float64) / HD)
    invr = np.repeat(inv, 2)
    posk = np.arange(TSEQ, dtype=np.float64)

    def mk(pos):
        fr = invr[:, None] * pos[None, :]
        return np.stack([np.cos(fr), np.sin(fr)]).astype(ml_dtypes.bfloat16)

    ropek = mk(posk)
    ropeq, masks = {}, {}
    tri = (np.arange(P)[:, None] <= np.arange(P)[None, :]).astype(np.float32)
    for par in (0, 1):
        pos = np.concatenate(
            [np.arange((2 * b + par) * P, (2 * b + par + 1) * P)
             for b in range(NT)]).astype(np.float64)
        ropeq[par] = mk(pos)
        blocks = [2 * b + par for b in range(NT)]
        m = np.zeros((16, P, 512), dtype=np.float32)
        for gi, grp in enumerate((blocks[:4], blocks[4:])):
            for jj in range(8):
                j = jj if gi == 0 else 8 + jj
                for bi, B in enumerate(grp):
                    if j < B:
                        m[j, :, bi * P:(bi + 1) * P] = 1.0
                    elif j == B:
                        m[j, :, bi * P:(bi + 1) * P] = tri
        masks[par] = m.astype(ml_dtypes.bfloat16)

    perm = np.zeros((P, P), dtype=np.float32)
    for i in range(HD // 2):
        perm[2 * i + 1, 2 * i] = -1.0
        perm[2 * i, 2 * i + 1] = 1.0
    constb = np.stack([np.eye(P, dtype=np.float32), perm,
                       np.ones((P, P), dtype=np.float32)]).astype(ml_dtypes.bfloat16)
    onesrf = np.ones((1, P), dtype=np.float32)
    fconst = np.tile(np.array([[LN_EPS, RMS_EPS, -0.5 * np.log(HD), 0.0]],
                              dtype=np.float32), (P, 1))
    return ropeq, ropek, constb, onesrf, fconst, masks


def make_in_maps(L0_attn_x, L0_attn_W_qkv, L0_attn_W_o, L0_ffn_gate,
                 L0_ffn_gate_up_proj, L0_ffn_down_proj):
    x = np.ascontiguousarray(np.asarray(L0_attn_x, dtype=np.float32))
    wqkvT = np.ascontiguousarray(np.asarray(L0_attn_W_qkv, dtype=np.float32).T)
    woT = np.ascontiguousarray(np.asarray(L0_attn_W_o, dtype=np.float32).T)
    gate = np.ascontiguousarray(np.asarray(L0_ffn_gate, dtype=np.float32))
    wgu = np.ascontiguousarray(np.asarray(L0_ffn_gate_up_proj, dtype=np.float32))
    wdn = np.ascontiguousarray(np.asarray(L0_ffn_down_proj, dtype=np.float32))
    ropeq, ropek, constb, onesrf, fconst, masks = _host_tables()
    in_maps = []
    for c in range(8):
        s, par = c // 2, c % 2
        xs = x[s]
        x_seq = np.ascontiguousarray(xs.reshape(NTS, P, D))
        x_own = np.ascontiguousarray(
            np.stack([xs[(2 * b + par) * P:(2 * b + par + 1) * P]
                      for b in range(NT)]))
        in_maps.append(dict(
            x_own=x_own, x_seq=x_seq, wqkvT=wqkvT, woT=woT, gate=gate,
            wgu=wgu, wdn=wdn, ropeq=ropeq[par], ropek=ropek,
            maskm=masks[par], constb=constb, onesrf=onesrf, fconst=fconst))
    return in_maps


def kernel(L0_attn_x, L0_attn_W_qkv, L0_attn_W_o, L0_ffn_gate,
           L0_ffn_gate_up_proj, L0_ffn_down_proj):
    in_maps = make_in_maps(L0_attn_x, L0_attn_W_qkv, L0_attn_W_o, L0_ffn_gate,
                           L0_ffn_gate_up_proj, L0_ffn_down_proj)
    nc = build_nc()
    res = run_bass_kernel_spmd(nc, in_maps, core_ids=list(range(8)))
    out = np.empty((4, TSEQ, D), dtype=np.float32)
    for c in range(8):
        s, par = c // 2, c % 2
        r = np.asarray(res.results[c]["out"])
        for b in range(NT):
            out[s, (2 * b + par) * P:(2 * b + par + 1) * P, :] = r[b]
    return out


# version 05
# speedup vs baseline: 1.3310x; 1.0049x over previous
import numpy as np
import ml_dtypes
import concourse.bass as bass
import concourse.mybir as mybir
from concourse.bass_utils import run_bass_kernel_spmd

F32 = mybir.dt.float32
BF16 = mybir.dt.bfloat16
AF = mybir.ActivationFunctionType
ALU = mybir.AluOpType
AX = mybir.AxisListType

P = 128
D = 2048
H = 16
HD = 128
TOWN = 1024
TSEQ = 2048
NE = 32
FI = 768
KD = D // P
KF = FI // P
NT = TOWN // P
NTS = TSEQ // P
LN_EPS = 1e-5
RMS_EPS = 1.1920929e-07

SB_BASE = 16512
SB_TOP = 229344


def build_nc(taps=()):
    nc = bass.Bass("TRN2", target_bir_lowering=False, debug=False)

    x_own_d = nc.dram_tensor("x_own", [NT, P, D], F32, kind="ExternalInput")
    x_seq_d = nc.dram_tensor("x_seq", [NTS, P, D], F32, kind="ExternalInput")
    wqkvT_d = nc.dram_tensor("wqkvT", [D, 3 * D], F32, kind="ExternalInput")
    woT_d = nc.dram_tensor("woT", [D, D], F32, kind="ExternalInput")
    gate_d = nc.dram_tensor("gate", [D, 32], F32, kind="ExternalInput")
    wgu_d = nc.dram_tensor("wgu", [NE, D, 2 * FI], F32, kind="ExternalInput")
    wdn_d = nc.dram_tensor("wdn", [NE, FI, D], F32, kind="ExternalInput")
    ropeq_d = nc.dram_tensor("ropeq", [2, P, TOWN], BF16, kind="ExternalInput")
    ropek_d = nc.dram_tensor("ropek", [2, P, TSEQ], BF16, kind="ExternalInput")
    mask_d = nc.dram_tensor("maskm", [16, P, 512], BF16, kind="ExternalInput")
    constb_d = nc.dram_tensor("constb", [3, P, P], BF16, kind="ExternalInput")
    onesrf_d = nc.dram_tensor("onesrf", [1, P], F32, kind="ExternalInput")
    fconst_d = nc.dram_tensor("fconst", [P, 4], F32, kind="ExternalInput")
    vscr_d = nc.dram_tensor("vscr", [NTS, P, D], BF16)
    qscr_d = nc.dram_tensor("qscr", [H, P, TOWN], BF16)
    kscr_d = nc.dram_tensor("kscr", [H, P, TSEQ], BF16)
    out_d = nc.dram_tensor("out", [NT, P, D], F32, kind="ExternalOutput")
    tap_d = {}
    for t in taps:
        shp = {"xn_own": [P, KD * TOWN], "xn_seq": [P, KD * TSEQ],
               "attn_out": [P, H * TOWN], "resid_mid": [NT, P, D],
               "wrt": [P, NT * 32], "qfm0": [P, TOWN], "kfm0": [P, TSEQ]}[t]
        tap_d[t] = nc.dram_tensor("tap_" + t, shp, F32, kind="ExternalOutput")

    def at(name, shape, dtype, off):
        esz = {BF16: 2, F32: 4}[dtype]
        nbytes = int(np.prod(shape[1:])) * esz
        assert SB_BASE + off + nbytes <= SB_TOP, (name, off, nbytes)
        return nc.alloc_sbuf_tensor_at(name, shape, dtype, offset=SB_BASE + off)

    xn_own = at("xn_own", [P, KD * TOWN], BF16, 0)
    onescol = at("onescol", [P, 1], BF16, 32768)
    onesrf = at("onesrf", [1, P], F32, 32800)
    onesb2d = at("onesb2d", [P, P], BF16, 210560)
    permb = at("permb", [P, P], BF16, 33312)
    identb = at("identb", [P, P], BF16, 33568)
    lnrow = at("lnrow", [P, 4], F32, 33824)
    fconst = at("fconst", [P, 4], F32, 33856)
    R2 = 33920
    xn_seq = at("xn_seq", [P, KD * TSEQ], BF16, R2)
    resid = at("resid", [P, NT * D], F32, R2)
    R3 = R2 + 65536
    ropeq = at("ropeq", [P, 2 * TOWN], BF16, R3)
    ropek = at("ropek", [P, 2 * TSEQ], BF16, R3 + 4096)
    maskm = at("maskm", [P, 16 * 512], BF16, R3 + 12288)
    woTb = [at("woTa", [P, 8 * 512], BF16, R3),
            at("woTbb", [P, 8 * 512], BF16, R3 + 8192)]
    silu_g = at("silu_g", [P, KF * TOWN], BF16, R3)
    gated = at("gated", [P, KF * TOWN], BF16, R3 + 12288)
    gateb = at("gateb", [P, KD * 32], BF16, R3 + 24576)
    wrt = at("wrt", [P, NT * 32], F32, R3 + 25600)
    tk_p = at("tk_p", [P, 32], F32, R3 + 26624)
    tk_w = at("tk_w", [P, 32], F32, R3 + 26752)
    tk_keep = at("tk_keep", [P, 32], F32, R3 + 26880)
    tk_mx = at("tk_mx", [P, 4], F32, R3 + 27008)
    tk_s = at("tk_s", [P, 1], F32, R3 + 27136)
    R4 = R3 + 28672
    vw_buf = at("vw_buf", [P, 2 * KD * 512], BF16, R4)
    vstage = at("vstage", [P, 2 * 512], BF16, R4 + 32768)
    wqh = at("wqh", [P, D], BF16, R4)
    wkh = at("wkh", [P, D], BF16, R4 + 4096)
    q_fm = at("q_fm", [P, 2 * TOWN], BF16, R4 + 8192)
    k_fm = at("k_fm", [P, 2 * TSEQ], BF16, R4 + 12288)
    q_sb = at("q_sb", [P, 4 * 512], BF16, R4 + 20480)
    qa = at("qa", [P, 4 * 512], F32, R4 + 24576)
    qb = at("qb", [P, 4 * 512], F32, R4 + 32768)
    rbln = at("rbln", [P, 2 * 512], F32, R4 + 40960)
    rstd_sb = at("rstd_sb", [P, 2 * 512], BF16, R4 + 45056)
    qr_fm = at("qr_fm", [P, 2 * TOWN], BF16, R4)
    kr_fm = at("kr_fm", [P, 2 * TSEQ], BF16, R4 + 4096)
    v_tm = at("v_tm", [P, 2 * D], BF16, R4 + 12288)
    attnT = at("attnT", [P, 4 * 512], BF16, R4 + 20480)
    rb_sb = at("rb_sb", [P, 512], BF16, R4 + 24576 + 4096)
    wg_bf = at("wg_bf", [P, KD * FI], BF16, R4)
    wu_bf = at("wu_bf", [P, KD * FI], BF16, R4 + 24576)
    R5 = R4 + 49152
    xstage = at("xstage", [P, 2 * D], F32, R5)
    lnsq = at("lnsq", [P, D], F32, R5 + 16384)
    xn_tmp = at("xn_tmp", [P, D], BF16, R5 + 24576)
    attn_out = at("attn_out", [P, H * TOWN], BF16, R5)
    wd_bf = at("wd_bf", [P, KF * D], BF16, R5)

    from contextlib import ExitStack
    ES = ExitStack()
    PB = [ES.enter_context(nc.psum_tensor(f"PB{i}", [P, 512], F32))
          for i in range(8)]

    sem = lambda n: ES.enter_context(nc.semaphore(n))
    SEM = {n: sem(n) for n in
           ("s_wq", "s_wrdy", "s_xq", "s_xrdy", "s_aq", "s_ardy",
            "s_pe", "s_act", "s_dve", "s_out", "s_vs0", "s_vs1", "s_tap")}
    DMA_SEMS = {"s_wq", "s_xq", "s_aq", "s_out", "s_vs0", "s_vs1", "s_tap"}
    cnt = {k: 0 for k in SEM}

    blocks, cur, shadow = [], None, None

    def new_block():
        nonlocal cur, shadow
        cur = {k: [] for k in ("sync", "tensor", "vector", "scalar", "gpsimd")}
        shadow = {k: (set(), set()) for k in cur}
        blocks.append(cur)

    def em(eng, fn, wait=None, inc=None, reads=(), writes=()):
        rs, ws = shadow[eng]
        if eng in ("vector", "scalar", "gpsimd"):
            if (ws & set(reads)) or (set(writes) & (ws | rs)):
                cur[eng].append((lambda e: e.drain(), [], None))
                rs.clear(); ws.clear()
        rs |= set(reads); ws |= set(writes)
        n = None
        if inc is not None:
            step = 16 if inc in DMA_SEMS else 1
            cnt[inc] += step
            n = (SEM[inc], step)
        wl = [(SEM[s], v) for s, v in (wait or [])
              if s is not None and v is not None]
        cur[eng].append((fn, wl, n))
        return cnt[inc] if inc else None

    def q_barrier(eng, q_sem, rdy_sem):
        em(eng, lambda e, v=cnt[q_sem], s=SEM[q_sem]: e.wait_ge(s, v))
        return em(eng, lambda e: e.nop(), inc=rdy_sem)

    def replay_all():
        def mk(eng_ops):
            def f(eng):
                for fn, waits, inc in eng_ops:
                    for s, v in waits:
                        eng.wait_ge(s, v)
                    r = fn(eng)
                    if inc is not None:
                        s, step = inc
                        if r is None:
                            r = eng.nop()
                        r.then_inc(s, step)
            return f
        for ops in blocks:
            with nc.Block() as block:
                block.sync(mk(ops["sync"]))
                block.tensor(mk(ops["tensor"]))
                block.vector(mk(ops["vector"]))
                block.scalar(mk(ops["scalar"]))
                block.gpsimd(mk(ops["gpsimd"]))

    def rearr_w(dram2d, c0, c1):
        return dram2d[:].rearrange("(k p) o -> p k o", p=P)[:, :, c0:c1]

    new_block()
    pT = [PB[0][:].bitcast(BF16), PB[1][:].bitcast(BF16)]

    em("scalar", lambda e: e.dma_start(out=identb[:], in_=constb_d[0]), inc="s_aq")
    em("scalar", lambda e: e.dma_start(out=permb[:], in_=constb_d[1]), inc="s_aq")
    em("vector", lambda e: e.memset(onescol[:], 1.0), inc="s_dve", writes=("onescol",))
    em("scalar", lambda e: e.dma_start(out=onesrf[:], in_=onesrf_d[:]), inc="s_aq")
    em("scalar", lambda e: e.dma_start(out=onesb2d[:], in_=constb_d[2]), inc="s_aq")
    em("scalar", lambda e: e.dma_start(out=fconst[:], in_=fconst_d[:]), inc="s_aq")
    em("scalar", lambda e: e.dma_start(out=ropeq[:, 0:TOWN], in_=ropeq_d[0]), inc="s_aq")
    em("scalar", lambda e: e.dma_start(out=ropeq[:, TOWN:2 * TOWN], in_=ropeq_d[1]), inc="s_aq")
    em("scalar", lambda e: e.dma_start(out=ropek[:, 0:TSEQ], in_=ropek_d[0]), inc="s_aq")
    em("scalar", lambda e: e.dma_start(out=ropek[:, TSEQ:2 * TSEQ], in_=ropek_d[1]), inc="s_aq")
    em("scalar", lambda e: e.dma_start(
        out=maskm[:].rearrange("p (j t) -> p j t", t=512),
        in_=mask_d[:].rearrange("j p t -> p j t")), inc="s_aq")
    consts_rdy = q_barrier("scalar", "s_aq", "s_ardy")
    onescol_dve = cnt["s_dve"]

    vw_rdy, vw_pe = {}, {}

    def stream_vw(og):
        par = og & 1
        em("gpsimd", lambda e, og=og, par=par: e.dma_start(
            out=vw_buf[:, par * KD * 512:(par + 1) * KD * 512].rearrange(
                "p (k o) -> p k o", o=512),
            in_=rearr_w(wqkvT_d, 2 * D + og * 512, 2 * D + (og + 1) * 512)),
            wait=[("s_pe", vw_pe.get(og - 2))], inc="s_wq",
            writes=(f"vw{par}",))
        vw_rdy[og] = q_barrier("gpsimd", "s_wq", "s_wrdy")

    stream_vw(0)
    stream_vw(1)

    x_rdy, ln_done, chunk_act = [], [], []
    tp_guard = [None, None]
    tp_last = None
    a1 = None
    for i in range(NTS + NT):
        sl = i & 1
        w = [("s_dve", ln_done[i - 2])] if i >= 2 else []
        em("sync", lambda e, i=i, sl=sl: e.dma_start(
            out=xstage[:, sl * D:(sl + 1) * D],
            in_=(x_seq_d[i] if i < NTS else x_own_d[i - NTS])),
            wait=w, inc="s_xq")
        x_rdy.append(q_barrier("sync", "s_xq", "s_xrdy"))

        em("vector", lambda e, sl=sl: e.tensor_scalar(
            out=lnsq[:], in0=xstage[:, sl * D:(sl + 1) * D],
            scalar1=1.0, scalar2=None, op0=ALU.mult, op1=ALU.add,
            accum_out=lnrow[:, 0:1]),
            wait=[("s_xrdy", x_rdy[i]), ("s_act", a1)],
            inc="s_dve", reads=(f"xst{sl}",), writes=("lnrow0", "lnsq"))
        v2 = em("vector", lambda e: e.tensor_scalar(
            out=lnrow[:, 1:2], in0=lnrow[:, 0:1], scalar1=-1.0 / D, scalar2=None,
            op0=ALU.mult), inc="s_dve", reads=("lnrow0",), writes=("lnrow1",))
        a1 = em("scalar", lambda e, sl=sl: e.activation(
            out=lnsq[:], in_=xstage[:, sl * D:(sl + 1) * D], func=AF.Square,
            bias=lnrow[:, 1:2], accum_out=lnrow[:, 2:3]),
            wait=[("s_dve", v2)], inc="s_act",
            reads=(f"xst{sl}", "lnrow1"), writes=("lnsq", "lnrow2"))
        em("scalar", lambda e: e.activation(
            out=lnrow[:, 3:4], in_=lnrow[:, 2:3], func=AF.Ln,
            bias=fconst[:, 0:1], scale=1.0 / D), inc="s_act",
            reads=("lnrow2",), writes=("lnrow3",))
        a3 = em("scalar", lambda e: e.activation(
            out=lnrow[:, 3:4], in_=lnrow[:, 3:4], func=AF.Exp,
            scale=-0.5), inc="s_act", reads=("lnrow3",), writes=("lnrow3",))
        v3 = em("vector", lambda e, sl=sl: e.tensor_scalar(
            out=xn_tmp[:], in0=xstage[:, sl * D:(sl + 1) * D],
            scalar1=lnrow[:, 1:2], scalar2=lnrow[:, 3:4],
            op0=ALU.add, op1=ALU.mult),
            wait=[("s_act", a3), ("s_pe", tp_last)], inc="s_dve",
            reads=(f"xst{sl}", "lnrow1", "lnrow3"), writes=("xn_tmp",))
        ln_done.append(v3)
        dst, tcols, tb = ((xn_seq, TSEQ, i) if i < NTS else (xn_own, TOWN, i - NTS))
        for k in range(KD):
            par = k & 1
            w = [("s_dve", v3), ("s_act", tp_guard[par])]
            if i == 0 and k == 0:
                w.append(("s_ardy", consts_rdy))
            pe = em("tensor", lambda e, k=k, par=par: e.transpose(
                out=pT[par][:, 0:P], in_=xn_tmp[:, k * P:(k + 1) * P],
                identity=identb[:]), wait=w, inc="s_pe")
            tp_guard[par] = em(
                "scalar",
                lambda e, k=k, tb=tb, par=par, dst=dst, tcols=tcols:
                e.copy(out=dst[:, k * tcols + tb * P: k * tcols + (tb + 1) * P],
                       in_=pT[par][:, 0:P]),
                wait=[("s_pe", pe)], inc="s_act", writes=("xn",))
        tp_last = cnt["s_pe"]
        chunk_act.append(cnt["s_act"])
    xn_all_act = cnt["s_act"]

    vst_dma = {}
    vst_guard = [None, None]
    for og in range(4):
        par = og & 1
        if og >= 2:
            stream_vw(og)
        for tc in range(NTS):
            idx0 = og * NTS + tc
            ppar = idx0 & 1
            pp = PB[2 + ppar]

            def v_mms(eng, tc=tc, par=par, pp=pp):
                r = None
                for k in range(KD):
                    r = eng.matmul(
                        out=pp[:],
                        lhsT=xn_seq[:, k * TSEQ + tc * P: k * TSEQ + (tc + 1) * P],
                        rhs=vw_buf[:, par * KD * 512 + k * 512:
                                   par * KD * 512 + (k + 1) * 512],
                        start=(k == 0), stop=(k == KD - 1))
                return r
            w = [("s_wrdy", vw_rdy[og]), ("s_act", chunk_act[tc]),
                 ("s_act", vst_guard[ppar])]
            pe = em("tensor", v_mms, wait=w, inc="s_pe")
            idx = og * NTS + tc
            slot = idx & 1
            wv = [("s_pe", pe)]
            if idx >= 2:
                wv.append((f"s_vs{slot}", 16 * (idx // 2)))
            vst_guard[ppar] = em(
                "scalar", lambda e, ppar=ppar, pp=pp:
                e.copy(out=vstage[:, ppar * 512:(ppar + 1) * 512], in_=pp[:]),
                wait=wv, inc="s_act", writes=(f"vstg{ppar}",))
            em("scalar", lambda e, tc=tc, og=og, ppar=ppar: e.dma_start(
                out=vscr_d[tc, :, og * 512:(og + 1) * 512],
                in_=vstage[:, ppar * 512:(ppar + 1) * 512]),
                inc=f"s_vs{slot}", reads=(f"vstg{ppar}",))
        vw_pe[og] = cnt["s_pe"]
    em("scalar", lambda e, v=cnt["s_vs0"], s=SEM["s_vs0"]: e.wait_ge(s, v))
    em("scalar", lambda e, v=cnt["s_vs1"], s=SEM["s_vs1"]: e.wait_ge(s, v))
    vscr_all = em("scalar", lambda e: e.nop(), inc="s_ardy")

    if "xn_own" in tap_d:
        td = tap_d["xn_own"]
        em("gpsimd", lambda e, td=td: e.dma_start(out=td[:], in_=xn_own[:]),
           wait=[("s_act", xn_all_act)], inc="s_tap")
    if "xn_seq" in tap_d:
        td = tap_d["xn_seq"]
        em("gpsimd", lambda e, td=td: e.dma_start(out=td[:], in_=xn_seq[:]),
           wait=[("s_act", xn_all_act)], inc="s_tap")

    new_block()
    pQK = [PB[0], PB[1], PB[2], PB[3]]
    pSSB = [PB[4], PB[5]]

    wq_rdy, wk_rdy = {}, {}
    spill_rdy = {}

    tiles = []
    for h in range(H):
        for which, ntg in (("q", 2), ("k", 4)):
            for tg in range(ntg):
                tiles.append((h, which, tg))
    NTI = len(tiles)

    st = [dict() for _ in range(NTI)]

    def tinfo(i):
        h, which, tg = tiles[i]
        wbuf = wqh if which == "q" else wkh
        rdy = (wq_rdy if which == "q" else wk_rdy)[h]
        dst = q_fm if which == "q" else k_fm
        dcols = TOWN if which == "q" else TSEQ
        rope = ropeq if which == "q" else ropek
        xsrc, xcols = (xn_own, TOWN) if which == "q" else (xn_seq, TSEQ)
        bias2 = fconst[:, 2:3] if which == "q" else fconst[:, 3:4]
        return h, which, tg, wbuf, rdy, dst, dcols, rope, xsrc, xcols, bias2

    def stream_qkw(h):
        wq = [("s_pe", st[6 * (h - 1) + 1].get("pe1"))] if h > 0 else None
        em("gpsimd", lambda e, h=h: e.dma_start(
            out=wqh[:].rearrange("p (k o) -> p k o", o=HD),
            in_=rearr_w(wqkvT_d, h * HD, (h + 1) * HD)),
            wait=wq, inc="s_wq", writes=("wqh",))
        wq_rdy[h] = q_barrier("gpsimd", "s_wq", "s_wrdy")
        wk = [("s_pe", st[6 * (h - 1) + 5].get("pe1"))] if h > 0 else None
        em("gpsimd", lambda e, h=h: e.dma_start(
            out=wkh[:].rearrange("p (k o) -> p k o", o=HD),
            in_=rearr_w(wqkvT_d, D + h * HD, D + (h + 1) * HD)),
            wait=wk, inc="s_wq", writes=("wkh",))
        wk_rdy[h] = q_barrier("gpsimd", "s_wq", "s_wrdy")

    stream_qkw(0)

    def emit_mmg(i):
        h, which, tg, wbuf, rdy, dst, dcols, rope, xsrc, xcols, bias2 = tinfo(i)
        cp = i % 4
        pq = pQK[cp]

        def qk_mms(eng, wbuf=wbuf, xsrc=xsrc, xcols=xcols, tg=tg, pq=pq):
            r = None
            for k in range(KD):
                r = eng.matmul(
                    out=pq[:], lhsT=wbuf[:, k * HD:(k + 1) * HD],
                    rhs=xsrc[:, k * xcols + tg * 512: k * xcols + tg * 512 + 512],
                    start=(k == 0), stop=(k == KD - 1))
            return r
        w = [("s_wrdy", rdy), ("s_act", xn_all_act)]
        if i >= 4:
            w.append(("s_dve", st[i - 4].get("vb")))
        st[i]["pe1"] = em("tensor", qk_mms, wait=w, inc="s_pe")
        qsb = q_sb[:, cp * 512:(cp + 1) * 512]
        wv = [("s_pe", st[i]["pe1"])]
        if i >= 4:
            wv += [("s_pe", st[i - 4].get("pe3")), ("s_dve", st[i - 4].get("vn"))]
        st[i]["ev"] = em("scalar", lambda e, qsb=qsb, pq=pq: e.copy(out=qsb, in_=pq[:]),
                         wait=wv, inc="s_act", writes=(f"q_sb{cp}",))
        qav = qa[:, cp * 512:(cp + 1) * 512]
        wva = [("s_act", st[i]["ev"])]
        if i >= 4:
            wva.append(("s_act", st[i - 4].get("sq")))
        em("vector", lambda e, rope=rope, tg=tg, qsb=qsb, qav=qav:
           e.tensor_tensor(out=qav, in0=qsb,
                           in1=rope[:, tg * 512: tg * 512 + 512], op=ALU.mult),
           wait=wva, inc="s_dve", reads=(f"q_sb{cp}",), writes=(f"qa{cp}",))

    def emit_perm(i):
        h, which, tg, wbuf, rdy, dst, dcols, rope, xsrc, xcols, bias2 = tinfo(i)
        cp = i % 4
        pq = pQK[cp]
        qsb = q_sb[:, cp * 512:(cp + 1) * 512]
        qav = qa[:, cp * 512:(cp + 1) * 512]
        qbv = qb[:, cp * 512:(cp + 1) * 512]
        pe2 = em("tensor", lambda e, qsb=qsb, pq=pq: e.matmul(
            out=pq[:], lhsT=permb[:], rhs=qsb, start=True, stop=True),
            wait=[("s_act", st[i]["ev"])], inc="s_pe")
        st[i]["pe2"] = pe2
        vb = em("vector", lambda e, rope=rope, tg=tg, dcols=dcols, pq=pq, qbv=qbv:
                e.tensor_tensor(out=qbv, in0=pq[:],
                                in1=rope[:, dcols + tg * 512: dcols + tg * 512 + 512],
                                op=ALU.mult),
                wait=[("s_pe", pe2)], inc="s_dve", writes=(f"qb{cp}",))
        st[i]["vb"] = vb
        em("vector", lambda e, qav=qav, qbv=qbv: e.tensor_add(
            out=qav, in0=qav, in1=qbv),
            inc="s_dve", reads=(f"qa{cp}", f"qb{cp}"), writes=(f"qa{cp}",))
        st[i]["vq"] = cnt["s_dve"]
        st[i]["sq"] = em("scalar", lambda e, qav=qav, qsb=qsb: e.activation(
            out=qsb, in_=qav, func=AF.Square),
            wait=[("s_dve", st[i]["vq"]), ("s_pe", pe2)], inc="s_act",
            reads=(f"qa{cp}",), writes=(f"q_sb{cp}",))

    def emit_ss(i):
        cp = i % 4
        cp2 = i % 2
        qsb = q_sb[:, cp * 512:(cp + 1) * 512]
        w = [("s_act", st[i]["sq"])]
        if i >= 2:
            w.append(("s_act", st[i - 2].get("l1")))
        st[i]["pe3"] = em("tensor", lambda e, qsb=qsb, cp2=cp2: e.matmul(
            out=pSSB[cp2][:], lhsT=onesb2d[:], rhs=qsb, start=True, stop=True),
            wait=w, inc="s_pe")
        st[i]["l1"] = em("scalar", lambda e, cp2=cp2: e.activation(
            out=rbln[:, cp2 * 512:(cp2 + 1) * 512], in_=pSSB[cp2][:], func=AF.Ln,
            bias=fconst[:, 1:2], scale=1.0 / HD),
            wait=[("s_pe", st[i]["pe3"])], inc="s_act", writes=(f"rbln{cp2}",))
        h, which, tg, wbuf, rdy, dst, dcols, rope, xsrc, xcols, bias2 = tinfo(i)
        st[i]["l2"] = em("scalar", lambda e, bias2=bias2, cp2=cp2: e.activation(
            out=rstd_sb[:, cp2 * 512:(cp2 + 1) * 512],
            in_=rbln[:, cp2 * 512:(cp2 + 1) * 512], func=AF.Exp,
            scale=-0.5, bias=bias2),
            wait=[("s_dve", st[i - 2].get("vn")) if i >= 2 else (None, None)],
            inc="s_act", reads=(f"rbln{cp2}",), writes=(f"rstd_sb{cp2}",))

    def emit_bc(i):
        h, which, tg, wbuf, rdy, dst, dcols, rope, xsrc, xcols, bias2 = tinfo(i)
        cp = i % 4
        cp2 = i % 2
        qav = qa[:, cp * 512:(cp + 1) * 512]
        hpar = h & 1
        w6 = [("s_act", st[i]["l2"])]
        if h >= 2:
            w6.append(("s_ardy", spill_rdy[h - 2]))
        st[i]["vn"] = em("vector",
                         lambda e, dst=dst, tg=tg, hpar=hpar, dcols=dcols, cp2=cp2, qav=qav:
                         e.tensor_tensor(
                             out=dst[:, hpar * dcols + tg * 512:
                                     hpar * dcols + tg * 512 + 512],
                             in0=qav, in1=rstd_sb[:, cp2 * 512:(cp2 + 1) * 512],
                             op=ALU.mult),
                         wait=w6, inc="s_dve",
                         reads=(f"qa{cp}", f"rstd_sb{cp2}"), writes=("qkfm",))
        if tiles[i][1] == "k" and tiles[i][2] == 3:
            em("scalar", lambda e, h=h, hpar=hpar: e.dma_start(
                out=qscr_d[h], in_=q_fm[:, hpar * TOWN:(hpar + 1) * TOWN]),
                wait=[("s_dve", st[i]["vn"])], inc="s_aq")
            em("scalar", lambda e, h=h, hpar=hpar: e.dma_start(
                out=kscr_d[h], in_=k_fm[:, hpar * TSEQ:(hpar + 1) * TSEQ]),
                inc="s_aq")
            spill_rdy[h] = q_barrier("scalar", "s_aq", "s_ardy")

    for i in range(NTI + 3):
        if i < NTI:
            emit_mmg(i)
            if tiles[i][1] == "k" and tiles[i][2] == 3 and tiles[i][0] + 1 < H:
                stream_qkw(tiles[i][0] + 1)
        if i - 1 >= 0 and i - 1 < NTI:
            emit_perm(i - 1)
        if i - 3 >= 0:
            emit_bc(i - 3)
        if i - 2 >= 0 and i - 2 < NTI:
            emit_ss(i - 2)
    qk_spill_all = cnt["s_ardy"]

    new_block()
    pSC = [PB[0], PB[1], PB[2], PB[3]]
    pAV = [PB[4], PB[5]]
    pSUMB = PB[7]

    r_rdy = {}
    vhead_pe = {}

    def reload_head(h):
        par = h & 1
        wv = [("s_ardy", vscr_all), ("s_ardy", qk_spill_all),
              ("s_pe", vhead_pe.get(h - 2))]
        em("scalar", lambda e, h=h, par=par: e.dma_start(
            out=qr_fm[:, par * TOWN:(par + 1) * TOWN], in_=qscr_d[h]),
            wait=wv, inc="s_aq", writes=(f"qr{par}",))
        em("scalar", lambda e, h=h, par=par: e.dma_start(
            out=kr_fm[:, par * TSEQ:(par + 1) * TSEQ], in_=kscr_d[h]),
            inc="s_aq", writes=(f"kr{par}",))
        em("scalar", lambda e, h=h, par=par: e.dma_start(
            out=v_tm[:, par * D:(par + 1) * D].rearrange("p (tc o) -> p tc o", o=HD),
            in_=vscr_d[:].rearrange("tc p o -> p tc o")[:, :, h * HD:(h + 1) * HD]),
            inc="s_aq", writes=(f"v_tm{par}",))
        r_rdy[h] = q_barrier("scalar", "s_aq", "s_ardy")

    reload_head(0)
    reload_head(1)

    ents = []
    for h in range(H):
        for grp in range(2):
            js = list(range(8)) if grp == 0 else list(range(16))
            for idx, j in enumerate(js):
                ents.append((h, grp, j, idx == 0, idx == len(js) - 1))
    NEN = len(ents)
    est = [dict() for _ in range(NEN)]
    av_done = [None, None]
    sum_lr1 = [None]
    gidx = [0]

    def emit_scores(t):
        h, grp, j, first, last = ents[t]
        par = h & 1
        cp = t % 4
        w = [("s_ardy", r_rdy[h])]
        if t >= 4:
            w += [("s_act", est[t - 4].get("ex")), ("s_pe", est[t - 4].get("av_pe"))]
        pe = em("tensor", lambda e, j=j, grp=grp, par=par, cp=cp:
                e.matmul(out=pSC[cp][:],
                         lhsT=kr_fm[:, par * TSEQ + j * P: par * TSEQ + (j + 1) * P],
                         rhs=qr_fm[:, par * TOWN + grp * 512:
                                  par * TOWN + grp * 512 + 512],
                         start=True, stop=True, skip_group_check=True),
                wait=w, inc="s_pe")
        est[t]["pe"] = pe
        ex = em("scalar", lambda e, cp=cp: e.activation(
            out=attnT[:, cp * 512:(cp + 1) * 512], in_=pSC[cp][:], func=AF.Exp),
            wait=[("s_pe", pe)] + ([("s_pe", est[t - 4].get("av_pe"))] if t >= 4 else []),
            inc="s_act", writes=(f"attnT{cp}",))
        est[t]["ex"] = ex
        mi = j if (grp == 0 or j >= 8) else None
        dep = ("s_act", ex)
        if mi is not None:
            vm = em("vector", lambda e, cp=cp, mi=mi: e.tensor_mul(
                out=attnT[:, cp * 512:(cp + 1) * 512],
                in0=attnT[:, cp * 512:(cp + 1) * 512],
                in1=maskm[:, mi * 512:(mi + 1) * 512]),
                wait=[("s_act", ex)], inc="s_dve",
                reads=(f"attnT{cp}",), writes=(f"attnT{cp}",))
            dep = ("s_dve", vm)
        est[t]["dep"] = dep

    def emit_av(t):
        h, grp, j, first, last = ents[t]
        par = h & 1
        cp = t % 4
        gp = gidx[0] & 1
        w5 = [est[t]["dep"]]
        if first:
            w5 += [("s_dve", av_done[gp]), ("s_act", sum_lr1[0])]
        em("tensor", lambda e, j=j, par=par, cp=cp, first=first, last=last, gp=gp:
           e.matmul(out=pAV[gp][:],
                    lhsT=v_tm[:, par * D + j * P: par * D + (j + 1) * P],
                    rhs=attnT[:, cp * 512:(cp + 1) * 512],
                    start=first, stop=last, skip_group_check=True),
           wait=w5, inc="s_pe")
        pe_s = em("tensor", lambda e, cp=cp, first=first, last=last:
                  e.matmul(out=pSUMB[:], lhsT=onesb2d[:],
                           rhs=attnT[:, cp * 512:(cp + 1) * 512],
                           start=first, stop=last, skip_group_check=True),
                  inc="s_pe")
        est[t]["av_pe"] = pe_s
        if last:
            lr1 = em("scalar", lambda e: e.activation(
                out=rbln[:, 0:512], in_=pSUMB[:], func=AF.Ln),
                wait=[("s_pe", pe_s)], inc="s_act", writes=("rbln0",))
            sum_lr1[0] = lr1
            lr2 = em("scalar", lambda e: e.activation(
                out=rb_sb[:], in_=rbln[:, 0:512], func=AF.Exp, scale=-1.0),
                wait=[("s_dve", av_done[1 - gp])], inc="s_act",
                reads=("rbln0",), writes=("rb_sb",))
            ao = em("vector", lambda e, h=h, grp=grp, gp=gp: e.tensor_tensor(
                out=attn_out[:, h * TOWN + grp * 512: h * TOWN + grp * 512 + 512],
                in0=pAV[gp][:], in1=rb_sb[:], op=ALU.mult),
                wait=[("s_act", lr2)], inc="s_dve",
                reads=("rb_sb",), writes=("attn_out",))
            av_done[gp] = ao
            gidx[0] += 1
            if grp == 1:
                vhead_pe[h] = cnt["s_pe"]
                if h + 2 < H:
                    reload_head(h + 2)

    for t in range(NEN + 3):
        if t < NEN:
            emit_scores(t)
        if t - 3 >= 0:
            emit_av(t - 3)
    attn_done_dve = cnt["s_dve"]

    if "attn_out" in tap_d:
        td = tap_d["attn_out"]
        em("gpsimd", lambda e, td=td: e.dma_start(out=td[:], in_=attn_out[:]),
           wait=[("s_dve", attn_done_dve)], inc="s_tap")

    new_block()

    em("sync", lambda e: e.dma_start(
        out=resid[:].rearrange("p (tb d) -> p tb d", d=D),
        in_=x_own_d[:].rearrange("tb p d -> p tb d")),
        wait=[("s_tap", cnt["s_tap"] or None)], inc="s_xq")
    xown_rdy = q_barrier("sync", "s_xq", "s_xrdy")

    wo_rdy = {}
    wo_buf_pe = {}
    y_evac = {}

    def stream_wo(og, half):
        par = (og * 2 + half) & 1
        em("gpsimd", lambda e, og=og, half=half, par=par: e.dma_start(
            out=woTb[par][:].rearrange("p (k o) -> p k o", o=512),
            in_=woT_d[:].rearrange("(k p) o -> p k o", p=P)[
                :, half * 8:(half + 1) * 8, og * 512:(og + 1) * 512]),
            wait=[("s_pe", wo_buf_pe.get((og * 2 + half) - 2))], inc="s_wq",
            writes=(f"woT{par}",))
        wo_rdy[(og, half)] = q_barrier("gpsimd", "s_wq", "s_wrdy")

    stream_wo(0, 0)
    stream_wo(0, 1)
    for og in range(4):
        for half in range(2):
            par = (og * 2 + half) & 1
            for tb in range(NT):
                def y_mms(eng, tb=tb, half=half, par=par):
                    r = None
                    for cc in range(8):
                        c = half * 8 + cc
                        r = eng.matmul(
                            out=PB[tb][:],
                            lhsT=attn_out[:, c * TOWN + tb * P:
                                          c * TOWN + (tb + 1) * P],
                            rhs=woTb[par][:, cc * 512:(cc + 1) * 512],
                            start=(half == 0 and cc == 0),
                            stop=(half == 1 and cc == 7),
                            skip_group_check=True)
                    return r
                w = [("s_wrdy", wo_rdy[(og, half)]), ("s_dve", y_evac.get(tb))]
                pe = em("tensor", y_mms, wait=w, inc="s_pe")
                if half == 1:
                    y_evac[tb] = em(
                        "vector", lambda e, tb=tb, og=og: e.tensor_tensor(
                            out=resid[:, tb * D + og * 512:
                                      tb * D + og * 512 + 512],
                            in0=resid[:, tb * D + og * 512:
                                      tb * D + og * 512 + 512],
                            in1=PB[tb][:], op=ALU.add),
                        wait=[("s_pe", pe), ("s_xrdy", xown_rdy)], inc="s_dve",
                        reads=("resid",), writes=("resid",))
            wo_buf_pe[og * 2 + half] = cnt["s_pe"]
            nxt = og * 2 + half + 2
            if nxt < 8:
                stream_wo(nxt // 2, nxt % 2)
    wo_done_dve = cnt["s_dve"]

    moe_rdy = {}
    em("gpsimd", lambda e: e.dma_start(
        out=wg_bf[:].rearrange("p (k f) -> p k f", f=FI),
        in_=wgu_d[0].rearrange("(k p) f -> p k f", p=P)[:, :, 0:FI]), inc="s_wq")
    moe_rdy[("g", 0)] = q_barrier("gpsimd", "s_wq", "s_wrdy")
    em("gpsimd", lambda e: e.dma_start(
        out=wu_bf[:].rearrange("p (k f) -> p k f", f=FI),
        in_=wgu_d[0].rearrange("(k p) f -> p k f", p=P)[:, :, FI:2 * FI]), inc="s_wq")
    moe_rdy[("u", 0)] = q_barrier("gpsimd", "s_wq", "s_wrdy")

    resid_tap_out = None
    if "resid_mid" in tap_d:
        td = tap_d["resid_mid"]
        em("gpsimd", lambda e, td=td: e.dma_start(
            out=td[:].rearrange("tb p d -> p tb d"),
            in_=resid[:].rearrange("p (tb d) -> p tb d", d=D)),
            wait=[("s_dve", wo_done_dve)], inc="s_tap")
        resid_tap_out = cnt["s_tap"]

    new_block()
    pH_ = [PB[0], PB[1]]
    pR = PB[2]
    pD_ = [PB[4], PB[5], PB[6], PB[7]]

    em("gpsimd", lambda e: e.dma_start(
        out=gateb[:].rearrange("p (k o) -> p k o", o=32),
        in_=gate_d[:].rearrange("(k p) o -> p k o", p=P)), inc="s_wq")
    gate_rdy = q_barrier("gpsimd", "s_wq", "s_wrdy")

    rt_done = []
    for tb in range(NT):
        def r_mms(eng, tb=tb):
            r = None
            for k in range(KD):
                r = eng.matmul(out=pR[:, 0:32],
                               lhsT=xn_own[:, k * TOWN + tb * P:
                                           k * TOWN + (tb + 1) * P],
                               rhs=gateb[:, k * 32:(k + 1) * 32],
                               start=(k == 0), stop=(k == KD - 1))
            return r
        w = [("s_wrdy", gate_rdy)]
        if tb > 0:
            w.append(("s_dve", rt_done[-1]))
        pe = em("tensor", r_mms, wait=w, inc="s_pe")
        vmx = em("vector", lambda e: e.reduce_max(
            out=tk_s[:], in_=pR[:, 0:32], axis=AX.X, negate=True),
            wait=[("s_pe", pe)], inc="s_dve", writes=("tk_s",))
        aex = em("scalar", lambda e: e.activation(
            out=tk_p[:], in_=pR[:, 0:32], func=AF.Exp, bias=tk_s[:],
            accum_out=tk_mx[:, 0:1]),
            wait=[("s_dve", vmx)], inc="s_act",
            reads=("tk_s",), writes=("tk_p", "tk_mx"))
        em("vector", lambda e: e.reciprocal(out=tk_s[:], in_=tk_mx[:, 0:1]),
           wait=[("s_act", aex)], inc="s_dve",
           reads=("tk_mx",), writes=("tk_s",))
        em("vector", lambda e: e.tensor_scalar(
            out=tk_p[:], in0=tk_p[:], scalar1=tk_s[:], scalar2=None, op0=ALU.mult),
            inc="s_dve", reads=("tk_p", "tk_s"), writes=("tk_p",))
        em("vector", lambda e: e.tensor_copy(out=tk_w[:], in_=tk_p[:]),
           inc="s_dve", reads=("tk_p",), writes=("tk_w",))
        for it in range(4):
            em("vector", lambda e, it=it: e.reduce_max(
                out=tk_mx[:, it:it + 1], in_=tk_w[:], axis=AX.X),
                inc="s_dve", reads=("tk_w",), writes=("tk_mx",))
            if it < 3:
                em("vector", lambda e, it=it: e.tensor_scalar(
                    out=tk_keep[:], in0=tk_w[:], scalar1=tk_mx[:, it:it + 1],
                    scalar2=None, op0=ALU.is_ge),
                    inc="s_dve", reads=("tk_w", "tk_mx"), writes=("tk_keep",))
                em("vector", lambda e: e.scalar_tensor_tensor(
                    out=tk_w[:], in0=tk_keep[:], scalar=-2.0, in1=tk_w[:],
                    op0=ALU.mult, op1=ALU.add),
                    inc="s_dve", reads=("tk_keep", "tk_w"), writes=("tk_w",))
        em("vector", lambda e: e.reduce_sum(out=tk_s[:], in_=tk_mx[:], axis=AX.X),
           inc="s_dve", reads=("tk_mx",), writes=("tk_s",))
        em("vector", lambda e: e.reciprocal(out=tk_s[:], in_=tk_s[:]),
           inc="s_dve", reads=("tk_s",), writes=("tk_s",))
        em("vector", lambda e: e.tensor_scalar(
            out=tk_keep[:], in0=tk_p[:], scalar1=tk_mx[:, 3:4], scalar2=None,
            op0=ALU.is_ge), inc="s_dve",
            reads=("tk_p", "tk_mx"), writes=("tk_keep",))
        v = em("vector", lambda e, tb=tb: e.scalar_tensor_tensor(
            out=wrt[:, tb * 32:(tb + 1) * 32], in0=tk_p[:], scalar=tk_s[:],
            in1=tk_keep[:], op0=ALU.mult, op1=ALU.mult),
            inc="s_dve", reads=("tk_p", "tk_s", "tk_keep"), writes=("wrt",))
        rt_done.append(v)
    router_done = cnt["s_dve"]
    if "wrt" in tap_d:
        td = tap_d["wrt"]
        em("gpsimd", lambda e, td=td: e.dma_start(out=td[:], in_=wrt[:]),
           wait=[("s_dve", router_done)], inc="s_tap")

    pe_after = {}

    def moe_piece(kind, e):
        if kind == "g":
            dst = wg_bf[:].rearrange("p (k f) -> p k f", f=FI)
            src = wgu_d[e].rearrange("(k p) f -> p k f", p=P)[:, :, 0:FI]
            gw = pe_after.get(("G", e - 1))
        elif kind == "u":
            dst = wu_bf[:].rearrange("p (k f) -> p k f", f=FI)
            src = wgu_d[e].rearrange("(k p) f -> p k f", p=P)[:, :, FI:2 * FI]
            gw = pe_after.get(("U", e - 1))
        else:
            dst = wd_bf[:].rearrange("p (k f) -> p k f", f=D)
            src = wdn_d[e].rearrange("(k p) f -> p k f", p=P)
            gw = pe_after.get(("D", e - 1))
        em("gpsimd", lambda e_, dst=dst, src=src: e_.dma_start(out=dst, in_=src),
           wait=[("s_pe", gw)], inc="s_wq")
        moe_rdy[(kind, e)] = q_barrier("gpsimd", "s_wq", "s_wrdy")

    hpar = [0]
    pH_guard = [None, None]
    silu_done = {}
    pass1_done = {}
    last_D_evac = {}

    if cnt["s_tap"]:
        em("gpsimd", lambda e, v=cnt["s_tap"], s=SEM["s_tap"]: e.wait_ge(s, v))
    moe_piece("d", 0)
    for e in range(NE):
        for j in range(KF):
            for tg in range(2):
                p_ = hpar[0]; hpar[0] ^= 1
                pp = pH_[p_]

                def g_mms(eng, j=j, tg=tg, pp=pp):
                    r = None
                    for k in range(KD):
                        r = eng.matmul(
                            out=pp[:],
                            lhsT=wg_bf[:, k * FI + j * P: k * FI + (j + 1) * P],
                            rhs=xn_own[:, k * TOWN + tg * 512:
                                       k * TOWN + tg * 512 + 512],
                            start=(k == 0), stop=(k == KD - 1))
                    return r
                gw = pH_guard[p_]
                pe = em("tensor", g_mms,
                        wait=[("s_wrdy", moe_rdy[("g", e)])] + ([gw] if gw else []),
                        inc="s_pe")
                p1 = pass1_done.get((j, tg))
                sv = em("scalar", lambda e_, j=j, tg=tg, pp=pp: e_.activation(
                    out=silu_g[:, j * TOWN + tg * 512: j * TOWN + tg * 512 + 512],
                    in_=pp[:], func=AF.Silu),
                    wait=[("s_pe", pe)] + ([p1] if p1 else []),
                    inc="s_act", writes=("silu_g",))
                silu_done[(j, tg)] = sv
                pH_guard[p_] = ("s_act", sv)
        pe_after[("G", e)] = cnt["s_pe"]
        if e + 1 < NE:
            moe_piece("g", e + 1)
        for j in range(KF):
            for tg in range(2):
                p_ = hpar[0]; hpar[0] ^= 1
                pp = pH_[p_]

                def u_mms(eng, j=j, tg=tg, pp=pp):
                    r = None
                    for k in range(KD):
                        r = eng.matmul(
                            out=pp[:],
                            lhsT=wu_bf[:, k * FI + j * P: k * FI + (j + 1) * P],
                            rhs=xn_own[:, k * TOWN + tg * 512:
                                       k * TOWN + tg * 512 + 512],
                            start=(k == 0), stop=(k == KD - 1))
                    return r
                gw = pH_guard[p_]
                pe = em("tensor", u_mms,
                        wait=[("s_wrdy", moe_rdy[("u", e)])] + ([gw] if gw else []),
                        inc="s_pe")
                w1 = [("s_pe", pe), ("s_act", silu_done[(j, tg)]),
                      ("s_pe", pe_after.get(("D", e - 1)))]
                v = em("vector", lambda e_, j=j, tg=tg, pp=pp: e_.tensor_tensor(
                    out=gated[:, j * TOWN + tg * 512: j * TOWN + tg * 512 + 512],
                    in0=pp[:],
                    in1=silu_g[:, j * TOWN + tg * 512: j * TOWN + tg * 512 + 512],
                    op=ALU.mult),
                    wait=w1, inc="s_dve", reads=("silu_g",), writes=("gated",))
                pass1_done[(j, tg)] = ("s_dve", v)
                pH_guard[p_] = ("s_dve", v)
        pe_after[("U", e)] = cnt["s_pe"]
        gated_done = cnt["s_dve"]
        if e + 1 < NE:
            moe_piece("u", e + 1)
        for tb in range(NT):
            for m in range(4):
                def d_mms(eng, tb=tb, m=m):
                    r = None
                    for k in range(KF):
                        r = eng.matmul(
                            out=pD_[m][:],
                            lhsT=gated[:, k * TOWN + tb * P:
                                       k * TOWN + (tb + 1) * P],
                            rhs=wd_bf[:, k * D + m * 512: k * D + (m + 1) * 512],
                            start=(k == 0), stop=(k == KF - 1))
                    return r
                w = [("s_wrdy", moe_rdy[("d", e)]), ("s_dve", gated_done),
                     ("s_dve", router_done), ("s_dve", last_D_evac.get(m))]
                pe = em("tensor", d_mms, wait=w, inc="s_pe")
                v = em("vector", lambda e_, tb=tb, m=m, e=e: e_.scalar_tensor_tensor(
                    out=resid[:, tb * D + m * 512: tb * D + (m + 1) * 512],
                    in0=pD_[m][:],
                    scalar=wrt[:, tb * 32 + e: tb * 32 + e + 1],
                    in1=resid[:, tb * D + m * 512: tb * D + (m + 1) * 512],
                    op0=ALU.mult, op1=ALU.add),
                    wait=[("s_pe", pe)] + (
                        [("s_tap", cnt["s_tap"] or None)] if e == 0 else []),
                    inc="s_dve",
                    reads=("resid", "wrt"), writes=("resid",))
                last_D_evac[m] = v
        pe_after[("D", e)] = cnt["s_pe"]
        if e + 1 < NE:
            moe_piece("d", e + 1)
    moe_done = cnt["s_dve"]

    new_block()
    em("sync", lambda e: e.dma_start(
        out=out_d[:].rearrange("tb p d -> p tb d"),
        in_=resid[:].rearrange("p (tb d) -> p tb d", d=D)),
        wait=[("s_dve", moe_done)], inc="s_out")
    em("sync", lambda e, v=cnt["s_out"], s=SEM["s_out"]: e.wait_ge(s, v))
    em("gpsimd", lambda e, v=cnt["s_tap"], s=SEM["s_tap"]: e.wait_ge(s, v))

    with ES:
        replay_all()
    return nc


def _host_tables():
    inv = 1.0 / 10000.0 ** (np.arange(0, HD, 2, dtype=np.float64) / HD)
    invr = np.repeat(inv, 2)
    posk = np.arange(TSEQ, dtype=np.float64)

    def mk(pos):
        fr = invr[:, None] * pos[None, :]
        return np.stack([np.cos(fr), np.sin(fr)]).astype(ml_dtypes.bfloat16)

    ropek = mk(posk)
    ropeq, masks = {}, {}
    tri = (np.arange(P)[:, None] <= np.arange(P)[None, :]).astype(np.float32)
    for par in (0, 1):
        pos = np.concatenate(
            [np.arange((2 * b + par) * P, (2 * b + par + 1) * P)
             for b in range(NT)]).astype(np.float64)
        ropeq[par] = mk(pos)
        blocks = [2 * b + par for b in range(NT)]
        m = np.zeros((16, P, 512), dtype=np.float32)
        for gi, grp in enumerate((blocks[:4], blocks[4:])):
            for jj in range(8):
                j = jj if gi == 0 else 8 + jj
                for bi, B in enumerate(grp):
                    if j < B:
                        m[j, :, bi * P:(bi + 1) * P] = 1.0
                    elif j == B:
                        m[j, :, bi * P:(bi + 1) * P] = tri
        masks[par] = m.astype(ml_dtypes.bfloat16)

    perm = np.zeros((P, P), dtype=np.float32)
    for i in range(HD // 2):
        perm[2 * i + 1, 2 * i] = -1.0
        perm[2 * i, 2 * i + 1] = 1.0
    constb = np.stack([np.eye(P, dtype=np.float32), perm,
                       np.ones((P, P), dtype=np.float32)]).astype(ml_dtypes.bfloat16)
    onesrf = np.ones((1, P), dtype=np.float32)
    fconst = np.tile(np.array([[LN_EPS, RMS_EPS, -0.5 * np.log(HD), 0.0]],
                              dtype=np.float32), (P, 1))
    return ropeq, ropek, constb, onesrf, fconst, masks


def make_in_maps(L0_attn_x, L0_attn_W_qkv, L0_attn_W_o, L0_ffn_gate,
                 L0_ffn_gate_up_proj, L0_ffn_down_proj):
    x = np.ascontiguousarray(np.asarray(L0_attn_x, dtype=np.float32))
    wqkvT = np.ascontiguousarray(np.asarray(L0_attn_W_qkv, dtype=np.float32).T)
    woT = np.ascontiguousarray(np.asarray(L0_attn_W_o, dtype=np.float32).T)
    gate = np.ascontiguousarray(np.asarray(L0_ffn_gate, dtype=np.float32))
    wgu = np.ascontiguousarray(np.asarray(L0_ffn_gate_up_proj, dtype=np.float32))
    wdn = np.ascontiguousarray(np.asarray(L0_ffn_down_proj, dtype=np.float32))
    ropeq, ropek, constb, onesrf, fconst, masks = _host_tables()
    in_maps = []
    for c in range(8):
        s, par = c // 2, c % 2
        xs = x[s]
        x_seq = np.ascontiguousarray(xs.reshape(NTS, P, D))
        x_own = np.ascontiguousarray(
            np.stack([xs[(2 * b + par) * P:(2 * b + par + 1) * P]
                      for b in range(NT)]))
        in_maps.append(dict(
            x_own=x_own, x_seq=x_seq, wqkvT=wqkvT, woT=woT, gate=gate,
            wgu=wgu, wdn=wdn, ropeq=ropeq[par], ropek=ropek,
            maskm=masks[par], constb=constb, onesrf=onesrf, fconst=fconst))
    return in_maps


def kernel(L0_attn_x, L0_attn_W_qkv, L0_attn_W_o, L0_ffn_gate,
           L0_ffn_gate_up_proj, L0_ffn_down_proj):
    in_maps = make_in_maps(L0_attn_x, L0_attn_W_qkv, L0_attn_W_o, L0_ffn_gate,
                           L0_ffn_gate_up_proj, L0_ffn_down_proj)
    nc = build_nc()
    res = run_bass_kernel_spmd(nc, in_maps, core_ids=list(range(8)))
    out = np.empty((4, TSEQ, D), dtype=np.float32)
    for c in range(8):
        s, par = c // 2, c % 2
        r = np.asarray(res.results[c]["out"])
        for b in range(NT):
            out[s, (2 * b + par) * P:(2 * b + par + 1) * P, :] = r[b]
    return out
